# version 02
# baseline (speedup 1.0000x reference)
import contextlib

import numpy as np
import concourse.bass as bass
import concourse.mybir as mybir
from concourse.bass_utils import run_bass_kernel_spmd

F32 = mybir.dt.float32
BF16 = mybir.dt.bfloat16
AF = mybir.ActivationFunctionType
ALU = mybir.AluOpType
AX = mybir.AxisListType
AP = bass.AP

D = 1024
SEQ = 4096
BATCH = 4
NCORE = 8
NT = 2048
HALO = 260
TBUF = HALO + NT
NHG = 8
DH = 64
WIN = 256
QKS = 8.0
DI = 2730
NL = 6
GB_MAR2 = 1.0201
GB_SHIFT = 65.0
NTILE = NT // 128
HTILE = 2
ATILE = NTILE + HTILE
NEG = -1.0e9
GBLK = 2
PAIRS = [[0, 1], [2, 3], [4, 5], [6, 7]]
STAGE = "full"
ENGINES = ("sync", "tensor", "vector", "scalar", "gpsimd")

ICHUNKS = [(i * 128, 128) for i in range(21)] + [(21 * 128, 42)]
IBLOCKS = []
_i = 0
while _i < len(ICHUNKS):
    IBLOCKS.append(ICHUNKS[_i:_i + GBLK])
    _i += GBLK


def host_constants(core):
    import ml_dtypes
    half = core % 2
    out = {}
    out["ident"] = np.eye(128, dtype=ml_dtypes.bfloat16)
    out["ones_row"] = np.ones((1, 128), dtype=ml_dtypes.bfloat16)
    out["ones_col"] = np.ones((128, 1), dtype=ml_dtypes.bfloat16)
    out["ones_sq"] = np.ones((128, 128), dtype=ml_dtypes.bfloat16)
    r = np.arange(128)[:, None]
    c = np.arange(384)[None, :]
    tri = np.zeros((128, 384), np.float32)
    tri[:, :128] = np.where(c[:, :128] >= r, 0.0, NEG)
    tri[:, 256:384] = np.where((c[:, 256:384] - 256) <= r, 0.0, NEG)
    m0 = tri.copy()
    m1 = tri.copy()
    if half == 0:
        m0[:, :256] = NEG
        m1[:, :128] = NEG
    out["masks"] = np.ascontiguousarray(np.concatenate([m0, m1, tri], axis=1))
    out["halo_gate"] = np.full((128, 1), float(half), np.float32)
    return out


def host_rotab(core, q_scale, k_scale):
    half = core % 2
    s = half * NT
    t = np.arange(s - HALO, s + NT, dtype=np.float64)
    d_half = np.arange(0, DH, 2, dtype=np.float64)
    inv_freq = 1.0 / 10000.0 ** (d_half / DH)
    freqs = t[:, None] * inv_freq[None, :]
    freqs = np.concatenate([freqs, freqs], -1)
    base_scale = (d_half + 0.4 * DH) / (1.4 * DH)
    power = (t - SEQ // 2) / (WIN // 2)
    scale = base_scale[None, :] ** power[:, None]
    scale = np.concatenate([scale, scale], -1)
    cos = np.cos(freqs)
    sin = np.sin(freqs)
    inv = 1.0 / scale
    qs = np.asarray(q_scale, np.float64)
    ks = np.asarray(k_scale, np.float64)
    tabs = np.empty((NL, TBUF, 4 * DH), np.float32)
    for l in range(NL):
        cq = cos * scale * qs[l][None, :]
        ck = cos * inv * ks[l][None, :]
        sq = sin * scale
        sk = sin * inv
        saq = np.empty_like(sq)
        saq[:, :32] = -sq[:, :32] * qs[l][None, 32:64]
        saq[:, 32:] = sq[:, 32:] * qs[l][None, 0:32]
        sak = np.empty_like(sk)
        sak[:, :32] = -sk[:, :32] * ks[l][None, 32:64]
        sak[:, 32:] = sk[:, 32:] * ks[l][None, 0:32]
        tabs[l] = np.concatenate([cq, saq, ck, sak], axis=1)
    return np.ascontiguousarray(tabs.reshape(NL * TBUF, 4 * DH))



class Em:
    def __init__(self, p, eng):
        self._p = p
        self._eng = eng

    def __getattr__(self, meth):
        p, eng = self._p, self._eng

        def call(*args, _inc=False, **kw):
            if _inc:
                sname = p.eng_sem_of[eng]
                sem = p.sem[sname]
                tok = p.tok(sname, 1)
                p.q[eng].append(
                    lambda e, a=args, k=kw, m=meth, s=sem:
                    getattr(e, m)(*a, **k).then_inc(s, 1))
                return tok
            p.q[eng].append(
                lambda e, a=args, k=kw, m=meth: getattr(e, m)(*a, **k))
            return None
        return call

    def dma(self, out, in_, ch=None, **kw):
        p, eng = self._p, self._eng
        sname = ch if ch is not None else p.dma_sem_of[eng]
        sem = p.get_dma_sem(sname)
        tok = p.tok(sname, 16)
        p.q[eng].append(
            lambda e, o=out, i=in_, s=sem, k=kw:
            e.dma_start(out=o, in_=i, **k).then_inc(s, 16))
        return tok

    def wait(self, tok):
        if tok is None:
            return
        p, eng = self._p, self._eng
        sname, v = tok
        sem = p.sem[sname]
        p.q[eng].append(lambda e, s=sem, vv=v: e.wait_ge(s, vv))

    def wait_all(self, toks):
        best = {}
        for t in toks:
            if t is None:
                continue
            sname, v = t
            best[sname] = max(best.get(sname, 0), v)
        for sname, v in best.items():
            self.wait((sname, v))


class P:
    def __init__(self, nc, ctx):
        self.nc = nc
        self.ctx = ctx
        self.q = {e: [] for e in ENGINES}
        self.sem = {}
        self.cnt = {}
        for name in ("dS", "dG", "tE", "vE", "sE", "gE", "yE", "cc", "bar"):
            self.sem[name] = ctx.enter_context(nc.semaphore(name))
            self.cnt[name] = 0
        self.n_barriers = 0
        self.dma_chs = set(["dS", "dG"])
        self.dma_sem_of = {"sync": "dS", "gpsimd": "dG"}
        self.eng_sem_of = {"tensor": "tE", "vector": "vE", "scalar": "sE",
                           "gpsimd": "gE", "sync": "yE"}
        self.y = Em(self, "sync")
        self.t = Em(self, "tensor")
        self.v = Em(self, "vector")
        self.s = Em(self, "scalar")
        self.g = Em(self, "gpsimd")
        self.all = (self.y, self.t, self.v, self.s, self.g)

    def tok(self, sname, n):
        self.cnt[sname] += n
        return (sname, self.cnt[sname])

    def get_dma_sem(self, sname):
        if sname not in self.sem:
            self.sem[sname] = self.ctx.enter_context(
                self.nc.semaphore(sname))
            self.cnt[sname] = 0
            self.dma_chs.add(sname)
        return self.sem[sname]

    def emit_raw(self, eng, fn):
        self.q[eng].append(fn)

    def collective(self, in_ap, out_ap):
        sem = self.sem["cc"]
        tok = self.tok("cc", 1)
        self.q["gpsimd"].append(
            lambda e, i=in_ap, o=out_ap, s=sem:
            e.collective_compute("AllGather", ALU.bypass,
                                 replica_groups=PAIRS, ins=[i],
                                 outs=[o]).then_inc(s, 1))
        self.q["gpsimd"].append(
            lambda e, s=sem, v=tok[1]: e.wait_ge(s, v))
        return tok

    def barrier(self):
        self.n_barriers += 1
        bar = self.sem["bar"]
        for em in self.all:
            eng = em._eng
            if eng != "sync":
                self.q[eng].append(lambda e: e.drain())
            self.q[eng].append(lambda e, b=bar: e.nop().then_inc(b, 1))
        toks = [("bar", 5 * self.n_barriers)]
        for ch in self.dma_chs:
            if self.cnt[ch]:
                toks.append((ch, self.cnt[ch]))
        for em in self.all:
            em.wait_all(toks)

    def finalize(self, block):
        for eng in ENGINES:
            dec = getattr(block, eng)
            lst = self.q[eng]

            def runner(e, lst=lst):
                for fn in lst:
                    fn(e)
            dec(runner)


def rr(i, n):
    return slice(i * n, (i + 1) * n)



class Kern:
    def __init__(self, stage=STAGE, n_layers=NL):
        self.stage = stage
        self.n_layers = n_layers
        nc = bass.Bass("TRN2", target_bir_lowering=False)
        self.nc = nc

        def dri(n, sh, dt=F32):
            h = nc.dram_tensor(n, sh, dt, kind="ExternalInput")
            return h
        self.x_own_t = dri("x_own", [NT, D])
        self.wqkv_g_t = dri("wqkv_g", [D, NHG * 192])
        self.bqkv_gT_t = dri("bqkv_gT", [NHG * 192, 1])
        self.ident_t = dri("ident", [128, 128], BF16)
        self.ones_row_t = dri("ones_row", [1, 128], BF16)
        self.ones_col_t = dri("ones_col", [128, 1], BF16)
        self.ones_sq_t = dri("ones_sq", [128, 128], BF16)
        self.masks_t = dri("masks", [128, 3 * 384])
        self.gate_t = dri("halo_gate", [128, 1])
        self.rotab_t = dri("rotab", [NL * TBUF, 4 * DH])
        self.x_t = dri("x", [SEQ, D])
        self.wqkv_i_t = dri("wqkv_i", [NL * D, 3 * D])
        self.wo_i_t = dri("wo_i", [NL * D, D])
        self.w1_i_t = dri("w1_i", [NL * D, 2 * DI])
        self.w2_i_t = dri("w2_i", [NL * DI + 128, D])
        self.b1T_t = dri("b1T", [NL * 2 * DI, 1])
        self.b2T_t = dri("b2T", [NL * D, 1])
        self.lng_t = dri("lng", [NL, D])
        self.lnb_t = dri("lnb", [NL, D])
        self.y_t = nc.dram_tensor("y", [NT, D], F32, kind="ExternalOutput")
        self.o_scr_t = nc.dram_tensor("o_scr", [NHG * 256, 16 * DH], F32)
        self.x1_scr_t = nc.dram_tensor("x1_scr", [NT, D], F32)
        self.oi_scr_t = nc.dram_tensor("oi_scr", [2 * D, NT], BF16)
        self.cb_in_t = nc.dram_tensor("cb_in", [128 * 8 * HALO], F32)
        self.cb_out_t = nc.dram_tensor("cb_out", [2 * 128 * 8 * HALO], F32)

    def sbuf(self, name, shape, dt=F32):
        self._uniq = getattr(self, "_uniq", 0) + 1
        return self.ectx.enter_context(
            self.nc.sbuf_tensor("%s_%d" % (name, self._uniq), shape, dt))

    def st(self, c, a, b):
        return self.stream[:, c * TBUF + a: c * TBUF + b]

    def build(self):
        nc = self.nc
        with contextlib.ExitStack() as ectx:
            self.ectx = ectx
            p = P(nc, ectx)
            self.p = p
            self.pp = [ectx.enter_context(nc.psum_tensor("pool%d" % i,
                                                         [128, 512], F32))
                       for i in range(8)]
            self.ident = self.sbuf("ident_sb", [128, 128], BF16)
            self.ones_row = self.sbuf("ones_row_sb", [1, 128], BF16)
            self.ones_col = self.sbuf("ones_col_sb", [128, 1], BF16)
            self.ones_sq = self.sbuf("ones_sq_sb", [128, 128], BF16)
            self.gate = self.sbuf("gate_sb", [128, 1])
            self.eps = self.sbuf("eps_sb", [128, 1])
            self.stream = None
            self.masks = None

            p.g.memset(self.eps[:, :], 1e-5)
            t0 = [p.y.dma(self.ident[:], self.ident_t.ap(), ch="init"),
                  p.y.dma(self.ones_row[:], self.ones_row_t.ap(), ch="init"),
                  p.y.dma(self.ones_col[:], self.ones_col_t.ap(), ch="init"),
                  p.y.dma(self.ones_sq[:], self.ones_sq_t.ap(), ch="init"),
                  p.y.dma(self.gate[:], self.gate_t.ap(), ch="init")]
            for em in p.all:
                em.wait_all(t0)
            p.barrier()

            self.phase_A()
            if self.stage != "a":
                self.exchange_halo(HALO)
                for layer in range(self.n_layers):
                    if layer > 0:
                        self.exchange_halo(HALO)
                    self.phase_attn(layer)
                    self.exchange_halo(1)
                    self.phase_ffn(layer)
                self.phase_final()

            with nc.Block() as block:
                p.finalize(block)
        return nc

    def split(self, src, hi, lo, em=None):
        p = self.p
        em = em or p.v
        em.tensor_copy(hi, src)
        em.drain()
        return em.tensor_tensor(lo, src, hi, ALU.subtract, _inc=True)

    def split_bias(self, src, bias, hi, lo):
        p = self.p
        th = p.s.activation(hi, src, AF.Identity, bias=bias, scale=1.0, _inc=True)
        p.v.wait(th)
        return p.v.scalar_tensor_tensor(lo, src, bias, hi, ALU.add,
                                        ALU.subtract, _inc=True)

    def tr_pair(self, hi, lo, out_psum, kdim=128, start=True, stop=True):
        p = self.p
        i_ = self.ident[0:kdim, 0:kdim]
        p.t.matmul(out_psum, hi, i_, start=start, stop=False)
        return p.t.matmul(out_psum, lo, i_, start=False, stop=stop, _inc=True)

    def mm3(self, out_psum, lh_h, lh_l, rh_h, rh_l, start, stop, inc=True):
        p = self.p
        p.t.matmul(out_psum, lh_h, rh_h, start=start, stop=False)
        p.t.matmul(out_psum, lh_h, rh_l, start=False, stop=False)
        return p.t.matmul(out_psum, lh_l, rh_h, start=False, stop=stop,
                          _inc=inc)

    def load_split_w(self, src_t, row0, col_sl, n_kc, ncols, st_buf, hi, lo,
                     dst_col0=0, dst_stride=None, ch="w"):
        p = self.p
        if dst_stride is None:
            dst_stride = ncols
        src = src_t.ap()
        if not hasattr(self, "_stage_guard"):
            self._stage_guard = {}
        if not hasattr(self, "_stage_geom"):
            self._stage_geom = {}
        if self._stage_geom.get(id(st_buf)) != ncols:
            for pq in (0, 1):
                tk = self._stage_guard.pop((id(st_buf), pq), None)
                if tk is not None:
                    p.y.wait(tk)
            self._stage_geom[id(st_buf)] = ncols
        toks = []
        for kc in range(n_kc):
            par = kc % 2
            key = (id(st_buf), par)
            prev = self._stage_guard.get(key)
            if prev is not None:
                p.y.wait(prev)
            tld = p.y.dma(st_buf[:, rr(par, ncols)],
                          src[row0 + kc * 128: row0 + (kc + 1) * 128, col_sl],
                          ch=ch + str(par))
            p.v.wait(tld)
            c0 = dst_col0 + kc * dst_stride
            tk = self.split(st_buf[:, rr(par, ncols)],
                            hi[:, c0:c0 + ncols], lo[:, c0:c0 + ncols])
            self._stage_guard[key] = tk
            toks.append(tk)
        return toks

    def exchange_halo(self, n):
        p = self.p
        p.barrier()
        nel = 128 * 8 * n
        src = AP(self.stream, TBUF - n, [[8 * TBUF, 128], [TBUF, 8], [1, n]])
        t1 = p.g.dma(AP(self.cb_in_t, 0, [[8 * n, 128], [n, 8], [1, n]]), src,
                     ch="cb", allow_slow_non_contiguous=True)
        p.g.wait(t1)
        p.collective(self.cb_in_t.ap()[0:nel], self.cb_out_t.ap()[0:2 * nel])
        dst = AP(self.stream, HALO - n, [[8 * TBUF, 128], [TBUF, 8], [1, n]])
        t2 = p.g.dma(dst, AP(self.cb_out_t, 0, [[8 * n, 128], [n, 8], [1, n]]),
                     ch="cb", allow_slow_non_contiguous=True)
        p.g.drain()
        tg = p.g.nop(_inc=True)
        p.v.wait(t2)
        p.v.wait(tg)
        p.v.tensor_scalar(dst, dst, self.gate[:, :], None, ALU.mult)
        p.barrier()

    def phase_A(self):
        p = self.p
        sv = self.ectx
        with contextlib.ExitStack() as actx:
            self.ectx = actx
            xfh = self.sbuf("xfh", [128, 8 * SEQ], BF16)
            xfl = self.sbuf("xfl", [128, 8 * SEQ], BF16)
            self.A0_loadx(xfh, xfl)
            p.barrier()
            self.A1_heads(xfh, xfl)
            p.barrier()
            self.ectx = sv
        self.stream = self.sbuf("stream", [128, 8 * TBUF])
        self.masks = self.sbuf("masks_sb", [128, 3 * 384])
        tmk = p.y.dma(self.masks[:], self.masks_t.ap(), ch="init")
        p.g.memset(AP(self.stream, 0,
                      [[8 * TBUF, 128], [TBUF, 8], [1, HALO]]), 0.0)
        p.g.drain()
        tms = p.g.nop(_inc=True)
        for em in p.all:
            em.wait(tms)
            em.wait(tmk)
        self.A2_x1()
        p.barrier()

    def A0_loadx(self, xfh, xfl):
        p = self.p
        sv = self.ectx
        a0ctx = contextlib.ExitStack()
        self.ectx = a0ctx
        xst = self.sbuf("xst", [128, 2 * D])
        xsh = self.sbuf("xsh", [128, 2 * D], BF16)
        xsl = self.sbuf("xsl", [128, 2 * D], BF16)
        x_ap = self.x_t.ap()
        sp_prev = [None, None]
        pe_prev = [None, None]
        ev_prev = {}
        for t in range(SEQ // 128):
            par = t % 2
            if sp_prev[par] is not None:
                p.y.wait(sp_prev[par])
            tok_ld = p.y.dma(xst[:, rr(par, D)], x_ap[rr(t, 128), :], ch="xA%d" % par)
            p.v.wait(tok_ld)
            if pe_prev[par] is not None:
                p.v.wait(pe_prev[par])
            sp = self.split(xst[:, rr(par, D)], xsh[:, rr(par, D)],
                            xsl[:, rr(par, D)])
            sp_prev[par] = sp
            p.t.wait(sp)
            for c in range(8):
                pp = self.pp[c % 2][:, 0:128]
                ev = ev_prev.get(c % 2)
                if ev is not None:
                    p.t.wait(ev)
                mm = self.tr_pair(
                    xsh[:, par * D + c * 128: par * D + (c + 1) * 128],
                    xsl[:, par * D + c * 128: par * D + (c + 1) * 128], pp)
                if c == 7:
                    pe_prev[par] = mm
                p.v.wait(mm)
                dst_h = xfh[:, c * SEQ + t * 128: c * SEQ + (t + 1) * 128]
                dst_l = xfl[:, c * SEQ + t * 128: c * SEQ + (t + 1) * 128]
                p.v.tensor_copy(dst_h, pp)
                p.v.drain()
                ev_prev[c % 2] = p.v.tensor_tensor(dst_l, pp, dst_h,
                                                   ALU.subtract, _inc=True)
        a0ctx.close()
        self.ectx = sv

    def A1_heads(self, xfh, xfl):
        p = self.p
        wqh = self.sbuf("wg_qh", [128, 8 * 64], BF16)
        wql = self.sbuf("wg_ql", [128, 8 * 64], BF16)
        wkh = self.sbuf("wg_kh", [128, 8 * 64], BF16)
        wkl = self.sbuf("wg_kl", [128, 8 * 64], BF16)
        wvh = self.sbuf("wg_vh", [128, 8 * 64], BF16)
        wvl = self.sbuf("wg_vl", [128, 8 * 64], BF16)
        wst = self.sbuf("wgst", [128, 2 * 64])
        bq = self.sbuf("bq", [64, 3])
        qh_s = self.sbuf("qh_s", [64, SEQ], BF16)
        ql_s = self.sbuf("ql_s", [64, SEQ], BF16)
        kh_s = self.sbuf("kh_s", [64, SEQ], BF16)
        kl_s = self.sbuf("kl_s", [64, SEQ], BF16)
        vtm = self.sbuf("vtm", [128, 32 * 128], BF16)
        vf = self.sbuf("vf", [64, 2 * 512])
        vfh = self.sbuf("vfh", [64, 2 * 512], BF16)
        vfl = self.sbuf("vfl", [64, 2 * 512], BF16)
        sq2 = self.sbuf("sq2", [64, 2 * 512], BF16)
        mxg = self.sbuf("mxg", [128, 8])
        mx = self.sbuf("mx", [128, 4])
        pf = self.sbuf("pf", [128, 512])
        ph = self.sbuf("ph", [128, 512], BF16)
        pl = self.sbuf("pl", [128, 512], BF16)
        rrow = self.sbuf("rrow", [1, 512])
        rhi = self.sbuf("rhi", [1, 512], BF16)
        rlo = self.sbuf("rlo", [1, 512], BF16)
        ot1 = vf[:, 0:512]
        ot2 = vf[:, 512:1024]
        oth = vfh[:, 0:512]
        otl = vfl[:, 0:512]
        otr = self.sbuf("otr", [128, 2 * 64])
        B = dict(wqh=wqh, wql=wql, wkh=wkh, wkl=wkl, wvh=wvh, wvl=wvl,
                 wst=wst, bq=bq, qh_s=qh_s, ql_s=ql_s, kh_s=kh_s, kl_s=kl_s,
                 vtm=vtm, vf=vf, vfh=vfh, vfl=vfl, sq2=sq2, mxg=mxg, mx=mx,
                 pf=pf, ph=ph, pl=pl, rrow=rrow, rhi=rhi, rlo=rlo, ot1=ot1,
                 ot2=ot2, oth=oth, otl=otl, otr=otr, xfh=xfh, xfl=xfl)
        for lh in range(NHG):
            self.A1_one_head(lh, B)

    def A1_one_head(self, lh, B):
        p = self.p
        pp = self.pp
        xfh, xfl = B["xfh"], B["xfl"]
        wtoks = []
        for i, nm in enumerate(("q", "k", "v")):
            wtoks += self.load_split_w(
                self.wqkv_g_t, 0,
                slice(lh * 192 + i * 64, lh * 192 + (i + 1) * 64),
                8, 64, B["wst"], B["w" + nm + "h"], B["w" + nm + "l"],
                ch="wg")
        btoks = [p.y.dma(B["bq"][:, i:i + 1],
                         self.bqkv_gT_t.ap()[lh * 192 + i * 64:
                                             lh * 192 + (i + 1) * 64, :],
                         ch="bias")
                 for i in range(3)]
        p.t.wait_all(wtoks)
        p.s.wait_all(btoks + wtoks)
        p.v.wait_all(btoks)

        ev_tok = {}
        vt_tok = {}
        for g in range(8):
            for i, nm in enumerate(("q", "k", "v")):
                ps = pp[i][0:64, :]
                prev = ev_tok.get((g - 1, i))
                if prev is not None:
                    p.t.wait(prev)
                for kc in range(8):
                    mm = self.mm3(
                        ps,
                        B["w" + nm + "h"][:, rr(kc, 64)],
                        B["w" + nm + "l"][:, rr(kc, 64)],
                        xfh[:, kc * SEQ + g * 512: kc * SEQ + (g + 1) * 512],
                        xfl[:, kc * SEQ + g * 512: kc * SEQ + (g + 1) * 512],
                        start=(kc == 0), stop=(kc == 7),
                        inc=(kc == 7))
                p.v.wait(mm)
                p.s.wait(mm)
                bias = B["bq"][:, i:i + 1]
                if i < 2:
                    hi = (B["qh_s"] if i == 0 else B["kh_s"])[:, rr(g, 512)]
                    lo = (B["ql_s"] if i == 0 else B["kl_s"])[:, rr(g, 512)]
                    ev_tok[(g, i)] = self.split_bias(ps, bias, hi, lo)
                else:
                    par = g % 2
                    tv = p.s.activation(B["vf"][:, rr(par, 512)], ps, AF.Identity,
                                        bias=bias, scale=1.0, _inc=True)
                    p.v.wait(tv)
                    spv = self.split(B["vf"][:, rr(par, 512)],
                                     B["vfh"][:, rr(par, 512)],
                                     B["vfl"][:, rr(par, 512)])
                    ev_tok[(g, i)] = spv
                    p.t.wait(spv)
                    for tc in range(4):
                        tg = g * 4 + tc
                        ptp = pp[3 + tc % 2][:, 0:128]
                        pv = vt_tok.get(tg - 2)
                        if pv is not None:
                            p.t.wait(pv)
                        p.t.matmul(ptp[:, 0:64],
                                   B["vfh"][:, par * 512 + tc * 128:
                                            par * 512 + (tc + 1) * 128],
                                   self.ident[0:64, 0:64],
                                   start=True, stop=False)
                        mmv = p.t.matmul(
                            ptp[:, 64:128],
                            B["vfl"][:, par * 512 + tc * 128:
                                     par * 512 + (tc + 1) * 128],
                            self.ident[0:64, 0:64],
                            start=False, stop=True, _inc=True)
                        p.v.wait(mmv)
                        vt_tok[tg] = p.v.tensor_copy(
                            B["vtm"][:, rr(tg, 128)], ptp, _inc=True)
        p.barrier()

        mx, mxg, sq2 = B["mx"], B["mxg"], B["sq2"]
        for i, src in enumerate((B["qh_s"], B["kh_s"])):
            prev_red = None
            for g in range(8):
                par = g % 2
                tsq = p.s.activation(sq2[:, rr(par, 512)], src[:, rr(g, 512)],
                                     AF.Square, _inc=True)
                p.t.wait(tsq)
                if prev_red is not None:
                    p.t.wait(prev_red)
                mm = p.t.matmul(pp[5][:, :], self.ones_sq[0:64, :],
                                sq2[:, rr(par, 512)], start=True, stop=True,
                                _inc=True)
                p.v.wait(mm)
                prev_red = p.v.tensor_reduce(mxg[:, g:g + 1], pp[5][:, :],
                                             AX.X, ALU.max, _inc=True)
                p.s.wait(prev_red)
            p.v.drain()
            p.v.tensor_reduce(mx[:, i:i + 1], mxg[:, 0:8], AX.X, ALU.max)
            p.v.drain()
        p.v.tensor_tensor(mx[:, 2:3], mx[:, 0:1], mx[:, 1:2], ALU.mult)
        p.v.drain()
        tb = p.v.nop(_inc=True)
        p.s.wait(tb)
        tb2 = p.s.activation(mx[:, 2:3], mx[:, 2:3], AF.Sqrt, scale=GB_MAR2,
                             _inc=True)
        p.v.wait(tb2)
        tb3 = p.v.tensor_scalar(mx[:, 3:4], mx[:, 2:3], -1.0, GB_SHIFT,
                                ALU.mult, ALU.add, _inc=True)
        p.s.wait(tb3)

        pf, ph, pl = B["pf"], B["ph"], B["pl"]
        qh_s, ql_s, kh_s, kl_s = B["qh_s"], B["ql_s"], B["kh_s"], B["kl_s"]
        p.t.wait(tb3)
        for g in range(8):
            exp_tok = {}
            split_tok = {}
            pv_tok = {}
            for jc in range(32):
                par = jc % 2
                ps = pp[5 + par][:, :]
                prev = exp_tok.get(jc - 2)
                if prev is not None:
                    p.t.wait(prev)
                p.t.matmul(ps, kh_s[:, rr(jc, 128)], qh_s[:, rr(g, 512)],
                           start=True, stop=False)
                p.t.matmul(ps, kh_s[:, rr(jc, 128)], ql_s[:, rr(g, 512)],
                           start=False, stop=False)
                mm = p.t.matmul(ps, kl_s[:, rr(jc, 128)], qh_s[:, rr(g, 512)],
                                start=False, stop=True, _inc=True)
                p.s.wait(mm)
                prev_sp = split_tok.get(jc - 1)
                if prev_sp is not None:
                    p.s.wait(prev_sp)
                te = p.s.activation(pf[:, :], ps, AF.Exp,
                                    bias=mx[:, 3:4], scale=1.0, _inc=True)
                exp_tok[jc] = te
                p.v.wait(te)
                prev_pv = pv_tok.get(jc - 1)
                if prev_pv is not None:
                    p.v.wait(prev_pv)
                p.v.tensor_copy(ph[:, :], pf[:, :])
                p.v.drain()
                tsp = p.v.tensor_tensor(pl[:, :], pf[:, :], ph[:, :],
                                        ALU.subtract, _inc=True)
                split_tok[jc] = tsp
                p.t.wait(tsp)
                vt = B["vtm"][:, rr(jc, 128)]
                p.t.matmul(pp[7][:, :], vt, ph[:, :],
                           start=(jc == 0), stop=False)
                p.t.matmul(pp[7][:, :], vt, pl[:, :],
                           start=False, stop=(jc == 31))
                p.t.matmul(pp[3][0:1, :], self.ones_col[:, :], ph[:, :],
                           start=(jc == 0), stop=False)
                pv_tok[jc] = p.t.matmul(
                    pp[3][0:1, :], self.ones_col[:, :], pl[:, :],
                    start=False, stop=(jc == 31), _inc=True)
            p.v.wait(pv_tok[31])
            p.v.tensor_scalar(B["rrow"][:, :], pp[3][0:1, :], 1e-35, None,
                              ALU.max)
            p.v.drain()
            p.v.reciprocal(B["rrow"][:, :], B["rrow"][:, :])
            p.v.drain()
            trr = self.split(B["rrow"][:, :], B["rhi"][:, :], B["rlo"][:, :])
            p.t.wait(trr)
            p.t.matmul(pp[4][0:64, :], self.ones_row[0:1, 0:64],
                       B["rhi"][:, :], start=True, stop=False)
            tR = p.t.matmul(pp[4][0:64, :], self.ones_row[0:1, 0:64],
                            B["rlo"][:, :], start=False, stop=True, _inc=True)
            p.v.wait(tR)
            ot1, ot2, oth, otl, otr = (B["ot1"], B["ot2"], B["oth"],
                                       B["otl"], B["otr"])
            p.v.tensor_copy(ot1[:, :], pp[7][64:128, :])
            p.v.drain()
            p.v.tensor_tensor(ot2[:, :], ot1[:, :], pp[7][0:64, :], ALU.add)
            p.v.drain()
            p.v.tensor_tensor(ot1[:, :], ot2[:, :], pp[4][0:64, :], ALU.mult)
            p.v.drain()
            tsp2 = self.split(ot1[:, :], oth[:, :], otl[:, :])
            p.t.wait(tsp2)
            ev_otr = {}
            for tc in range(4):
                par2 = tc % 2
                ptp = pp[par2][:, 0:64]
                pv = ev_otr.get(tc - 2)
                if pv is not None:
                    p.t.wait(pv)
                mmt = self.tr_pair(oth[:, rr(tc, 128)], otl[:, rr(tc, 128)],
                                   ptp, kdim=64)
                p.v.wait(mmt)
                if pv is not None:
                    p.v.wait(pv)
                tev = p.v.tensor_copy(otr[:, rr(par2, 64)], ptp, _inc=True)
                p.g.wait(tev)
                tg2 = g * 4 + tc
                dst = AP(self.o_scr_t, (lh * 256 + tg2 * 8) * (16 * DH),
                         [[16 * DH, 8], [DH, 16], [1, DH]])
                ev_otr[tc] = p.g.dma(dst, AP(otr, par2 * 64,
                                             [[2 * 64, 128], [1, 64]]),
                                     ch="oscr%d" % par2)
            p.barrier()

    def A2_x1(self):
        p = self.p
        sv = self.ectx
        with contextlib.ExitStack() as actx:
            self.ectx = actx
            at = self.sbuf("a_t", [128, 2 * D])
            xt = self.sbuf("x_t2", [128, 2 * D])
            x1t = self.sbuf("x1_t", [128, 2 * D])
            x1h = self.sbuf("x1_h", [128, D], BF16)
            x1l = self.sbuf("x1_l", [128, D], BF16)
            sqt = self.sbuf("sq_t", [128, D])
            stat = self.sbuf("statA", [128, 4])
            osc = self.o_scr_t.ap()
            xo = self.x_own_t.ap()
            done_tok = {}
            for t in range(NTILE):
                par = t % 2
                pd = done_tok.get(t - 2, [])
                for tk in pd:
                    p.y.wait(tk)
                ta = p.y.dma(at[:, rr(par, D)], osc[rr(t, 128), :], ch="a2a%d" % par)
                tx = p.y.dma(xt[:, rr(par, D)], xo[rr(t, 128), :], ch="a2x%d" % par)
                p.v.wait(ta)
                p.v.wait(tx)
                for tk in pd:
                    p.v.wait(tk)
                p.v.tensor_tensor(x1t[:, rr(par, D)], at[:, rr(par, D)],
                                  xt[:, rr(par, D)], ALU.add)
                p.v.drain()
                p.v.tensor_reduce(stat[:, 0:1], x1t[:, rr(par, D)], AX.X,
                                  ALU.add)
                p.v.drain()
                p.v.tensor_scalar(stat[:, 1:2], stat[:, 0:1], 1.0 / D, None,
                                  ALU.mult)
                p.v.drain()
                p.v.tensor_scalar(x1t[:, rr(par, D)], x1t[:, rr(par, D)],
                                  stat[:, 1:2], None, ALU.subtract)
                p.v.drain()
                p.v.drain()
                tsq = p.v.nop(_inc=True)
                p.s.wait(tsq)
                t2 = p.s.activation(sqt[:, :], x1t[:, rr(par, D)], AF.Square,
                                    accum_out=stat[:, 2:3], _inc=True)
                p.v.wait(t2)
                p.v.drain()
                tvv = p.v.nop(_inc=True)
                p.s.wait(tvv)
                t3 = p.s.activation(stat[:, 2:3], stat[:, 2:3], AF.Sqrt,
                                    scale=1.0 / D, bias=self.eps[:, :], _inc=True)
                p.v.wait(t3)
                p.v.reciprocal(stat[:, 3:4], stat[:, 2:3])
                p.v.drain()
                tx1 = p.v.tensor_scalar(x1t[:, rr(par, D)], x1t[:, rr(par, D)],
                                        stat[:, 3:4], None, ALU.mult,
                                        _inc=True)
                p.g.wait(tx1)
                tw1 = p.g.dma(self.x1_scr_t.ap()[rr(t, 128), :],
                              x1t[:, rr(par, D)], ch="x1w%d" % par)
                if self.stage == "a":
                    tw2 = p.g.dma(self.y_t.ap()[rr(t, 128), :],
                                  x1t[:, rr(par, D)], ch="yw%d" % par)
                    done_tok[t] = [tw1, tw2]
                    continue
                p.v.drain()
                tspl = self.split(x1t[:, rr(par, D)], x1h[:, :], x1l[:, :])
                p.t.wait(tspl)
                last_ev = None
                for c in range(8):
                    ppt = self.pp[c % 2][:, 0:128]
                    mmt = self.tr_pair(x1h[:, rr(c, 128)], x1l[:, rr(c, 128)],
                                       ppt)
                    p.v.wait(mmt)
                    last_ev = p.v.tensor_copy(
                        self.st(c, HALO + t * 128, HALO + (t + 1) * 128),
                        ppt, _inc=True)
                    p.t.wait(last_ev)
                done_tok[t] = [tw1, last_ev]
        self.ectx = sv

    def phase_attn(self, layer):
        p = self.p
        pp = self.pp
        sv = self.ectx
        with contextlib.ExitStack() as actx:
            self.ectx = actx
            wh = self.sbuf("wA_h", [128, 8 * 1536], BF16)
            wl = self.sbuf("wA_l", [128, 8 * 1536], BF16)
            wst = self.sbuf("wA_st", [128, 2 * 512])
            rt = self.sbuf("rtab", [128, 2 * 256])
            uh = self.sbuf("u_h", [128, 2 * D], BF16)
            ul = self.sbuf("u_l", [128, 2 * D], BF16)
            tq1 = self.sbuf("tq1", [128, 512])
            tq2 = self.sbuf("tq2", [128, 512])
            s2 = self.sbuf("s2", [128, 8])
            rsq = self.sbuf("rsq", [128, 8])
            sqf = self.sbuf("sqf", [128, 512])
            qsph = self.sbuf("qsph", [128, 512], BF16)
            qspl = self.sbuf("qspl", [128, 512], BF16)
            q_sh = self.sbuf("q_sh", [64, 8 * 128], BF16)
            q_sl = self.sbuf("q_sl", [64, 8 * 128], BF16)
            krh = self.sbuf("krh", [64, 8 * 4 * 128], BF16)
            krl = self.sbuf("krl", [64, 8 * 4 * 128], BF16)
            v_ring = self.sbuf("v_ring", [128, 8 * 4 * 128], BF16)
            sm = self.sbuf("sm", [128, 384])
            astat = self.sbuf("astat", [128, 4])
            pfi = self.sbuf("pfi", [128, 384])
            phb = self.sbuf("phb", [128, 384], BF16)
            plb = self.sbuf("plb", [128, 384], BF16)
            pth = self.sbuf("pth", [128, 384], BF16)
            ptl = self.sbuf("ptl", [128, 384], BF16)
            ogh = self.sbuf("ogh", [64, 8 * 512], BF16)
            ogl = self.sbuf("ogl", [64, 8 * 512], BF16)

            for hh in range(2):
                wtoks = []
                for i in range(3):
                    wtoks += self.load_split_w(
                        self.wqkv_i_t, layer * D,
                        slice(i * D + hh * 512, i * D + hh * 512 + 512),
                        8, 512, wst, wh, wl, dst_col0=i * 512,
                        dst_stride=1536, ch="wqi")
                p.t.wait_all(wtoks)

                qkv_cons = {}
                ring_cons = {}
                og_cons = None
                u_cons = [None, None]
                rt_cons = [None, None]
                for t in range(ATILE):
                    own = t >= HTILE
                    c0 = HALO - 256 + t * 128
                    par = t % 2
                    if u_cons[par] is not None:
                        p.v.wait(u_cons[par])
                    for lo_c, sh in ((0, 0), (4, -1)):
                        si = AP(self.stream, lo_c * TBUF + c0 + sh,
                                [[8 * TBUF, 128], [TBUF, 4], [1, 128]])
                        ho = AP(uh, par * D + lo_c * 128,
                                [[2 * D, 128], [128, 4], [1, 128]])
                        lo = AP(ul, par * D + lo_c * 128,
                                [[2 * D, 128], [128, 4], [1, 128]])
                        tu = self.split(si, ho, lo)
                    p.t.wait(tu)
                    if rt_cons[par] is not None:
                        p.y.wait(rt_cons[par])
                    trt = p.y.dma(rt[:, rr(par, 256)],
                                  self.rotab_t.ap()[layer * TBUF + c0:
                                                    layer * TBUF + c0 + 128, :],
                                  ch="rt%d" % par)
                    L = dict(rt=rt, s2=s2, rsq=rsq, sqf=sqf, tq1=tq1, tq2=tq2,
                             qsph=qsph, qspl=qspl, q_sh=q_sh, q_sl=q_sl,
                             krh=krh, krl=krl,
                             v_ring=v_ring, sm=sm, astat=astat, pfi=pfi,
                             phb=phb, plb=plb, pth=pth, ptl=ptl, ogh=ogh,
                             ogl=ogl, ring_cons=ring_cons, trt=trt)
                    for i, base in enumerate((0, 512, 1024)):
                        idx = 0 if i != 1 else 1
                        ps = pp[idx][:, :]
                        prev = qkv_cons.get(idx)
                        if prev is not None:
                            p.t.wait(prev)
                        for kc in range(8):
                            mm = self.mm3(
                                ps,
                                uh[:, par * D + kc * 128:
                                   par * D + (kc + 1) * 128],
                                ul[:, par * D + kc * 128:
                                   par * D + (kc + 1) * 128],
                                wh[:, kc * 1536 + base:
                                   kc * 1536 + base + 512],
                                wl[:, kc * 1536 + base:
                                   kc * 1536 + base + 512],
                                start=(kc == 0), stop=(kc == 7),
                                inc=(kc == 7))
                        if i == 0 and own:
                            qkv_cons[idx] = self.proc_q(t, par, mm, L)
                        elif i == 0:
                            qkv_cons[idx] = mm
                        elif i == 1:
                            qkv_cons[idx] = self.proc_k(t, par, mm, L)
                        else:
                            qkv_cons[idx] = self.proc_v(t, par, mm, L)
                    u_cons[par] = ("tE", p.cnt["tE"])
                    rt_cons[par] = qkv_cons[1]
                    if own:
                        to = t - HTILE
                        cls = 0 if to == 0 else (1 if to == 1 else 2)
                        p.t.wait(qkv_cons[0])
                        p.t.wait(qkv_cons[1])
                        atok = self.attn_tile(t, to, cls, hh, L)
                        for s in (t - 2, t - 1, t):
                            ring_cons[s % 4] = atok
                        if to % 4 == 3:
                            og_cons = self.flush_og(to // 4, hh, L, atok)
                            L["og_cons"] = og_cons
                        self._og_cons = og_cons
                p.barrier()
            self.ectx = sv
        self.phase_wo(layer)
        p.barrier()

    def proc_q(self, t, par, mm_tok, L):
        p = self.p
        pp = self.pp
        tok = self.qk_common(pp[0], mm_tok, L, par, 0)
        p.t.wait(tok)
        qsph, qspl = L["qsph"], L["qspl"]
        q_sh, q_sl = L["q_sh"], L["q_sl"]
        last = None
        prev_ev = [None, None]
        for h in range(8):
            hp = h % 2
            ptp = pp[2 + hp]
            if prev_ev[hp] is not None:
                p.t.wait(prev_ev[hp])
            p.t.matmul(ptp[0:64, 0:128], qsph[:, rr(h, 64)],
                       self.ident[:, :], start=True, stop=False)
            mmq = p.t.matmul(ptp[0:64, 128:256], qspl[:, rr(h, 64)],
                             self.ident[:, :], start=False, stop=True,
                             _inc=True)
            p.v.wait(mmq)
            p.v.tensor_copy(q_sh[:, rr(h, 128)], ptp[0:64, 0:128])
            last = p.v.tensor_copy(q_sl[:, rr(h, 128)], ptp[0:64, 128:256],
                                   _inc=True)
            prev_ev[hp] = last
        return last

    def proc_k(self, t, par, mm_tok, L):
        p = self.p
        pp = self.pp
        tok = self.qk_common(pp[1], mm_tok, L, par, 128)
        p.t.wait(tok)
        qsph, qspl = L["qsph"], L["qspl"]
        krh, krl = L["krh"], L["krl"]
        slot = t % 4
        rc = L["ring_cons"].get(slot)
        last = None
        prev_ev = [None, None]
        for h in range(8):
            hp = h % 2
            ptp = pp[2 + hp]
            if prev_ev[hp] is not None:
                p.t.wait(prev_ev[hp])
            if rc is not None and h == 0:
                p.v.wait(rc)
            p.t.matmul(ptp[0:64, 0:128], qsph[:, rr(h, 64)],
                       self.ident[:, :], start=True, stop=False)
            mmk = p.t.matmul(ptp[0:64, 128:256], qspl[:, rr(h, 64)],
                             self.ident[:, :], start=False, stop=True,
                             _inc=True)
            p.v.wait(mmk)
            p.v.tensor_copy(krh[:, rr(h * 4 + slot, 128)], ptp[0:64, 0:128])
            last = p.v.tensor_copy(krl[:, rr(h * 4 + slot, 128)],
                                   ptp[0:64, 128:256], _inc=True)
            prev_ev[hp] = last
        return last

    def qk_common(self, ps_t, mm_tok, L, par, rtbase):
        p = self.p
        rt, s2, rsq, sqf = L["rt"], L["s2"], L["rsq"], L["sqf"]
        tq1, tq2 = L["tq1"], L["tq2"]
        qsph, qspl = L["qsph"], L["qspl"]
        p.s.wait(mm_tok)
        p.s.wait(L["trt"])
        tsq = p.s.activation(sqf[:, :], ps_t[:, :], AF.Square, _inc=True)
        p.v.wait(tsq)
        p.v.tensor_reduce(s2[:, :],
                          AP(sqf, 0, [[512, 128], [64, 8], [1, 64]]),
                          AX.X, ALU.add)
        p.v.drain()
        p.v.tensor_scalar(s2[:, :], s2[:, :], 1e-24, None, ALU.max)
        p.v.drain()
        p.v.reciprocal(rsq[:, :], s2[:, :])
        p.v.drain()
        t_r = p.v.nop(_inc=True)
        p.s.wait(t_r)
        tsr = p.s.activation(rsq[:, :], rsq[:, :], AF.Sqrt, _inc=True)
        p.v.wait(tsr)
        p.v.tensor_tensor(
            AP(tq1, 0, [[512, 128], [64, 8], [1, 64]]),
            AP(ps_t, 0, [[512, 128], [64, 8], [1, 64]]),
            AP(rt, par * 256 + rtbase, [[512, 128], [0, 8], [1, 64]]),
            ALU.mult)
        p.v.tensor_tensor(
            AP(tq2, 0, [[512, 128], [64, 8], [1, 32]]),
            AP(ps_t, 32, [[512, 128], [64, 8], [1, 32]]),
            AP(rt, par * 256 + rtbase + 64, [[512, 128], [0, 8], [1, 32]]),
            ALU.mult)
        p.v.tensor_tensor(
            AP(tq2, 32, [[512, 128], [64, 8], [1, 32]]),
            AP(ps_t, 0, [[512, 128], [64, 8], [1, 32]]),
            AP(rt, par * 256 + rtbase + 96, [[512, 128], [0, 8], [1, 32]]),
            ALU.mult)
        p.v.drain()
        p.v.tensor_tensor(tq1[:, :], tq1[:, :], tq2[:, :], ALU.add)
        p.v.drain()
        p.v.tensor_tensor(
            AP(tq1, 0, [[512, 128], [64, 8], [1, 64]]),
            AP(tq1, 0, [[512, 128], [64, 8], [1, 64]]),
            AP(rsq, 0, [[8, 128], [1, 8], [0, 64]]), ALU.mult)
        p.v.drain()
        return self.split(tq1[:, :], qsph[:, :], qspl[:, :])

    def proc_v(self, t, par, mm_tok, L):
        p = self.p
        ps_t = self.pp[0]
        v_ring = L["v_ring"]
        slot = t % 4
        rc = L["ring_cons"].get(slot)
        p.v.wait(mm_tok)
        if rc is not None:
            p.v.wait(rc)
        p.v.tensor_copy(
            AP(v_ring, slot * 128, [[8 * 4 * 128, 128], [512, 8], [1, 64]]),
            AP(ps_t, 0, [[512, 128], [64, 8], [1, 64]]))
        p.v.drain()
        return p.v.tensor_tensor(
            AP(v_ring, slot * 128 + 64,
               [[8 * 4 * 128, 128], [512, 8], [1, 64]]),
            AP(ps_t, 0, [[512, 128], [64, 8], [1, 64]]),
            AP(v_ring, slot * 128, [[8 * 4 * 128, 128], [512, 8], [1, 64]]),
            ALU.subtract, _inc=True)

    def attn_tile(self, t, to, cls, hh, L):
        p = self.p
        pp = self.pp
        q_sh, q_sl, v_ring = L["q_sh"], L["q_sl"], L["v_ring"]
        krh, krl = L["krh"], L["krl"]
        sm, astat, pfi = L["sm"], L["astat"], L["pfi"]
        phb, plb, pth, ptl = L["phb"], L["plb"], L["pth"], L["ptl"]
        ogh, ogl = L["ogh"], L["ogl"]
        last_tok = None
        og_guard = getattr(self, "_og_cons", None) if to % 4 == 0 else None
        for h in range(8):
            psS = pp[4]
            for jc in range(3):
                slot = (t - 2 + jc) % 4
                p.t.matmul(psS[:, rr(jc, 128)], q_sh[:, rr(h, 128)],
                           krh[:, rr(h * 4 + slot, 128)],
                           start=(jc == 0), stop=False)
                p.t.matmul(psS[:, rr(jc, 128)], q_sh[:, rr(h, 128)],
                           krl[:, rr(h * 4 + slot, 128)],
                           start=False, stop=False)
                mmS = p.t.matmul(psS[:, rr(jc, 128)], q_sl[:, rr(h, 128)],
                                 krh[:, rr(h * 4 + slot, 128)],
                                 start=False, stop=(jc == 2),
                                 _inc=(jc == 2))
            p.v.wait(mmS)
            p.v.scalar_tensor_tensor(sm[:, :], psS[:, 0:384], QKS,
                                     self.masks[:, rr(cls, 384)], ALU.mult,
                                     ALU.add)
            p.v.drain()
            p.v.tensor_reduce(astat[:, 0:1], sm[:, :], AX.X, ALU.max)
            p.v.drain()
            tneg = p.v.tensor_scalar(astat[:, 1:2], astat[:, 0:1], -1.0, None,
                                     ALU.mult, _inc=True)
            p.s.wait(tneg)
            texp = p.s.activation(pfi[:, :], sm[:, :], AF.Exp,
                                  bias=astat[:, 1:2],
                                  accum_out=astat[:, 2:3], _inc=True)
            p.v.wait(texp)
            p.v.reciprocal(astat[:, 3:4], astat[:, 2:3])
            p.v.drain()
            p.v.tensor_scalar(phb[:, :], pfi[:, :], astat[:, 3:4], None,
                              ALU.mult)
            p.v.drain()
            tspl = p.v.scalar_tensor_tensor(plb[:, :], pfi[:, :],
                                            astat[:, 3:4], phb[:, :],
                                            ALU.mult, ALU.subtract, _inc=True)
            p.t.wait(tspl)
            for jc in range(3):
                p.t.matmul(pp[5][:, rr(jc, 128)], phb[:, rr(jc, 128)],
                           self.ident[:, :], start=(jc == 0), stop=(jc == 2))
            for jc in range(3):
                mmT = p.t.matmul(pp[6][:, rr(jc, 128)], plb[:, rr(jc, 128)],
                                 self.ident[:, :], start=(jc == 0),
                                 stop=(jc == 2), _inc=(jc == 2))
            p.v.wait(mmT)
            p.v.tensor_copy(pth[:, :], pp[5][:, 0:384])
            tptl = p.v.tensor_copy(ptl[:, :], pp[6][:, 0:384], _inc=True)
            p.t.wait(tptl)
            if last_tok is not None:
                p.t.wait(last_tok)
            psO = pp[7][0:64, 0:128]
            for jc in range(3):
                slot = (t - 2 + jc) % 4
                vh = v_ring[:, (h * 4 + slot) * 128:
                            (h * 4 + slot) * 128 + 64]
                vl = v_ring[:, (h * 4 + slot) * 128 + 64:
                            (h * 4 + slot) * 128 + 128]
                p.t.matmul(psO, vh, pth[:, rr(jc, 128)],
                           start=(jc == 0), stop=False)
                p.t.matmul(psO, vh, ptl[:, rr(jc, 128)], start=False,
                           stop=False)
                mmO = p.t.matmul(psO, vl, pth[:, rr(jc, 128)], start=False,
                                 stop=(jc == 2), _inc=(jc == 2))
            p.v.wait(mmO)
            if og_guard is not None and h == 0:
                p.v.wait(og_guard)
            col = h * 512 + (to % 4) * 128
            p.v.tensor_copy(ogh[:, col:col + 128], psO)
            p.v.drain()
            last_tok = p.v.tensor_tensor(ogl[:, col:col + 128], psO,
                                         ogh[:, col:col + 128], ALU.subtract,
                                         _inc=True)
        return last_tok

    def flush_og(self, grp, hh, L, atok):
        p = self.p
        ogh, ogl = L["ogh"], L["ogl"]
        oi = self.oi_scr_t.ap()
        p.g.wait(atok)
        toks = []
        for h in range(8):
            row = hh * 512 + h * 64
            toks.append(p.g.dma(oi[row:row + 64, rr(grp, 512)],
                                ogh[:, rr(h, 512)], ch="og"))
            toks.append(p.g.dma(oi[D + row:D + row + 64, rr(grp, 512)],
                                ogl[:, rr(h, 512)], ch="og"))
        return toks[-1]

    def phase_wo(self, layer):
        p = self.p
        pp = self.pp
        sv = self.ectx
        with contextlib.ExitStack() as actx:
            self.ectx = actx
            woh = self.sbuf("wo_h", [128, 8 * 1024], BF16)
            wol = self.sbuf("wo_l", [128, 8 * 1024], BF16)
            wst = self.sbuf("wo_st", [128, 2 * 1024])
            obh = self.sbuf("ob_h", [128, 8 * 512], BF16)
            obl = self.sbuf("ob_l", [128, 8 * 512], BF16)
            wtoks = self.load_split_w(self.wo_i_t, layer * D, slice(0, D),
                                      8, 1024, wst, woh, wol, ch="wwo")
            p.t.wait_all(wtoks)
            oi = self.oi_scr_t.ap()
            ev_par = [None, None]
            for g in range(4):
                ltoks = []
                for c in range(8):
                    ltoks.append(p.y.dma(obh[:, rr(c, 512)],
                                         oi[rr(c, 128), rr(g, 512)],
                                         ch="ob"))
                    ltoks.append(p.y.dma(obl[:, rr(c, 512)],
                                         oi[D + c * 128: D + (c + 1) * 128,
                                            rr(g, 512)], ch="ob"))
                p.t.wait_all(ltoks)
                last_ev = None
                for of in range(8):
                    ps = pp[of % 2][:, :]
                    if ev_par[of % 2] is not None:
                        p.t.wait(ev_par[of % 2])
                    for kc in range(8):
                        mm = self.mm3(ps,
                                      woh[:, kc * 1024 + of * 128:
                                          kc * 1024 + (of + 1) * 128],
                                      wol[:, kc * 1024 + of * 128:
                                          kc * 1024 + (of + 1) * 128],
                                      obh[:, rr(kc, 512)],
                                      obl[:, rr(kc, 512)],
                                      start=(kc == 0), stop=(kc == 7),
                                      inc=(kc == 7))
                    p.v.wait(mm)
                    dst = self.st(of, HALO + g * 512, HALO + (g + 1) * 512)
                    ev = p.v.tensor_tensor(dst, dst, ps, ALU.add, _inc=True)
                    ev_par[of % 2] = ev
                    last_ev = ev
                p.y.wait(("tE", p.cnt["tE"]))
            self.ectx = sv

    def phase_ffn(self, layer):
        p = self.p
        pp = self.pp
        sv = self.ectx
        with contextlib.ExitStack() as actx:
            self.ectx = actx
            xnh = self.sbuf("xn_h", [128, 8 * NT], BF16)
            xnl = self.sbuf("xn_l", [128, 8 * NT], BF16)
            uh = self.sbuf("uf_h", [128, 512], BF16)
            ul = self.sbuf("uf_l", [128, 512], BF16)
            u2f = self.sbuf("u2f", [128, 512])
            u2h = self.sbuf("u2h", [128, 512], BF16)
            u2l = self.sbuf("u2l", [128, 512], BF16)
            rowp = self.sbuf("rowp", [1, 5 * 512])
            rwh = self.sbuf("rwh", [1, 4 * 512], BF16)
            gbh = self.sbuf("gbh", [1, 4 * D], BF16)
            ones5 = self.sbuf("ones512", [1, 512], BF16)
            xnf = self.sbuf("xnf", [128, 512])
            hgf = self.sbuf("hgf", [128, 256])
            hfh = self.sbuf("hf_h", [128, 256], BF16)
            hfl = self.sbuf("hf_l", [128, 256], BF16)
            w1h = self.sbuf("w1_h", [128, GBLK * 8 * 256], BF16)
            w1l = self.sbuf("w1_l", [128, GBLK * 8 * 256], BF16)
            w2h = self.sbuf("w2_h", [128, GBLK * 1024], BF16)
            w2l = self.sbuf("w2_l", [128, GBLK * 1024], BF16)
            wst = self.sbuf("wf_st", [128, 2 * 512])
            b2c = self.sbuf("b2c", [128, 8])
            b1c = self.sbuf("b1c", [128, 2 * len(ICHUNKS)])

            p.g.memset(ones5[:, :], 1.0)
            p.g.drain()
            tg1 = p.g.nop(_inc=True)
            tgb = p.y.dma(rowp[0:1, 0:D], self.lng_t.ap()[layer:layer + 1, :], ch="lnb")
            p.v.wait(tgb)
            tg_s = self.split(rowp[0:1, 0:D], gbh[:, 0:D], gbh[:, D:2 * D])
            p.y.wait(tg_s)
            tgb = p.y.dma(rowp[0:1, 0:D], self.lnb_t.ap()[layer:layer + 1, :], ch="lnb")
            p.v.wait(tgb)
            tspl = self.split(rowp[0:1, 0:D], gbh[:, 2 * D:3 * D],
                              gbh[:, 3 * D:4 * D])
            tb2 = p.y.dma(b2c[:, :], AP(self.b2T_t, layer * D,
                                        [[1, 128], [128, 8]]),
                          ch="bias2", allow_slow_non_contiguous=True)
            b1toks = []
            for ci, (c0, w) in enumerate(ICHUNKS):
                b1toks.append(p.y.dma(
                    b1c[0:w, 2 * ci:2 * ci + 1],
                    self.b1T_t.ap()[layer * 2 * DI + c0:
                                    layer * 2 * DI + c0 + w, :], ch="bias2"))
                b1toks.append(p.y.dma(
                    b1c[0:w, 2 * ci + 1:2 * ci + 2],
                    self.b1T_t.ap()[layer * 2 * DI + DI + c0:
                                    layer * 2 * DI + DI + c0 + w, :],
                    ch="bias2"))

            stats_cons = None
            prev_spu2 = None
            for g in range(4):
                a0 = HALO + g * 512
                for c in range(8):
                    sh = -1 if c >= 4 else 0
                    u_ap = self.st(c, a0 + sh, a0 + 512 + sh)
                    tspu = self.split(u_ap, uh[:, :], ul[:, :])
                    if prev_spu2 is not None:
                        p.s.wait(prev_spu2)
                    tq = p.s.activation(u2f[:, :], u_ap, AF.Square, _inc=True)
                    p.v.wait(tq)
                    tspu2 = self.split(u2f[:, :], u2h[:, :], u2l[:, :])
                    prev_spu2 = tspu2
                    p.t.wait(tspu)
                    p.t.wait(tspu2)
                    if c == 0 and stats_cons is not None:
                        p.t.wait(stats_cons)
                    p.t.matmul(pp[2][0:1, :], self.ones_col[:, :], uh[:, :],
                               start=(c == 0), stop=False)
                    p.t.matmul(pp[2][0:1, :], self.ones_col[:, :], ul[:, :],
                               start=False, stop=(c == 7))
                    p.t.matmul(pp[3][0:1, :], self.ones_col[:, :], u2h[:, :],
                               start=(c == 0), stop=False)
                    tmm = p.t.matmul(pp[3][0:1, :], self.ones_col[:, :],
                                     u2l[:, :], start=False, stop=(c == 7),
                                     _inc=True)
                    p.v.wait(tmm)
                p.v.tensor_scalar(rowp[0:1, 0:512], pp[2][0:1, :], 1.0 / D,
                                  None, ALU.mult)
                p.v.drain()
                p.v.tensor_tensor(rowp[0:1, 512:1024], rowp[0:1, 0:512],
                                  rowp[0:1, 0:512], ALU.mult)
                p.v.drain()
                p.v.scalar_tensor_tensor(rowp[0:1, 1024:1536], pp[3][0:1, :],
                                         1.0 / D, rowp[0:1, 512:1024],
                                         ALU.mult, ALU.subtract)
                p.v.drain()
                stats_cons = tvar = p.v.nop(_inc=True)
                p.s.wait(tvar)
                tsr = p.s.activation(rowp[0:1, 1024:1536], rowp[0:1, 1024:1536],
                                     AF.Sqrt, bias=self.eps[0:1, :], _inc=True)
                p.v.wait(tsr)
                p.v.reciprocal(rowp[0:1, 1536:2048], rowp[0:1, 1024:1536])
                p.v.drain()
                p.v.tensor_tensor(rowp[0:1, 2048:2560], rowp[0:1, 0:512],
                                  rowp[0:1, 1536:2048], ALU.mult)
                p.v.drain()
                p.v.tensor_scalar(rowp[0:1, 2048:2560], rowp[0:1, 2048:2560],
                                  -1.0, None, ALU.mult)
                p.v.drain()
                self.split(rowp[0:1, 1536:2048], rwh[:, 0:512],
                           rwh[:, 512:1024])
                trw = self.split(rowp[0:1, 2048:2560], rwh[:, 1024:1536],
                                 rwh[:, 1536:2048])
                p.t.wait(trw)
                if g == 0:
                    p.t.wait(tspl)
                for c in range(8):
                    if g == 0 and c == 0:
                        p.t.wait(tg1)
                    gh_c = gbh[0:1, rr(c, 128)]
                    gl_c = gbh[0:1, D + c * 128: D + (c + 1) * 128]
                    bh_c = gbh[0:1, 2 * D + c * 128: 2 * D + (c + 1) * 128]
                    bl_c = gbh[0:1, 3 * D + c * 128: 3 * D + (c + 1) * 128]
                    p.t.matmul(pp[4][:, :], gh_c, rwh[:, 0:512],
                               start=True, stop=False)
                    p.t.matmul(pp[4][:, :], gh_c, rwh[:, 512:1024],
                               start=False, stop=False)
                    p.t.matmul(pp[4][:, :], gl_c, rwh[:, 0:512],
                               start=False, stop=True)
                    p.t.matmul(pp[5][:, :], gh_c, rwh[:, 1024:1536],
                               start=True, stop=False)
                    p.t.matmul(pp[5][:, :], gh_c, rwh[:, 1536:2048],
                               start=False, stop=False)
                    p.t.matmul(pp[5][:, :], gl_c, rwh[:, 1024:1536],
                               start=False, stop=False)
                    p.t.matmul(pp[5][:, :], bh_c, ones5[:, :],
                               start=False, stop=False)
                    tm2 = p.t.matmul(pp[5][:, :], bl_c, ones5[:, :],
                                     start=False, stop=True, _inc=True)
                    p.v.wait(tm2)
                    sh = -1 if c >= 4 else 0
                    u_ap = self.st(c, a0 + sh, a0 + 512 + sh)
                    p.v.tensor_tensor(xnf[:, :], u_ap, pp[4][:, :], ALU.mult)
                    p.v.drain()
                    p.v.tensor_tensor(xnf[:, :], xnf[:, :], pp[5][:, :],
                                      ALU.add)
                    p.v.drain()
                    txn = self.split(xnf[:, :],
                                     xnh[:, c * NT + g * 512:
                                         c * NT + (g + 1) * 512],
                                     xnl[:, c * NT + g * 512:
                                         c * NT + (g + 1) * 512])
                    p.t.wait(txn)
            p.barrier()

            psa_cons = {}
            psg_cons = {}
            for bi, blk in enumerate(IBLOCKS):
                p.y.wait(("tE", p.cnt["tE"]))
                w1toks = []
                w2toks = []
                for ci_in_b, (c0, w) in enumerate(blk):
                    w1toks += self.load_split_w(
                        self.w1_i_t, layer * D, slice(c0, c0 + w), 8, w,
                        wst, w1h, w1l, dst_col0=ci_in_b * 8 * 256,
                        dst_stride=256, ch="wf")
                    w1toks += self.load_split_w(
                        self.w1_i_t, layer * D, slice(DI + c0, DI + c0 + w),
                        8, w, wst, w1h, w1l,
                        dst_col0=ci_in_b * 8 * 256 + 128, dst_stride=256,
                        ch="wf")
                    w2toks += self.load_split_w(
                        self.w2_i_t, layer * DI + c0, slice(0, 512), 1,
                        512, wst, w2h, w2l, dst_col0=ci_in_b * 1024, ch="wf")
                    w2toks += self.load_split_w(
                        self.w2_i_t, layer * DI + c0, slice(512, 1024), 1,
                        512, wst, w2h, w2l, dst_col0=ci_in_b * 1024 + 512,
                        ch="wf")
                p.t.wait_all(w1toks + w2toks)
                for g in range(8):
                    a0 = HALO + g * 256
                    for ci_in_b, (c0, w) in enumerate(blk):
                        ci = bi * GBLK + ci_in_b
                        par = (g * len(blk) + ci_in_b) % 2
                        psa = pp[par][0:w, 0:256]
                        psg = pp[2 + par][0:w, 0:256]
                        if psa_cons.get(par) is not None:
                            p.t.wait(psa_cons[par])
                        if psg_cons.get(par) is not None:
                            p.t.wait(psg_cons[par])
                        for kc in range(8):
                            self.mm3(psa,
                                     w1h[:, ci_in_b * 8 * 256 + kc * 256:
                                         ci_in_b * 8 * 256 + kc * 256 + w],
                                     w1l[:, ci_in_b * 8 * 256 + kc * 256:
                                         ci_in_b * 8 * 256 + kc * 256 + w],
                                     xnh[:, kc * NT + g * 256:
                                         kc * NT + (g + 1) * 256],
                                     xnl[:, kc * NT + g * 256:
                                         kc * NT + (g + 1) * 256],
                                     start=(kc == 0), stop=(kc == 7),
                                     inc=False)
                        for kc in range(8):
                            self.mm3(psg,
                                     w1h[:, ci_in_b * 8 * 256 + kc * 256 + 128:
                                         ci_in_b * 8 * 256 + kc * 256 + 128 + w],
                                     w1l[:, ci_in_b * 8 * 256 + kc * 256 + 128:
                                         ci_in_b * 8 * 256 + kc * 256 + 128 + w],
                                     xnh[:, kc * NT + g * 256:
                                         kc * NT + (g + 1) * 256],
                                     xnl[:, kc * NT + g * 256:
                                         kc * NT + (g + 1) * 256],
                                     start=(kc == 0), stop=(kc == 7),
                                     inc=(kc == 7))
                        mmw1 = ("tE", p.cnt["tE"])
                        p.s.wait(mmw1)
                        tgl = p.s.activation(hgf[0:w, :], psg, AF.Gelu,
                                             bias=b1c[0:w, 2 * ci + 1:
                                                      2 * ci + 2],
                                             scale=1.0, _inc=True)
                        psg_cons[par] = tgl
                        p.v.wait(tgl)
                        tsa = p.v.scalar_tensor_tensor(
                            xnf[0:w, 0:256], psa,
                            b1c[0:w, 2 * ci: 2 * ci + 1],
                            hgf[0:w, :], ALU.add, ALU.mult, _inc=True)
                        psa_cons[par] = tsa
                        p.v.drain()
                        p.v.tensor_copy(hfh[0:w, :], xnf[0:w, 0:256])
                        p.v.drain()
                        thl = p.v.tensor_tensor(hfl[0:w, :], xnf[0:w, 0:256],
                                                hfh[0:w, :], ALU.subtract,
                                                _inc=True)
                        p.t.wait(thl)
                        first = ci_in_b == 0
                        last = ci_in_b == len(blk) - 1
                        for of in range(8):
                            pso = pp[4 + of // 2][:, (of % 2) * 256:
                                                  (of % 2) * 256 + 256]
                            self.mm3(pso,
                                     w2h[0:w, ci_in_b * 1024 + of * 128:
                                         ci_in_b * 1024 + (of + 1) * 128],
                                     w2l[0:w, ci_in_b * 1024 + of * 128:
                                         ci_in_b * 1024 + (of + 1) * 128],
                                     hfh[0:w, :], hfl[0:w, :],
                                     start=(first and (of % 2 == 0)),
                                     stop=(last and (of % 2 == 1)),
                                     inc=(of == 7))
                        mmw2 = ("tE", p.cnt["tE"])
                        p.v.wait(mmw2)
                    for of in range(8):
                        pso = pp[4 + of // 2][:, (of % 2) * 256:
                                              (of % 2) * 256 + 256]
                        dst = self.st(of, a0, a0 + 256)
                        if bi == len(IBLOCKS) - 1:
                            ev = p.v.scalar_tensor_tensor(
                                dst, pso, b2c[:, of:of + 1], dst, ALU.add,
                                ALU.add, _inc=(of == 7))
                        else:
                            ev = p.v.tensor_tensor(dst, dst, pso, ALU.add,
                                                   _inc=(of == 7))
                    p.t.wait(ev)
            self.ectx = sv
        p.barrier()

    def phase_final(self):
        p = self.p
        pp = self.pp
        sv = self.ectx
        with contextlib.ExitStack() as actx:
            self.ectx = actx
            yh = self.sbuf("y_h", [128, D], BF16)
            yl = self.sbuf("y_l", [128, D], BF16)
            yt = self.sbuf("y_t2", [128, 2 * D])
            x1t = self.sbuf("x1b", [128, 2 * D])
            sqt = self.sbuf("sqF", [128, D])
            stat = self.sbuf("statF", [128, 4])
            dma_done = {}
            prev_tr = None
            for t in range(NTILE):
                par = t % 2
                a0 = HALO + t * 128
                if prev_tr is not None:
                    p.v.wait(prev_tr)
                tspl = self.split(
                    AP(self.stream, a0, [[8 * TBUF, 128], [TBUF, 4], [1, 128]]),
                    AP(yh, 0, [[D, 128], [128, 4], [1, 128]]),
                    AP(yl, 0, [[D, 128], [128, 4], [1, 128]]))
                tspl = self.split(
                    AP(self.stream, 4 * TBUF + a0,
                       [[8 * TBUF, 128], [TBUF, 4], [1, 128]]),
                    AP(yh, 512, [[D, 128], [128, 4], [1, 128]]),
                    AP(yl, 512, [[D, 128], [128, 4], [1, 128]]))
                p.t.wait(tspl)
                pd = dma_done.get(t - 2)
                if pd is not None:
                    p.y.wait(pd)
                tx1 = p.y.dma(x1t[:, rr(par, D)],
                              self.x1_scr_t.ap()[rr(t, 128), :],
                              ch="fx%d" % par)
                if pd is not None:
                    p.t.wait(pd)
                    p.v.wait(pd)
                for c in range(4):
                    ppt = pp[c % 2][:, 0:256]
                    mmt = self.tr_pair(yh[:, rr(2 * c, 128)],
                                       yl[:, rr(2 * c, 128)],
                                       ppt[:, 0:128], start=True, stop=False)
                    mmt = self.tr_pair(yh[:, rr(2 * c + 1, 128)],
                                       yl[:, rr(2 * c + 1, 128)],
                                       ppt[:, 128:256], start=False, stop=True)
                    p.v.wait(mmt)
                    p.v.tensor_copy(yt[:, par * D + 2 * c * 128:
                                       par * D + (2 * c + 2) * 128], ppt)
                    p.v.drain()
                    tev = p.v.nop(_inc=True)
                    p.t.wait(tev)
                prev_tr = ("tE", p.cnt["tE"])
                p.v.wait(tx1)
                p.v.drain()
                p.v.tensor_tensor(yt[:, rr(par, D)], yt[:, rr(par, D)],
                                  x1t[:, rr(par, D)], ALU.add)
                p.v.drain()
                p.v.tensor_reduce(stat[:, 0:1], yt[:, rr(par, D)], AX.X,
                                  ALU.add)
                p.v.drain()
                p.v.tensor_scalar(stat[:, 1:2], stat[:, 0:1], 1.0 / D, None,
                                  ALU.mult)
                p.v.drain()
                p.v.tensor_scalar(yt[:, rr(par, D)], yt[:, rr(par, D)],
                                  stat[:, 1:2], None, ALU.subtract)
                p.v.drain()
                p.v.drain()
                tsq = p.v.nop(_inc=True)
                p.s.wait(tsq)
                t2 = p.s.activation(sqt[:, :], yt[:, rr(par, D)], AF.Square,
                                    accum_out=stat[:, 2:3], _inc=True)
                p.v.wait(t2)
                p.v.drain()
                tvv = p.v.nop(_inc=True)
                p.s.wait(tvv)
                t3 = p.s.activation(stat[:, 2:3], stat[:, 2:3], AF.Sqrt,
                                    scale=1.0 / D, bias=self.eps[:, :], _inc=True)
                p.v.wait(t3)
                p.v.reciprocal(stat[:, 3:4], stat[:, 2:3])
                p.v.drain()
                ty = p.v.tensor_scalar(yt[:, rr(par, D)], yt[:, rr(par, D)],
                                       stat[:, 3:4], None, ALU.mult,
                                       _inc=True)
                p.g.wait(ty)
                dma_done[t] = p.g.dma(self.y_t.ap()[rr(t, 128), :],
                                      yt[:, rr(par, D)], ch="yw%d" % par)
            p.g.wait(dma_done[NTILE - 1])
            p.g.wait_all(list(dma_done.values()))
            self.ectx = sv


def build_nc(stage=STAGE, n_layers=NL):
    return Kern(stage, n_layers).build()



def kernel(L0_attn_x, L0_attn_W_qkv, L0_attn_b_qkv, L0_ffn_Wqkv,
           L0_ffn_q_scale, L0_ffn_k_scale, L0_ffn_Wo, L0_ffn_ln_g,
           L0_ffn_ln_b, L0_ffn_W1, L0_ffn_b1, L0_ffn_W2, L0_ffn_b2):
    x = np.ascontiguousarray(np.asarray(L0_attn_x, dtype=np.float32))
    wqkv_g = np.asarray(L0_attn_W_qkv, np.float32)
    bqkv_g = np.asarray(L0_attn_b_qkv, np.float32)
    nc = build_nc()
    f32c = lambda a: np.ascontiguousarray(np.asarray(a, np.float32))
    wqkv_i = f32c(L0_ffn_Wqkv).reshape(NL * D, 3 * D)
    wo_i = f32c(L0_ffn_Wo).reshape(NL * D, D)
    w1_i = f32c(L0_ffn_W1).reshape(NL * D, 2 * DI)
    w2_i = f32c(L0_ffn_W2).reshape(NL * DI, D)
    w2_i = np.concatenate([w2_i, np.zeros((128, D), np.float32)], axis=0)
    b1T = f32c(L0_ffn_b1).reshape(NL * 2 * DI, 1)
    b2T = f32c(L0_ffn_b2).reshape(NL * D, 1)
    lng = f32c(L0_ffn_ln_g)
    lnb = f32c(L0_ffn_ln_b)
    in_maps = []
    for c in range(NCORE):
        b, half = c // 2, c % 2
        hc = host_constants(c)
        h0 = half * NHG
        cols = []
        for lh in range(NHG):
            h = h0 + lh
            cols += list(range(h * 64, h * 64 + 64))
            cols += list(range(D + h * 64, D + h * 64 + 64))
            cols += list(range(2 * D + h * 64, 2 * D + h * 64 + 64))
        cols = np.asarray(cols)
        m = {
            "x": x[b],
            "x_own": np.ascontiguousarray(x[b, half * NT:(half + 1) * NT]),
            "wqkv_g": np.ascontiguousarray(wqkv_g[:, cols]),
            "bqkv_gT": np.ascontiguousarray(bqkv_g[cols][:, None]),
            "ident": hc["ident"], "ones_row": hc["ones_row"],
            "ones_col": hc["ones_col"], "ones_sq": hc["ones_sq"],
            "masks": hc["masks"], "halo_gate": hc["halo_gate"],
            "rotab": host_rotab(c, L0_ffn_q_scale, L0_ffn_k_scale),
            "wqkv_i": wqkv_i, "wo_i": wo_i, "w1_i": w1_i, "w2_i": w2_i,
            "b1T": b1T, "b2T": b2T, "lng": lng, "lnb": lnb,
        }
        in_maps.append(m)
    res = run_bass_kernel_spmd(nc, in_maps, core_ids=list(range(NCORE)))
    out = np.empty((BATCH, SEQ, D), np.float32)
    for c in range(NCORE):
        b, half = c // 2, c % 2
        out[b, half * NT:(half + 1) * NT] = res.results[c]["y"]
    return out


# version 03
# speedup vs baseline: 1.0107x; 1.0107x over previous
import contextlib

import numpy as np
import concourse.bass as bass
import concourse.mybir as mybir
from concourse.bass_utils import run_bass_kernel_spmd

F32 = mybir.dt.float32
BF16 = mybir.dt.bfloat16
AF = mybir.ActivationFunctionType
ALU = mybir.AluOpType
AX = mybir.AxisListType
AP = bass.AP

D = 1024
SEQ = 4096
BATCH = 4
NCORE = 8
NT = 2048
HALO = 260
TBUF = HALO + NT
NHG = 8
DH = 64
WIN = 256
QKS = 8.0
DI = 2730
NL = 6
GB_MAR2 = 1.0201
GB_SHIFT = 75.0
NTILE = NT // 128
HTILE = 2
ATILE = NTILE + HTILE
NEG = -1.0e9
GBLK = 2
PAIRS = [[0, 1], [2, 3], [4, 5], [6, 7]]
STAGE = "full"
ENGINES = ("sync", "tensor", "vector", "scalar", "gpsimd")

ICHUNKS = [(i * 128, 128) for i in range(21)] + [(21 * 128, 42)]
IBLOCKS = []
_i = 0
while _i < len(ICHUNKS):
    IBLOCKS.append(ICHUNKS[_i:_i + GBLK])
    _i += GBLK


def host_constants(core):
    import ml_dtypes
    half = core % 2
    out = {}
    out["ident"] = np.eye(128, dtype=ml_dtypes.bfloat16)
    out["ones_row"] = np.ones((1, 128), dtype=ml_dtypes.bfloat16)
    out["ones_col"] = np.ones((128, 1), dtype=ml_dtypes.bfloat16)
    out["ones_sq"] = np.ones((128, 128), dtype=ml_dtypes.bfloat16)
    r = np.arange(128)[:, None]
    c = np.arange(384)[None, :]
    tri = np.zeros((128, 384), np.float32)
    tri[:, :128] = np.where(c[:, :128] >= r, 0.0, NEG)
    tri[:, 256:384] = np.where((c[:, 256:384] - 256) <= r, 0.0, NEG)
    m0 = tri.copy()
    m1 = tri.copy()
    if half == 0:
        m0[:, :256] = NEG
        m1[:, :128] = NEG
    out["masks"] = np.ascontiguousarray(np.concatenate([m0, m1, tri], axis=1))
    out["halo_gate"] = np.full((128, 1), float(half), np.float32)
    return out


def host_rotab(core, q_scale, k_scale):
    half = core % 2
    s = half * NT
    t = np.arange(s - HALO, s + NT, dtype=np.float64)
    d_half = np.arange(0, DH, 2, dtype=np.float64)
    inv_freq = 1.0 / 10000.0 ** (d_half / DH)
    freqs = t[:, None] * inv_freq[None, :]
    freqs = np.concatenate([freqs, freqs], -1)
    base_scale = (d_half + 0.4 * DH) / (1.4 * DH)
    power = (t - SEQ // 2) / (WIN // 2)
    scale = base_scale[None, :] ** power[:, None]
    scale = np.concatenate([scale, scale], -1)
    cos = np.cos(freqs)
    sin = np.sin(freqs)
    inv = 1.0 / scale
    qs = np.asarray(q_scale, np.float64)
    ks = np.asarray(k_scale, np.float64)
    tabs = np.empty((NL, TBUF, 4 * DH), np.float32)
    for l in range(NL):
        cq = cos * scale * qs[l][None, :]
        ck = cos * inv * ks[l][None, :]
        sq = sin * scale
        sk = sin * inv
        saq = np.empty_like(sq)
        saq[:, :32] = -sq[:, :32] * qs[l][None, 32:64]
        saq[:, 32:] = sq[:, 32:] * qs[l][None, 0:32]
        sak = np.empty_like(sk)
        sak[:, :32] = -sk[:, :32] * ks[l][None, 32:64]
        sak[:, 32:] = sk[:, 32:] * ks[l][None, 0:32]
        tabs[l] = np.concatenate([cq, saq, ck, sak], axis=1)
    return np.ascontiguousarray(tabs.reshape(NL * TBUF, 4 * DH))



class Em:
    def __init__(self, p, eng):
        self._p = p
        self._eng = eng

    def __getattr__(self, meth):
        p, eng = self._p, self._eng

        def call(*args, _inc=False, **kw):
            if _inc:
                sname = p.eng_sem_of[eng]
                sem = p.sem[sname]
                tok = p.tok(sname, 1)
                p.q[eng].append(
                    lambda e, a=args, k=kw, m=meth, s=sem:
                    getattr(e, m)(*a, **k).then_inc(s, 1))
                return tok
            p.q[eng].append(
                lambda e, a=args, k=kw, m=meth: getattr(e, m)(*a, **k))
            return None
        return call

    def dma(self, out, in_, ch=None, **kw):
        p, eng = self._p, self._eng
        sname = ch if ch is not None else p.dma_sem_of[eng]
        sem = p.get_dma_sem(sname)
        tok = p.tok(sname, 16)
        p.q[eng].append(
            lambda e, o=out, i=in_, s=sem, k=kw:
            e.dma_start(out=o, in_=i, **k).then_inc(s, 16))
        return tok

    def wait(self, tok):
        if tok is None:
            return
        p, eng = self._p, self._eng
        sname, v = tok
        sem = p.sem[sname]
        p.q[eng].append(lambda e, s=sem, vv=v: e.wait_ge(s, vv))

    def wait_all(self, toks):
        best = {}
        for t in toks:
            if t is None:
                continue
            sname, v = t
            best[sname] = max(best.get(sname, 0), v)
        for sname, v in best.items():
            self.wait((sname, v))


class P:
    def __init__(self, nc, ctx):
        self.nc = nc
        self.ctx = ctx
        self.q = {e: [] for e in ENGINES}
        self.sem = {}
        self.cnt = {}
        for name in ("dS", "dG", "tE", "vE", "sE", "gE", "yE", "cc", "bar"):
            self.sem[name] = ctx.enter_context(nc.semaphore(name))
            self.cnt[name] = 0
        self.n_barriers = 0
        self.dma_chs = set(["dS", "dG"])
        self.dma_sem_of = {"sync": "dS", "gpsimd": "dG"}
        self.eng_sem_of = {"tensor": "tE", "vector": "vE", "scalar": "sE",
                           "gpsimd": "gE", "sync": "yE"}
        self.y = Em(self, "sync")
        self.t = Em(self, "tensor")
        self.v = Em(self, "vector")
        self.s = Em(self, "scalar")
        self.g = Em(self, "gpsimd")
        self.all = (self.y, self.t, self.v, self.s, self.g)

    def tok(self, sname, n):
        self.cnt[sname] += n
        return (sname, self.cnt[sname])

    def get_dma_sem(self, sname):
        if sname not in self.sem:
            self.sem[sname] = self.ctx.enter_context(
                self.nc.semaphore(sname))
            self.cnt[sname] = 0
            self.dma_chs.add(sname)
        return self.sem[sname]

    def emit_raw(self, eng, fn):
        self.q[eng].append(fn)

    def collective(self, in_ap, out_ap):
        sem = self.sem["cc"]
        tok = self.tok("cc", 1)
        self.q["gpsimd"].append(
            lambda e, i=in_ap, o=out_ap, s=sem:
            e.collective_compute("AllGather", ALU.bypass,
                                 replica_groups=PAIRS, ins=[i],
                                 outs=[o]).then_inc(s, 1))
        self.q["gpsimd"].append(
            lambda e, s=sem, v=tok[1]: e.wait_ge(s, v))
        return tok

    def barrier(self):
        self.n_barriers += 1
        bar = self.sem["bar"]
        for em in self.all:
            eng = em._eng
            if eng != "sync":
                self.q[eng].append(lambda e: e.drain())
            self.q[eng].append(lambda e, b=bar: e.nop().then_inc(b, 1))
        toks = [("bar", 5 * self.n_barriers)]
        for ch in self.dma_chs:
            if self.cnt[ch]:
                toks.append((ch, self.cnt[ch]))
        for em in self.all:
            em.wait_all(toks)

    def finalize(self, block):
        for eng in ENGINES:
            dec = getattr(block, eng)
            lst = self.q[eng]

            def runner(e, lst=lst):
                for fn in lst:
                    fn(e)
            dec(runner)


def rr(i, n):
    return slice(i * n, (i + 1) * n)



class Kern:
    def __init__(self, stage=STAGE, n_layers=NL):
        self.stage = stage
        self.n_layers = n_layers
        nc = bass.Bass("TRN2", target_bir_lowering=False)
        self.nc = nc

        def dri(n, sh, dt=F32):
            h = nc.dram_tensor(n, sh, dt, kind="ExternalInput")
            return h
        self.x_own_t = dri("x_own", [NT, D])
        self.wqkv_g_t = dri("wqkv_g", [D, NHG * 192])
        self.bqkv_gT_t = dri("bqkv_gT", [NHG * 192, 1])
        self.ident_t = dri("ident", [128, 128], BF16)
        self.ones_row_t = dri("ones_row", [1, 128], BF16)
        self.ones_col_t = dri("ones_col", [128, 1], BF16)
        self.ones_sq_t = dri("ones_sq", [128, 128], BF16)
        self.masks_t = dri("masks", [128, 3 * 384])
        self.gate_t = dri("halo_gate", [128, 1])
        self.rotab_t = dri("rotab", [NL * TBUF, 4 * DH])
        self.x_t = dri("x", [SEQ, D])
        self.wqkv_i_t = dri("wqkv_i", [NL * D, 3 * D])
        self.wo_i_t = dri("wo_i", [NL * D, D])
        self.w1_i_t = dri("w1_i", [NL * D, 2 * DI])
        self.w2_i_t = dri("w2_i", [NL * DI + 128, D])
        self.b1T_t = dri("b1T", [NL * 2 * DI, 1])
        self.b2T_t = dri("b2T", [NL * D, 1])
        self.lng_t = dri("lng", [NL, D])
        self.lnb_t = dri("lnb", [NL, D])
        self.y_t = nc.dram_tensor("y", [NT, D], F32, kind="ExternalOutput")
        self.o_scr_t = nc.dram_tensor("o_scr", [NHG * 256, 16 * DH], F32)
        self.x1_scr_t = nc.dram_tensor("x1_scr", [NT, D], F32)
        self.oi_scr_t = nc.dram_tensor("oi_scr", [2 * D, NT], BF16)
        self.cb_in_t = nc.dram_tensor("cb_in", [128 * 8 * HALO], F32)
        self.cb_out_t = nc.dram_tensor("cb_out", [2 * 128 * 8 * HALO], F32)

    def sbuf(self, name, shape, dt=F32):
        self._uniq = getattr(self, "_uniq", 0) + 1
        return self.ectx.enter_context(
            self.nc.sbuf_tensor("%s_%d" % (name, self._uniq), shape, dt))

    def st(self, c, a, b):
        return self.stream[:, c * TBUF + a: c * TBUF + b]

    def build(self):
        nc = self.nc
        with contextlib.ExitStack() as ectx:
            self.ectx = ectx
            p = P(nc, ectx)
            self.p = p
            self.pp = [ectx.enter_context(nc.psum_tensor("pool%d" % i,
                                                         [128, 512], F32))
                       for i in range(8)]
            self.ident = self.sbuf("ident_sb", [128, 128], BF16)
            self.ones_row = self.sbuf("ones_row_sb", [1, 128], BF16)
            self.ones_col = self.sbuf("ones_col_sb", [128, 1], BF16)
            self.ones_sq = self.sbuf("ones_sq_sb", [128, 128], BF16)
            self.gate = self.sbuf("gate_sb", [128, 1])
            self.eps = self.sbuf("eps_sb", [128, 1])
            self.stream = None
            self.masks = None

            p.g.memset(self.eps[:, :], 1e-5)
            t0 = [p.y.dma(self.ident[:], self.ident_t.ap(), ch="init"),
                  p.y.dma(self.ones_row[:], self.ones_row_t.ap(), ch="init"),
                  p.y.dma(self.ones_col[:], self.ones_col_t.ap(), ch="init"),
                  p.y.dma(self.ones_sq[:], self.ones_sq_t.ap(), ch="init"),
                  p.y.dma(self.gate[:], self.gate_t.ap(), ch="init")]
            for em in p.all:
                em.wait_all(t0)
            p.barrier()

            self.phase_A()
            if self.stage != "a":
                self.exchange_halo(HALO)
                for layer in range(self.n_layers):
                    if layer > 0:
                        self.exchange_halo(HALO)
                    self.phase_attn(layer)
                    self.exchange_halo(1)
                    self.phase_ffn(layer)
                self.phase_final()

            with nc.Block() as block:
                p.finalize(block)
        return nc

    def split(self, src, hi, lo, em=None):
        p = self.p
        em = em or p.v
        em.tensor_copy(hi, src)
        em.drain()
        return em.tensor_tensor(lo, src, hi, ALU.subtract, _inc=True)

    def split_bias(self, src, bias, hi, lo):
        p = self.p
        th = p.s.activation(hi, src, AF.Identity, bias=bias, scale=1.0, _inc=True)
        p.v.wait(th)
        return p.v.scalar_tensor_tensor(lo, src, bias, hi, ALU.add,
                                        ALU.subtract, _inc=True)

    def tr_pair(self, hi, lo, out_psum, kdim=128, start=True, stop=True):
        p = self.p
        i_ = self.ident[0:kdim, 0:kdim]
        p.t.matmul(out_psum, hi, i_, start=start, stop=False)
        return p.t.matmul(out_psum, lo, i_, start=False, stop=stop, _inc=True)

    def mm3(self, out_psum, lh_h, lh_l, rh_h, rh_l, start, stop, inc=True):
        p = self.p
        p.t.matmul(out_psum, lh_h, rh_h, start=start, stop=False)
        p.t.matmul(out_psum, lh_h, rh_l, start=False, stop=False)
        return p.t.matmul(out_psum, lh_l, rh_h, start=False, stop=stop,
                          _inc=inc)

    def load_split_w(self, src_t, row0, col_sl, n_kc, ncols, st_buf, hi, lo,
                     dst_col0=0, dst_stride=None, ch="w"):
        p = self.p
        if dst_stride is None:
            dst_stride = ncols
        src = src_t.ap()
        if not hasattr(self, "_stage_guard"):
            self._stage_guard = {}
        if not hasattr(self, "_stage_geom"):
            self._stage_geom = {}
        if self._stage_geom.get(id(st_buf)) != ncols:
            for pq in (0, 1):
                tk = self._stage_guard.pop((id(st_buf), pq), None)
                if tk is not None:
                    p.y.wait(tk)
            self._stage_geom[id(st_buf)] = ncols
        toks = []
        for kc in range(n_kc):
            par = kc % 2
            key = (id(st_buf), par)
            prev = self._stage_guard.get(key)
            if prev is not None:
                p.y.wait(prev)
            tld = p.y.dma(st_buf[:, rr(par, ncols)],
                          src[row0 + kc * 128: row0 + (kc + 1) * 128, col_sl],
                          ch=ch + str(par))
            p.v.wait(tld)
            c0 = dst_col0 + kc * dst_stride
            tk = self.split(st_buf[:, rr(par, ncols)],
                            hi[:, c0:c0 + ncols], lo[:, c0:c0 + ncols])
            self._stage_guard[key] = tk
            toks.append(tk)
        return toks

    def exchange_halo(self, n):
        p = self.p
        p.barrier()
        nel = 128 * 8 * n
        src = AP(self.stream, TBUF - n, [[8 * TBUF, 128], [TBUF, 8], [1, n]])
        t1 = p.g.dma(AP(self.cb_in_t, 0, [[8 * n, 128], [n, 8], [1, n]]), src,
                     ch="cb", allow_slow_non_contiguous=True)
        p.g.wait(t1)
        p.collective(self.cb_in_t.ap()[0:nel], self.cb_out_t.ap()[0:2 * nel])
        dst = AP(self.stream, HALO - n, [[8 * TBUF, 128], [TBUF, 8], [1, n]])
        t2 = p.g.dma(dst, AP(self.cb_out_t, 0, [[8 * n, 128], [n, 8], [1, n]]),
                     ch="cb", allow_slow_non_contiguous=True)
        p.g.drain()
        tg = p.g.nop(_inc=True)
        p.v.wait(t2)
        p.v.wait(tg)
        p.v.tensor_scalar(dst, dst, self.gate[:, :], None, ALU.mult)
        p.barrier()

    def phase_A(self):
        p = self.p
        sv = self.ectx
        with contextlib.ExitStack() as actx:
            self.ectx = actx
            xfh = self.sbuf("xfh", [128, 8 * SEQ], BF16)
            xfl = self.sbuf("xfl", [128, 8 * SEQ], BF16)
            self.A0_loadx(xfh, xfl)
            p.barrier()
            self.A1_heads(xfh, xfl)
            p.barrier()
            self.ectx = sv
        self.stream = self.sbuf("stream", [128, 8 * TBUF])
        self.masks = self.sbuf("masks_sb", [128, 3 * 384])
        tmk = p.y.dma(self.masks[:], self.masks_t.ap(), ch="init")
        p.g.memset(AP(self.stream, 0,
                      [[8 * TBUF, 128], [TBUF, 8], [1, HALO]]), 0.0)
        p.g.drain()
        tms = p.g.nop(_inc=True)
        for em in p.all:
            em.wait(tms)
            em.wait(tmk)
        self.A2_x1()
        p.barrier()

    def A0_loadx(self, xfh, xfl):
        p = self.p
        sv = self.ectx
        a0ctx = contextlib.ExitStack()
        self.ectx = a0ctx
        xst = self.sbuf("xst", [128, 2 * D])
        xsh = self.sbuf("xsh", [128, 2 * D], BF16)
        xsl = self.sbuf("xsl", [128, 2 * D], BF16)
        x_ap = self.x_t.ap()
        sp_prev = [None, None]
        pe_prev = [None, None]
        ev_prev = {}
        for t in range(SEQ // 128):
            par = t % 2
            if sp_prev[par] is not None:
                p.y.wait(sp_prev[par])
            tok_ld = p.y.dma(xst[:, rr(par, D)], x_ap[rr(t, 128), :], ch="xA%d" % par)
            p.v.wait(tok_ld)
            if pe_prev[par] is not None:
                p.v.wait(pe_prev[par])
            sp = self.split(xst[:, rr(par, D)], xsh[:, rr(par, D)],
                            xsl[:, rr(par, D)])
            sp_prev[par] = sp
            p.t.wait(sp)
            for c in range(8):
                pp = self.pp[c % 2][:, 0:128]
                ev = ev_prev.get(c % 2)
                if ev is not None:
                    p.t.wait(ev)
                mm = self.tr_pair(
                    xsh[:, par * D + c * 128: par * D + (c + 1) * 128],
                    xsl[:, par * D + c * 128: par * D + (c + 1) * 128], pp)
                if c == 7:
                    pe_prev[par] = mm
                p.v.wait(mm)
                dst_h = xfh[:, c * SEQ + t * 128: c * SEQ + (t + 1) * 128]
                dst_l = xfl[:, c * SEQ + t * 128: c * SEQ + (t + 1) * 128]
                p.v.tensor_copy(dst_h, pp)
                p.v.drain()
                ev_prev[c % 2] = p.v.tensor_tensor(dst_l, pp, dst_h,
                                                   ALU.subtract, _inc=True)
        a0ctx.close()
        self.ectx = sv

    def A1_heads(self, xfh, xfl):
        p = self.p
        wqh = self.sbuf("wg_qh", [128, 8 * 64], BF16)
        wql = self.sbuf("wg_ql", [128, 8 * 64], BF16)
        wkh = self.sbuf("wg_kh", [128, 8 * 64], BF16)
        wkl = self.sbuf("wg_kl", [128, 8 * 64], BF16)
        wvh = self.sbuf("wg_vh", [128, 8 * 64], BF16)
        wvl = self.sbuf("wg_vl", [128, 8 * 64], BF16)
        wst = self.sbuf("wgst", [128, 2 * 64])
        bq = self.sbuf("bq", [64, 3])
        qh_s = self.sbuf("qh_s", [64, SEQ], BF16)
        ql_s = self.sbuf("ql_s", [64, SEQ], BF16)
        kh_s = self.sbuf("kh_s", [64, SEQ], BF16)
        kl_s = self.sbuf("kl_s", [64, SEQ], BF16)
        vtm = self.sbuf("vtm", [128, 32 * 128], BF16)
        vf = self.sbuf("vf", [64, 2 * 512])
        vfh = self.sbuf("vfh", [64, 2 * 512], BF16)
        vfl = self.sbuf("vfl", [64, 2 * 512], BF16)
        sq2 = self.sbuf("sq2", [64, 2 * 512], BF16)
        mxg = self.sbuf("mxg", [128, 8])
        mx = self.sbuf("mx", [128, 4])
        pf = self.sbuf("pf", [128, 512])
        ph = self.sbuf("ph", [128, 512], BF16)
        pl = self.sbuf("pl", [128, 512], BF16)
        rrow = self.sbuf("rrow", [1, 512])
        rhi = self.sbuf("rhi", [1, 512], BF16)
        rlo = self.sbuf("rlo", [1, 512], BF16)
        ot1 = vf[:, 0:512]
        ot2 = vf[:, 512:1024]
        oth = vfh[:, 0:512]
        otl = vfl[:, 0:512]
        otr = self.sbuf("otr", [128, 2 * 64])
        B = dict(wqh=wqh, wql=wql, wkh=wkh, wkl=wkl, wvh=wvh, wvl=wvl,
                 wst=wst, bq=bq, qh_s=qh_s, ql_s=ql_s, kh_s=kh_s, kl_s=kl_s,
                 vtm=vtm, vf=vf, vfh=vfh, vfl=vfl, sq2=sq2, mxg=mxg, mx=mx,
                 pf=pf, ph=ph, pl=pl, rrow=rrow, rhi=rhi, rlo=rlo, ot1=ot1,
                 ot2=ot2, oth=oth, otl=otl, otr=otr, xfh=xfh, xfl=xfl)
        for lh in range(NHG):
            self.A1_one_head(lh, B)

    def A1_one_head(self, lh, B):
        p = self.p
        pp = self.pp
        xfh, xfl = B["xfh"], B["xfl"]
        wtoks = []
        for i, nm in enumerate(("q", "k", "v")):
            wtoks += self.load_split_w(
                self.wqkv_g_t, 0,
                slice(lh * 192 + i * 64, lh * 192 + (i + 1) * 64),
                8, 64, B["wst"], B["w" + nm + "h"], B["w" + nm + "l"],
                ch="wg")
        btoks = [p.y.dma(B["bq"][:, i:i + 1],
                         self.bqkv_gT_t.ap()[lh * 192 + i * 64:
                                             lh * 192 + (i + 1) * 64, :],
                         ch="bias")
                 for i in range(3)]
        p.t.wait_all(wtoks)
        p.s.wait_all(btoks + wtoks)
        p.v.wait_all(btoks)

        ev_tok = {}
        vt_tok = {}
        for g in range(8):
            for i, nm in enumerate(("q", "k", "v")):
                ps = pp[i][0:64, :]
                prev = ev_tok.get((g - 1, i))
                if prev is not None:
                    p.t.wait(prev)
                for kc in range(8):
                    mm = self.mm3(
                        ps,
                        B["w" + nm + "h"][:, rr(kc, 64)],
                        B["w" + nm + "l"][:, rr(kc, 64)],
                        xfh[:, kc * SEQ + g * 512: kc * SEQ + (g + 1) * 512],
                        xfl[:, kc * SEQ + g * 512: kc * SEQ + (g + 1) * 512],
                        start=(kc == 0), stop=(kc == 7),
                        inc=(kc == 7))
                p.v.wait(mm)
                p.s.wait(mm)
                bias = B["bq"][:, i:i + 1]
                if i < 2:
                    hi = (B["qh_s"] if i == 0 else B["kh_s"])[:, rr(g, 512)]
                    lo = (B["ql_s"] if i == 0 else B["kl_s"])[:, rr(g, 512)]
                    ev_tok[(g, i)] = self.split_bias(ps, bias, hi, lo)
                else:
                    par = g % 2
                    tv = p.s.activation(B["vf"][:, rr(par, 512)], ps, AF.Identity,
                                        bias=bias, scale=1.0, _inc=True)
                    p.v.wait(tv)
                    spv = self.split(B["vf"][:, rr(par, 512)],
                                     B["vfh"][:, rr(par, 512)],
                                     B["vfl"][:, rr(par, 512)])
                    ev_tok[(g, i)] = spv
                    p.t.wait(spv)
                    for tc in range(4):
                        tg = g * 4 + tc
                        ptp = pp[3 + tc % 2][:, 0:128]
                        pv = vt_tok.get(tg - 2)
                        if pv is not None:
                            p.t.wait(pv)
                        p.t.matmul(ptp[:, 0:64],
                                   B["vfh"][:, par * 512 + tc * 128:
                                            par * 512 + (tc + 1) * 128],
                                   self.ident[0:64, 0:64],
                                   start=True, stop=False)
                        mmv = p.t.matmul(
                            ptp[:, 64:128],
                            B["vfl"][:, par * 512 + tc * 128:
                                     par * 512 + (tc + 1) * 128],
                            self.ident[0:64, 0:64],
                            start=False, stop=True, _inc=True)
                        p.v.wait(mmv)
                        vt_tok[tg] = p.v.tensor_copy(
                            B["vtm"][:, rr(tg, 128)], ptp, _inc=True)
        p.barrier()

        mx, mxg, sq2 = B["mx"], B["mxg"], B["sq2"]
        for i, src in enumerate((B["qh_s"], B["kh_s"])):
            prev_red = None
            for g in range(8):
                par = g % 2
                tsq = p.s.activation(sq2[:, rr(par, 512)], src[:, rr(g, 512)],
                                     AF.Square, _inc=True)
                p.t.wait(tsq)
                if prev_red is not None:
                    p.t.wait(prev_red)
                mm = p.t.matmul(pp[5][:, :], self.ones_sq[0:64, :],
                                sq2[:, rr(par, 512)], start=True, stop=True,
                                _inc=True)
                p.v.wait(mm)
                prev_red = p.v.tensor_reduce(mxg[:, g:g + 1], pp[5][:, :],
                                             AX.X, ALU.max, _inc=True)
                p.s.wait(prev_red)
            p.v.drain()
            p.v.tensor_reduce(mx[:, i:i + 1], mxg[:, 0:8], AX.X, ALU.max)
            p.v.drain()
        p.v.tensor_tensor(mx[:, 2:3], mx[:, 0:1], mx[:, 1:2], ALU.mult)
        p.v.drain()
        tb = p.v.nop(_inc=True)
        p.s.wait(tb)
        tb2 = p.s.activation(mx[:, 2:3], mx[:, 2:3], AF.Sqrt, scale=GB_MAR2,
                             _inc=True)
        p.v.wait(tb2)
        tb3 = p.v.tensor_scalar(mx[:, 3:4], mx[:, 2:3], -1.0, GB_SHIFT,
                                ALU.mult, ALU.add, _inc=True)
        p.s.wait(tb3)

        pf, ph, pl = B["pf"], B["ph"], B["pl"]
        qh_s, ql_s, kh_s, kl_s = B["qh_s"], B["ql_s"], B["kh_s"], B["kl_s"]
        p.t.wait(tb3)
        for g in range(8):
            exp_tok = {}
            split_tok = {}
            pv_tok = {}
            for jc in range(32):
                par = jc % 2
                ps = pp[5 + par][:, :]
                prev = exp_tok.get(jc - 2)
                if prev is not None:
                    p.t.wait(prev)
                p.t.matmul(ps, kh_s[:, rr(jc, 128)], qh_s[:, rr(g, 512)],
                           start=True, stop=False)
                p.t.matmul(ps, kh_s[:, rr(jc, 128)], ql_s[:, rr(g, 512)],
                           start=False, stop=False)
                mm = p.t.matmul(ps, kl_s[:, rr(jc, 128)], qh_s[:, rr(g, 512)],
                                start=False, stop=True, _inc=True)
                p.s.wait(mm)
                prev_sp = split_tok.get(jc - 1)
                if prev_sp is not None:
                    p.s.wait(prev_sp)
                te = p.s.activation(pf[:, :], ps, AF.Exp,
                                    bias=mx[:, 3:4], scale=1.0, _inc=True)
                exp_tok[jc] = te
                p.v.wait(te)
                prev_pv = pv_tok.get(jc - 1)
                if prev_pv is not None:
                    p.v.wait(prev_pv)
                p.v.tensor_copy(ph[:, :], pf[:, :])
                p.v.drain()
                tsp = p.v.tensor_tensor(pl[:, :], pf[:, :], ph[:, :],
                                        ALU.subtract, _inc=True)
                split_tok[jc] = tsp
                p.t.wait(tsp)
                vt = B["vtm"][:, rr(jc, 128)]
                p.t.matmul(pp[7][:, :], vt, ph[:, :],
                           start=(jc == 0), stop=False)
                p.t.matmul(pp[7][:, :], vt, pl[:, :],
                           start=False, stop=(jc == 31))
                p.t.matmul(pp[3][0:1, :], self.ones_col[:, :], ph[:, :],
                           start=(jc == 0), stop=False)
                pv_tok[jc] = p.t.matmul(
                    pp[3][0:1, :], self.ones_col[:, :], pl[:, :],
                    start=False, stop=(jc == 31), _inc=True)
            p.v.wait(pv_tok[31])
            p.v.tensor_scalar(B["rrow"][:, :], pp[3][0:1, :], 1e-35, None,
                              ALU.max)
            p.v.drain()
            p.v.reciprocal(B["rrow"][:, :], B["rrow"][:, :])
            p.v.drain()
            trr = self.split(B["rrow"][:, :], B["rhi"][:, :], B["rlo"][:, :])
            p.t.wait(trr)
            p.t.matmul(pp[4][0:64, :], self.ones_row[0:1, 0:64],
                       B["rhi"][:, :], start=True, stop=False)
            tR = p.t.matmul(pp[4][0:64, :], self.ones_row[0:1, 0:64],
                            B["rlo"][:, :], start=False, stop=True, _inc=True)
            p.v.wait(tR)
            ot1, ot2, oth, otl, otr = (B["ot1"], B["ot2"], B["oth"],
                                       B["otl"], B["otr"])
            p.v.tensor_copy(ot1[:, :], pp[7][64:128, :])
            p.v.drain()
            p.v.tensor_tensor(ot2[:, :], ot1[:, :], pp[7][0:64, :], ALU.add)
            p.v.drain()
            p.v.tensor_tensor(ot1[:, :], ot2[:, :], pp[4][0:64, :], ALU.mult)
            p.v.drain()
            tsp2 = self.split(ot1[:, :], oth[:, :], otl[:, :])
            p.t.wait(tsp2)
            ev_otr = {}
            for tc in range(4):
                par2 = tc % 2
                ptp = pp[par2][:, 0:64]
                pv = ev_otr.get(tc - 2)
                if pv is not None:
                    p.t.wait(pv)
                mmt = self.tr_pair(oth[:, rr(tc, 128)], otl[:, rr(tc, 128)],
                                   ptp, kdim=64)
                p.v.wait(mmt)
                if pv is not None:
                    p.v.wait(pv)
                tev = p.v.tensor_copy(otr[:, rr(par2, 64)], ptp, _inc=True)
                p.g.wait(tev)
                tg2 = g * 4 + tc
                dst = AP(self.o_scr_t, (lh * 256 + tg2 * 8) * (16 * DH),
                         [[16 * DH, 8], [DH, 16], [1, DH]])
                ev_otr[tc] = p.g.dma(dst, AP(otr, par2 * 64,
                                             [[2 * 64, 128], [1, 64]]),
                                     ch="oscr%d" % par2)
            p.barrier()

    def A2_x1(self):
        p = self.p
        sv = self.ectx
        with contextlib.ExitStack() as actx:
            self.ectx = actx
            at = self.sbuf("a_t", [128, 2 * D])
            xt = self.sbuf("x_t2", [128, 2 * D])
            x1t = self.sbuf("x1_t", [128, 2 * D])
            x1h = self.sbuf("x1_h", [128, D], BF16)
            x1l = self.sbuf("x1_l", [128, D], BF16)
            sqt = self.sbuf("sq_t", [128, D])
            stat = self.sbuf("statA", [128, 4])
            osc = self.o_scr_t.ap()
            xo = self.x_own_t.ap()
            done_tok = {}
            for t in range(NTILE):
                par = t % 2
                pd = done_tok.get(t - 2, [])
                for tk in pd:
                    p.y.wait(tk)
                ta = p.y.dma(at[:, rr(par, D)], osc[rr(t, 128), :], ch="a2a%d" % par)
                tx = p.y.dma(xt[:, rr(par, D)], xo[rr(t, 128), :], ch="a2x%d" % par)
                p.v.wait(ta)
                p.v.wait(tx)
                for tk in pd:
                    p.v.wait(tk)
                p.v.tensor_tensor(x1t[:, rr(par, D)], at[:, rr(par, D)],
                                  xt[:, rr(par, D)], ALU.add)
                p.v.drain()
                p.v.tensor_reduce(stat[:, 0:1], x1t[:, rr(par, D)], AX.X,
                                  ALU.add)
                p.v.drain()
                p.v.tensor_scalar(stat[:, 1:2], stat[:, 0:1], 1.0 / D, None,
                                  ALU.mult)
                p.v.drain()
                p.v.tensor_scalar(x1t[:, rr(par, D)], x1t[:, rr(par, D)],
                                  stat[:, 1:2], None, ALU.subtract)
                p.v.drain()
                p.v.drain()
                tsq = p.v.nop(_inc=True)
                p.s.wait(tsq)
                t2 = p.s.activation(sqt[:, :], x1t[:, rr(par, D)], AF.Square,
                                    accum_out=stat[:, 2:3], _inc=True)
                p.v.wait(t2)
                p.v.drain()
                tvv = p.v.nop(_inc=True)
                p.s.wait(tvv)
                t3 = p.s.activation(stat[:, 2:3], stat[:, 2:3], AF.Sqrt,
                                    scale=1.0 / D, bias=self.eps[:, :], _inc=True)
                p.v.wait(t3)
                p.v.reciprocal(stat[:, 3:4], stat[:, 2:3])
                p.v.drain()
                tx1 = p.v.tensor_scalar(x1t[:, rr(par, D)], x1t[:, rr(par, D)],
                                        stat[:, 3:4], None, ALU.mult,
                                        _inc=True)
                p.g.wait(tx1)
                tw1 = p.g.dma(self.x1_scr_t.ap()[rr(t, 128), :],
                              x1t[:, rr(par, D)], ch="x1w%d" % par)
                if self.stage == "a":
                    tw2 = p.g.dma(self.y_t.ap()[rr(t, 128), :],
                                  x1t[:, rr(par, D)], ch="yw%d" % par)
                    done_tok[t] = [tw1, tw2]
                    continue
                p.v.drain()
                tspl = self.split(x1t[:, rr(par, D)], x1h[:, :], x1l[:, :])
                p.t.wait(tspl)
                last_ev = None
                for c in range(8):
                    ppt = self.pp[c % 2][:, 0:128]
                    mmt = self.tr_pair(x1h[:, rr(c, 128)], x1l[:, rr(c, 128)],
                                       ppt)
                    p.v.wait(mmt)
                    last_ev = p.v.tensor_copy(
                        self.st(c, HALO + t * 128, HALO + (t + 1) * 128),
                        ppt, _inc=True)
                    p.t.wait(last_ev)
                done_tok[t] = [tw1, last_ev]
        self.ectx = sv

    def phase_attn(self, layer):
        p = self.p
        pp = self.pp
        sv = self.ectx
        with contextlib.ExitStack() as actx:
            self.ectx = actx
            wh = self.sbuf("wA_h", [128, 8 * 1536], BF16)
            wl = self.sbuf("wA_l", [128, 8 * 1536], BF16)
            wst = self.sbuf("wA_st", [128, 2 * 512])
            rt = self.sbuf("rtab", [128, 2 * 256])
            uh = self.sbuf("u_h", [128, 2 * D], BF16)
            ul = self.sbuf("u_l", [128, 2 * D], BF16)
            tq1 = self.sbuf("tq1", [128, 512])
            tq2 = self.sbuf("tq2", [128, 512])
            s2 = self.sbuf("s2", [128, 8])
            rsq = self.sbuf("rsq", [128, 8])
            sqf = self.sbuf("sqf", [128, 512])
            qsph = self.sbuf("qsph", [128, 512], BF16)
            qspl = self.sbuf("qspl", [128, 512], BF16)
            q_sh = self.sbuf("q_sh", [64, 8 * 128], BF16)
            q_sl = self.sbuf("q_sl", [64, 8 * 128], BF16)
            krh = self.sbuf("krh", [64, 8 * 4 * 128], BF16)
            krl = self.sbuf("krl", [64, 8 * 4 * 128], BF16)
            v_ring = self.sbuf("v_ring", [128, 8 * 4 * 128], BF16)
            sm = self.sbuf("sm", [128, 384])
            astat = self.sbuf("astat", [128, 4])
            pfi = self.sbuf("pfi", [128, 384])
            phb = self.sbuf("phb", [128, 384], BF16)
            plb = self.sbuf("plb", [128, 384], BF16)
            pth = self.sbuf("pth", [128, 384], BF16)
            ptl = self.sbuf("ptl", [128, 384], BF16)
            ogh = self.sbuf("ogh", [64, 8 * 512], BF16)
            ogl = self.sbuf("ogl", [64, 8 * 512], BF16)

            for hh in range(2):
                wtoks = []
                for i in range(3):
                    wtoks += self.load_split_w(
                        self.wqkv_i_t, layer * D,
                        slice(i * D + hh * 512, i * D + hh * 512 + 512),
                        8, 512, wst, wh, wl, dst_col0=i * 512,
                        dst_stride=1536, ch="wqi")
                p.t.wait_all(wtoks)

                qkv_cons = {}
                ring_cons = {}
                og_cons = None
                u_cons = [None, None]
                rt_cons = [None, None]
                for t in range(ATILE):
                    own = t >= HTILE
                    c0 = HALO - 256 + t * 128
                    par = t % 2
                    if u_cons[par] is not None:
                        p.v.wait(u_cons[par])
                    for lo_c, sh in ((0, 0), (4, -1)):
                        si = AP(self.stream, lo_c * TBUF + c0 + sh,
                                [[8 * TBUF, 128], [TBUF, 4], [1, 128]])
                        ho = AP(uh, par * D + lo_c * 128,
                                [[2 * D, 128], [128, 4], [1, 128]])
                        lo = AP(ul, par * D + lo_c * 128,
                                [[2 * D, 128], [128, 4], [1, 128]])
                        tu = self.split(si, ho, lo)
                    p.t.wait(tu)
                    if rt_cons[par] is not None:
                        p.y.wait(rt_cons[par])
                    trt = p.y.dma(rt[:, rr(par, 256)],
                                  self.rotab_t.ap()[layer * TBUF + c0:
                                                    layer * TBUF + c0 + 128, :],
                                  ch="rt%d" % par)
                    L = dict(rt=rt, s2=s2, rsq=rsq, sqf=sqf, tq1=tq1, tq2=tq2,
                             qsph=qsph, qspl=qspl, q_sh=q_sh, q_sl=q_sl,
                             krh=krh, krl=krl,
                             v_ring=v_ring, sm=sm, astat=astat, pfi=pfi,
                             phb=phb, plb=plb, pth=pth, ptl=ptl, ogh=ogh,
                             ogl=ogl, ring_cons=ring_cons, trt=trt)
                    for i, base in enumerate((0, 512, 1024)):
                        idx = 0 if i != 1 else 1
                        ps = pp[idx][:, :]
                        prev = qkv_cons.get(idx)
                        if prev is not None:
                            p.t.wait(prev)
                        for kc in range(8):
                            mm = self.mm3(
                                ps,
                                uh[:, par * D + kc * 128:
                                   par * D + (kc + 1) * 128],
                                ul[:, par * D + kc * 128:
                                   par * D + (kc + 1) * 128],
                                wh[:, kc * 1536 + base:
                                   kc * 1536 + base + 512],
                                wl[:, kc * 1536 + base:
                                   kc * 1536 + base + 512],
                                start=(kc == 0), stop=(kc == 7),
                                inc=(kc == 7))
                        if i == 0 and own:
                            qkv_cons[idx] = self.proc_q(t, par, mm, L)
                        elif i == 0:
                            qkv_cons[idx] = mm
                        elif i == 1:
                            qkv_cons[idx] = self.proc_k(t, par, mm, L)
                        else:
                            qkv_cons[idx] = self.proc_v(t, par, mm, L)
                    u_cons[par] = ("tE", p.cnt["tE"])
                    rt_cons[par] = qkv_cons[1]
                    if own:
                        to = t - HTILE
                        cls = 0 if to == 0 else (1 if to == 1 else 2)
                        p.t.wait(qkv_cons[0])
                        p.t.wait(qkv_cons[1])
                        atok = self.attn_tile(t, to, cls, hh, L)
                        for s in (t - 2, t - 1, t):
                            ring_cons[s % 4] = atok
                        if to % 4 == 3:
                            og_cons = self.flush_og(to // 4, hh, L, atok)
                            L["og_cons"] = og_cons
                        self._og_cons = og_cons
                p.barrier()
            self.ectx = sv
        self.phase_wo(layer)
        p.barrier()

    def proc_q(self, t, par, mm_tok, L):
        p = self.p
        pp = self.pp
        tok = self.qk_common(pp[0], mm_tok, L, par, 0)
        p.t.wait(tok)
        qsph, qspl = L["qsph"], L["qspl"]
        q_sh, q_sl = L["q_sh"], L["q_sl"]
        last = None
        prev_ev = [None, None]
        for h in range(8):
            hp = h % 2
            ptp = pp[2 + hp]
            if prev_ev[hp] is not None:
                p.t.wait(prev_ev[hp])
            p.t.matmul(ptp[0:64, 0:128], qsph[:, rr(h, 64)],
                       self.ident[:, :], start=True, stop=False)
            mmq = p.t.matmul(ptp[0:64, 128:256], qspl[:, rr(h, 64)],
                             self.ident[:, :], start=False, stop=True,
                             _inc=True)
            p.v.wait(mmq)
            p.v.tensor_copy(q_sh[:, rr(h, 128)], ptp[0:64, 0:128])
            last = p.v.tensor_copy(q_sl[:, rr(h, 128)], ptp[0:64, 128:256],
                                   _inc=True)
            prev_ev[hp] = last
        return last

    def proc_k(self, t, par, mm_tok, L):
        p = self.p
        pp = self.pp
        tok = self.qk_common(pp[1], mm_tok, L, par, 128)
        p.t.wait(tok)
        qsph, qspl = L["qsph"], L["qspl"]
        krh, krl = L["krh"], L["krl"]
        slot = t % 4
        rc = L["ring_cons"].get(slot)
        last = None
        prev_ev = [None, None]
        for h in range(8):
            hp = h % 2
            ptp = pp[2 + hp]
            if prev_ev[hp] is not None:
                p.t.wait(prev_ev[hp])
            if rc is not None and h == 0:
                p.v.wait(rc)
            p.t.matmul(ptp[0:64, 0:128], qsph[:, rr(h, 64)],
                       self.ident[:, :], start=True, stop=False)
            mmk = p.t.matmul(ptp[0:64, 128:256], qspl[:, rr(h, 64)],
                             self.ident[:, :], start=False, stop=True,
                             _inc=True)
            p.v.wait(mmk)
            p.v.tensor_copy(krh[:, rr(h * 4 + slot, 128)], ptp[0:64, 0:128])
            last = p.v.tensor_copy(krl[:, rr(h * 4 + slot, 128)],
                                   ptp[0:64, 128:256], _inc=True)
            prev_ev[hp] = last
        return last

    def qk_common(self, ps_t, mm_tok, L, par, rtbase):
        p = self.p
        rt, s2, rsq, sqf = L["rt"], L["s2"], L["rsq"], L["sqf"]
        tq1, tq2 = L["tq1"], L["tq2"]
        qsph, qspl = L["qsph"], L["qspl"]
        p.s.wait(mm_tok)
        p.s.wait(L["trt"])
        tsq = p.s.activation(sqf[:, :], ps_t[:, :], AF.Square, _inc=True)
        p.v.wait(tsq)
        p.v.tensor_reduce(s2[:, :],
                          AP(sqf, 0, [[512, 128], [64, 8], [1, 64]]),
                          AX.X, ALU.add)
        p.v.drain()
        p.v.tensor_scalar(s2[:, :], s2[:, :], 1e-24, None, ALU.max)
        p.v.drain()
        p.v.reciprocal(rsq[:, :], s2[:, :])
        p.v.drain()
        t_r = p.v.nop(_inc=True)
        p.s.wait(t_r)
        tsr = p.s.activation(rsq[:, :], rsq[:, :], AF.Sqrt, _inc=True)
        p.v.wait(tsr)
        p.v.tensor_tensor(
            AP(tq1, 0, [[512, 128], [64, 8], [1, 64]]),
            AP(ps_t, 0, [[512, 128], [64, 8], [1, 64]]),
            AP(rt, par * 256 + rtbase, [[512, 128], [0, 8], [1, 64]]),
            ALU.mult)
        p.v.tensor_tensor(
            AP(tq2, 0, [[512, 128], [64, 8], [1, 32]]),
            AP(ps_t, 32, [[512, 128], [64, 8], [1, 32]]),
            AP(rt, par * 256 + rtbase + 64, [[512, 128], [0, 8], [1, 32]]),
            ALU.mult)
        p.v.tensor_tensor(
            AP(tq2, 32, [[512, 128], [64, 8], [1, 32]]),
            AP(ps_t, 0, [[512, 128], [64, 8], [1, 32]]),
            AP(rt, par * 256 + rtbase + 96, [[512, 128], [0, 8], [1, 32]]),
            ALU.mult)
        p.v.drain()
        p.v.tensor_tensor(tq1[:, :], tq1[:, :], tq2[:, :], ALU.add)
        p.v.drain()
        p.v.tensor_tensor(
            AP(tq1, 0, [[512, 128], [64, 8], [1, 64]]),
            AP(tq1, 0, [[512, 128], [64, 8], [1, 64]]),
            AP(rsq, 0, [[8, 128], [1, 8], [0, 64]]), ALU.mult)
        p.v.drain()
        return self.split(tq1[:, :], qsph[:, :], qspl[:, :])

    def proc_v(self, t, par, mm_tok, L):
        p = self.p
        ps_t = self.pp[0]
        v_ring = L["v_ring"]
        slot = t % 4
        rc = L["ring_cons"].get(slot)
        p.v.wait(mm_tok)
        if rc is not None:
            p.v.wait(rc)
        p.v.tensor_copy(
            AP(v_ring, slot * 128, [[8 * 4 * 128, 128], [512, 8], [1, 64]]),
            AP(ps_t, 0, [[512, 128], [64, 8], [1, 64]]))
        p.v.drain()
        return p.v.tensor_tensor(
            AP(v_ring, slot * 128 + 64,
               [[8 * 4 * 128, 128], [512, 8], [1, 64]]),
            AP(ps_t, 0, [[512, 128], [64, 8], [1, 64]]),
            AP(v_ring, slot * 128, [[8 * 4 * 128, 128], [512, 8], [1, 64]]),
            ALU.subtract, _inc=True)

    def attn_tile(self, t, to, cls, hh, L):
        p = self.p
        pp = self.pp
        q_sh, q_sl, v_ring = L["q_sh"], L["q_sl"], L["v_ring"]
        krh, krl = L["krh"], L["krl"]
        sm, astat, pfi = L["sm"], L["astat"], L["pfi"]
        phb, plb, pth, ptl = L["phb"], L["plb"], L["pth"], L["ptl"]
        ogh, ogl = L["ogh"], L["ogl"]
        last_tok = None
        og_guard = getattr(self, "_og_cons", None) if to % 4 == 0 else None
        for h in range(8):
            psS = pp[4]
            for jc in range(3):
                slot = (t - 2 + jc) % 4
                p.t.matmul(psS[:, rr(jc, 128)], q_sh[:, rr(h, 128)],
                           krh[:, rr(h * 4 + slot, 128)],
                           start=(jc == 0), stop=False)
                p.t.matmul(psS[:, rr(jc, 128)], q_sh[:, rr(h, 128)],
                           krl[:, rr(h * 4 + slot, 128)],
                           start=False, stop=False)
                mmS = p.t.matmul(psS[:, rr(jc, 128)], q_sl[:, rr(h, 128)],
                                 krh[:, rr(h * 4 + slot, 128)],
                                 start=False, stop=(jc == 2),
                                 _inc=(jc == 2))
            p.v.wait(mmS)
            p.v.scalar_tensor_tensor(sm[:, :], psS[:, 0:384], QKS,
                                     self.masks[:, rr(cls, 384)], ALU.mult,
                                     ALU.add)
            p.v.drain()
            p.v.tensor_reduce(astat[:, 0:1], sm[:, :], AX.X, ALU.max)
            p.v.drain()
            tneg = p.v.tensor_scalar(astat[:, 1:2], astat[:, 0:1], -1.0, None,
                                     ALU.mult, _inc=True)
            p.s.wait(tneg)
            texp = p.s.activation(pfi[:, :], sm[:, :], AF.Exp,
                                  bias=astat[:, 1:2],
                                  accum_out=astat[:, 2:3], _inc=True)
            p.v.wait(texp)
            p.v.reciprocal(astat[:, 3:4], astat[:, 2:3])
            p.v.drain()
            p.v.tensor_scalar(phb[:, :], pfi[:, :], astat[:, 3:4], None,
                              ALU.mult)
            p.v.drain()
            tspl = p.v.scalar_tensor_tensor(plb[:, :], pfi[:, :],
                                            astat[:, 3:4], phb[:, :],
                                            ALU.mult, ALU.subtract, _inc=True)
            p.t.wait(tspl)
            for jc in range(3):
                p.t.matmul(pp[5][:, rr(jc, 128)], phb[:, rr(jc, 128)],
                           self.ident[:, :], start=(jc == 0), stop=(jc == 2))
            for jc in range(3):
                mmT = p.t.matmul(pp[6][:, rr(jc, 128)], plb[:, rr(jc, 128)],
                                 self.ident[:, :], start=(jc == 0),
                                 stop=(jc == 2), _inc=(jc == 2))
            p.v.wait(mmT)
            p.v.tensor_copy(pth[:, :], pp[5][:, 0:384])
            tptl = p.v.tensor_copy(ptl[:, :], pp[6][:, 0:384], _inc=True)
            p.t.wait(tptl)
            if last_tok is not None:
                p.t.wait(last_tok)
            psO = pp[7][0:64, 0:128]
            for jc in range(3):
                slot = (t - 2 + jc) % 4
                vh = v_ring[:, (h * 4 + slot) * 128:
                            (h * 4 + slot) * 128 + 64]
                vl = v_ring[:, (h * 4 + slot) * 128 + 64:
                            (h * 4 + slot) * 128 + 128]
                p.t.matmul(psO, vh, pth[:, rr(jc, 128)],
                           start=(jc == 0), stop=False)
                p.t.matmul(psO, vh, ptl[:, rr(jc, 128)], start=False,
                           stop=False)
                mmO = p.t.matmul(psO, vl, pth[:, rr(jc, 128)], start=False,
                                 stop=(jc == 2), _inc=(jc == 2))
            p.v.wait(mmO)
            if og_guard is not None and h == 0:
                p.v.wait(og_guard)
            col = h * 512 + (to % 4) * 128
            p.v.tensor_copy(ogh[:, col:col + 128], psO)
            p.v.drain()
            last_tok = p.v.tensor_tensor(ogl[:, col:col + 128], psO,
                                         ogh[:, col:col + 128], ALU.subtract,
                                         _inc=True)
        return last_tok

    def flush_og(self, grp, hh, L, atok):
        p = self.p
        ogh, ogl = L["ogh"], L["ogl"]
        oi = self.oi_scr_t.ap()
        p.g.wait(atok)
        toks = []
        for h in range(8):
            row = hh * 512 + h * 64
            toks.append(p.g.dma(oi[row:row + 64, rr(grp, 512)],
                                ogh[:, rr(h, 512)], ch="og"))
            toks.append(p.g.dma(oi[D + row:D + row + 64, rr(grp, 512)],
                                ogl[:, rr(h, 512)], ch="og"))
        return toks[-1]

    def phase_wo(self, layer):
        p = self.p
        pp = self.pp
        sv = self.ectx
        with contextlib.ExitStack() as actx:
            self.ectx = actx
            woh = self.sbuf("wo_h", [128, 8 * 1024], BF16)
            wol = self.sbuf("wo_l", [128, 8 * 1024], BF16)
            wst = self.sbuf("wo_st", [128, 2 * 1024])
            obh = self.sbuf("ob_h", [128, 8 * 512], BF16)
            obl = self.sbuf("ob_l", [128, 8 * 512], BF16)
            wtoks = self.load_split_w(self.wo_i_t, layer * D, slice(0, D),
                                      8, 1024, wst, woh, wol, ch="wwo")
            p.t.wait_all(wtoks)
            oi = self.oi_scr_t.ap()
            ev_par = [None, None]
            for g in range(4):
                ltoks = []
                for c in range(8):
                    ltoks.append(p.y.dma(obh[:, rr(c, 512)],
                                         oi[rr(c, 128), rr(g, 512)],
                                         ch="ob"))
                    ltoks.append(p.y.dma(obl[:, rr(c, 512)],
                                         oi[D + c * 128: D + (c + 1) * 128,
                                            rr(g, 512)], ch="ob"))
                p.t.wait_all(ltoks)
                last_ev = None
                for of in range(8):
                    ps = pp[of % 2][:, :]
                    if ev_par[of % 2] is not None:
                        p.t.wait(ev_par[of % 2])
                    for kc in range(8):
                        mm = self.mm3(ps,
                                      woh[:, kc * 1024 + of * 128:
                                          kc * 1024 + (of + 1) * 128],
                                      wol[:, kc * 1024 + of * 128:
                                          kc * 1024 + (of + 1) * 128],
                                      obh[:, rr(kc, 512)],
                                      obl[:, rr(kc, 512)],
                                      start=(kc == 0), stop=(kc == 7),
                                      inc=(kc == 7))
                    p.v.wait(mm)
                    dst = self.st(of, HALO + g * 512, HALO + (g + 1) * 512)
                    ev = p.v.tensor_tensor(dst, dst, ps, ALU.add, _inc=True)
                    ev_par[of % 2] = ev
                    last_ev = ev
                p.y.wait(("tE", p.cnt["tE"]))
            self.ectx = sv

    def phase_ffn(self, layer):
        p = self.p
        pp = self.pp
        sv = self.ectx
        with contextlib.ExitStack() as actx:
            self.ectx = actx
            xnh = self.sbuf("xn_h", [128, 8 * NT], BF16)
            xnl = self.sbuf("xn_l", [128, 8 * NT], BF16)
            uh = self.sbuf("uf_h", [128, 512], BF16)
            ul = self.sbuf("uf_l", [128, 512], BF16)
            u2f = self.sbuf("u2f", [128, 512])
            u2h = self.sbuf("u2h", [128, 512], BF16)
            u2l = self.sbuf("u2l", [128, 512], BF16)
            rowp = self.sbuf("rowp", [1, 5 * 512])
            rwh = self.sbuf("rwh", [1, 4 * 512], BF16)
            gbh = self.sbuf("gbh", [1, 4 * D], BF16)
            ones5 = self.sbuf("ones512", [1, 512], BF16)
            xnf = self.sbuf("xnf", [128, 512])
            hgf = self.sbuf("hgf", [128, 256])
            hfh = self.sbuf("hf_h", [128, 256], BF16)
            hfl = self.sbuf("hf_l", [128, 256], BF16)
            w1h = self.sbuf("w1_h", [128, GBLK * 8 * 256], BF16)
            w1l = self.sbuf("w1_l", [128, GBLK * 8 * 256], BF16)
            w2h = self.sbuf("w2_h", [128, GBLK * 1024], BF16)
            w2l = self.sbuf("w2_l", [128, GBLK * 1024], BF16)
            wst = self.sbuf("wf_st", [128, 2 * 512])
            b2c = self.sbuf("b2c", [128, 8])
            b1c = self.sbuf("b1c", [128, 2 * len(ICHUNKS)])

            p.g.memset(ones5[:, :], 1.0)
            p.g.drain()
            tg1 = p.g.nop(_inc=True)
            tgb = p.y.dma(rowp[0:1, 0:D], self.lng_t.ap()[layer:layer + 1, :], ch="lnb")
            p.v.wait(tgb)
            tg_s = self.split(rowp[0:1, 0:D], gbh[:, 0:D], gbh[:, D:2 * D])
            p.y.wait(tg_s)
            tgb = p.y.dma(rowp[0:1, 0:D], self.lnb_t.ap()[layer:layer + 1, :], ch="lnb")
            p.v.wait(tgb)
            tspl = self.split(rowp[0:1, 0:D], gbh[:, 2 * D:3 * D],
                              gbh[:, 3 * D:4 * D])
            tb2 = p.y.dma(b2c[:, :], AP(self.b2T_t, layer * D,
                                        [[1, 128], [128, 8]]),
                          ch="bias2", allow_slow_non_contiguous=True)
            b1toks = []
            for ci, (c0, w) in enumerate(ICHUNKS):
                b1toks.append(p.y.dma(
                    b1c[0:w, 2 * ci:2 * ci + 1],
                    self.b1T_t.ap()[layer * 2 * DI + c0:
                                    layer * 2 * DI + c0 + w, :], ch="bias2"))
                b1toks.append(p.y.dma(
                    b1c[0:w, 2 * ci + 1:2 * ci + 2],
                    self.b1T_t.ap()[layer * 2 * DI + DI + c0:
                                    layer * 2 * DI + DI + c0 + w, :],
                    ch="bias2"))

            stats_cons = None
            prev_spu2 = None
            for g in range(4):
                a0 = HALO + g * 512
                for c in range(8):
                    sh = -1 if c >= 4 else 0
                    u_ap = self.st(c, a0 + sh, a0 + 512 + sh)
                    tspu = self.split(u_ap, uh[:, :], ul[:, :])
                    if prev_spu2 is not None:
                        p.s.wait(prev_spu2)
                    tq = p.s.activation(u2f[:, :], u_ap, AF.Square, _inc=True)
                    p.v.wait(tq)
                    tspu2 = self.split(u2f[:, :], u2h[:, :], u2l[:, :])
                    prev_spu2 = tspu2
                    p.t.wait(tspu)
                    p.t.wait(tspu2)
                    if c == 0 and stats_cons is not None:
                        p.t.wait(stats_cons)
                    p.t.matmul(pp[2][0:1, :], self.ones_col[:, :], uh[:, :],
                               start=(c == 0), stop=False)
                    p.t.matmul(pp[2][0:1, :], self.ones_col[:, :], ul[:, :],
                               start=False, stop=(c == 7))
                    p.t.matmul(pp[3][0:1, :], self.ones_col[:, :], u2h[:, :],
                               start=(c == 0), stop=False)
                    tmm = p.t.matmul(pp[3][0:1, :], self.ones_col[:, :],
                                     u2l[:, :], start=False, stop=(c == 7),
                                     _inc=True)
                    p.v.wait(tmm)
                p.v.tensor_scalar(rowp[0:1, 0:512], pp[2][0:1, :], 1.0 / D,
                                  None, ALU.mult)
                p.v.drain()
                p.v.tensor_tensor(rowp[0:1, 512:1024], rowp[0:1, 0:512],
                                  rowp[0:1, 0:512], ALU.mult)
                p.v.drain()
                p.v.scalar_tensor_tensor(rowp[0:1, 1024:1536], pp[3][0:1, :],
                                         1.0 / D, rowp[0:1, 512:1024],
                                         ALU.mult, ALU.subtract)
                p.v.drain()
                stats_cons = tvar = p.v.nop(_inc=True)
                p.s.wait(tvar)
                tsr = p.s.activation(rowp[0:1, 1024:1536], rowp[0:1, 1024:1536],
                                     AF.Sqrt, bias=self.eps[0:1, :], _inc=True)
                p.v.wait(tsr)
                p.v.reciprocal(rowp[0:1, 1536:2048], rowp[0:1, 1024:1536])
                p.v.drain()
                p.v.tensor_tensor(rowp[0:1, 2048:2560], rowp[0:1, 0:512],
                                  rowp[0:1, 1536:2048], ALU.mult)
                p.v.drain()
                p.v.tensor_scalar(rowp[0:1, 2048:2560], rowp[0:1, 2048:2560],
                                  -1.0, None, ALU.mult)
                p.v.drain()
                self.split(rowp[0:1, 1536:2048], rwh[:, 0:512],
                           rwh[:, 512:1024])
                trw = self.split(rowp[0:1, 2048:2560], rwh[:, 1024:1536],
                                 rwh[:, 1536:2048])
                p.t.wait(trw)
                if g == 0:
                    p.t.wait(tspl)
                for c in range(8):
                    if g == 0 and c == 0:
                        p.t.wait(tg1)
                    gh_c = gbh[0:1, rr(c, 128)]
                    gl_c = gbh[0:1, D + c * 128: D + (c + 1) * 128]
                    bh_c = gbh[0:1, 2 * D + c * 128: 2 * D + (c + 1) * 128]
                    bl_c = gbh[0:1, 3 * D + c * 128: 3 * D + (c + 1) * 128]
                    p.t.matmul(pp[4][:, :], gh_c, rwh[:, 0:512],
                               start=True, stop=False)
                    p.t.matmul(pp[4][:, :], gh_c, rwh[:, 512:1024],
                               start=False, stop=False)
                    p.t.matmul(pp[4][:, :], gl_c, rwh[:, 0:512],
                               start=False, stop=True)
                    p.t.matmul(pp[5][:, :], gh_c, rwh[:, 1024:1536],
                               start=True, stop=False)
                    p.t.matmul(pp[5][:, :], gh_c, rwh[:, 1536:2048],
                               start=False, stop=False)
                    p.t.matmul(pp[5][:, :], gl_c, rwh[:, 1024:1536],
                               start=False, stop=False)
                    p.t.matmul(pp[5][:, :], bh_c, ones5[:, :],
                               start=False, stop=False)
                    tm2 = p.t.matmul(pp[5][:, :], bl_c, ones5[:, :],
                                     start=False, stop=True, _inc=True)
                    p.v.wait(tm2)
                    sh = -1 if c >= 4 else 0
                    u_ap = self.st(c, a0 + sh, a0 + 512 + sh)
                    p.v.tensor_tensor(xnf[:, :], u_ap, pp[4][:, :], ALU.mult)
                    p.v.drain()
                    p.v.tensor_tensor(xnf[:, :], xnf[:, :], pp[5][:, :],
                                      ALU.add)
                    p.v.drain()
                    txn = self.split(xnf[:, :],
                                     xnh[:, c * NT + g * 512:
                                         c * NT + (g + 1) * 512],
                                     xnl[:, c * NT + g * 512:
                                         c * NT + (g + 1) * 512])
                    p.t.wait(txn)
            p.barrier()

            psa_cons = {}
            psg_cons = {}
            for bi, blk in enumerate(IBLOCKS):
                p.y.wait(("tE", p.cnt["tE"]))
                w1toks = []
                w2toks = []
                for ci_in_b, (c0, w) in enumerate(blk):
                    w1toks += self.load_split_w(
                        self.w1_i_t, layer * D, slice(c0, c0 + w), 8, w,
                        wst, w1h, w1l, dst_col0=ci_in_b * 8 * 256,
                        dst_stride=256, ch="wf")
                    w1toks += self.load_split_w(
                        self.w1_i_t, layer * D, slice(DI + c0, DI + c0 + w),
                        8, w, wst, w1h, w1l,
                        dst_col0=ci_in_b * 8 * 256 + 128, dst_stride=256,
                        ch="wf")
                    w2toks += self.load_split_w(
                        self.w2_i_t, layer * DI + c0, slice(0, 512), 1,
                        512, wst, w2h, w2l, dst_col0=ci_in_b * 1024, ch="wf")
                    w2toks += self.load_split_w(
                        self.w2_i_t, layer * DI + c0, slice(512, 1024), 1,
                        512, wst, w2h, w2l, dst_col0=ci_in_b * 1024 + 512,
                        ch="wf")
                p.t.wait_all(w1toks + w2toks)
                for g in range(8):
                    a0 = HALO + g * 256
                    for ci_in_b, (c0, w) in enumerate(blk):
                        ci = bi * GBLK + ci_in_b
                        par = (g * len(blk) + ci_in_b) % 2
                        psa = pp[par][0:w, 0:256]
                        psg = pp[2 + par][0:w, 0:256]
                        if psa_cons.get(par) is not None:
                            p.t.wait(psa_cons[par])
                        if psg_cons.get(par) is not None:
                            p.t.wait(psg_cons[par])
                        for kc in range(8):
                            self.mm3(psa,
                                     w1h[:, ci_in_b * 8 * 256 + kc * 256:
                                         ci_in_b * 8 * 256 + kc * 256 + w],
                                     w1l[:, ci_in_b * 8 * 256 + kc * 256:
                                         ci_in_b * 8 * 256 + kc * 256 + w],
                                     xnh[:, kc * NT + g * 256:
                                         kc * NT + (g + 1) * 256],
                                     xnl[:, kc * NT + g * 256:
                                         kc * NT + (g + 1) * 256],
                                     start=(kc == 0), stop=(kc == 7),
                                     inc=False)
                        for kc in range(8):
                            self.mm3(psg,
                                     w1h[:, ci_in_b * 8 * 256 + kc * 256 + 128:
                                         ci_in_b * 8 * 256 + kc * 256 + 128 + w],
                                     w1l[:, ci_in_b * 8 * 256 + kc * 256 + 128:
                                         ci_in_b * 8 * 256 + kc * 256 + 128 + w],
                                     xnh[:, kc * NT + g * 256:
                                         kc * NT + (g + 1) * 256],
                                     xnl[:, kc * NT + g * 256:
                                         kc * NT + (g + 1) * 256],
                                     start=(kc == 0), stop=(kc == 7),
                                     inc=(kc == 7))
                        mmw1 = ("tE", p.cnt["tE"])
                        p.s.wait(mmw1)
                        tgl = p.s.activation(hgf[0:w, :], psg, AF.Gelu,
                                             bias=b1c[0:w, 2 * ci + 1:
                                                      2 * ci + 2],
                                             scale=1.0, _inc=True)
                        psg_cons[par] = tgl
                        p.v.wait(tgl)
                        tsa = p.v.scalar_tensor_tensor(
                            xnf[0:w, 0:256], psa,
                            b1c[0:w, 2 * ci: 2 * ci + 1],
                            hgf[0:w, :], ALU.add, ALU.mult, _inc=True)
                        psa_cons[par] = tsa
                        p.v.drain()
                        p.v.tensor_copy(hfh[0:w, :], xnf[0:w, 0:256])
                        p.v.drain()
                        thl = p.v.tensor_tensor(hfl[0:w, :], xnf[0:w, 0:256],
                                                hfh[0:w, :], ALU.subtract,
                                                _inc=True)
                        p.t.wait(thl)
                        first = ci_in_b == 0
                        last = ci_in_b == len(blk) - 1
                        for of in range(8):
                            pso = pp[4 + of // 2][:, (of % 2) * 256:
                                                  (of % 2) * 256 + 256]
                            self.mm3(pso,
                                     w2h[0:w, ci_in_b * 1024 + of * 128:
                                         ci_in_b * 1024 + (of + 1) * 128],
                                     w2l[0:w, ci_in_b * 1024 + of * 128:
                                         ci_in_b * 1024 + (of + 1) * 128],
                                     hfh[0:w, :], hfl[0:w, :],
                                     start=(first and (of % 2 == 0)),
                                     stop=(last and (of % 2 == 1)),
                                     inc=(of == 7))
                        mmw2 = ("tE", p.cnt["tE"])
                        p.v.wait(mmw2)
                    for of in range(8):
                        pso = pp[4 + of // 2][:, (of % 2) * 256:
                                              (of % 2) * 256 + 256]
                        dst = self.st(of, a0, a0 + 256)
                        if bi == len(IBLOCKS) - 1:
                            ev = p.v.scalar_tensor_tensor(
                                dst, pso, b2c[:, of:of + 1], dst, ALU.add,
                                ALU.add, _inc=(of == 7))
                        else:
                            ev = p.v.tensor_tensor(dst, dst, pso, ALU.add,
                                                   _inc=(of == 7))
                    p.t.wait(ev)
            self.ectx = sv
        p.barrier()

    def phase_final(self):
        p = self.p
        pp = self.pp
        sv = self.ectx
        with contextlib.ExitStack() as actx:
            self.ectx = actx
            yh = self.sbuf("y_h", [128, D], BF16)
            yl = self.sbuf("y_l", [128, D], BF16)
            yt = self.sbuf("y_t2", [128, 2 * D])
            x1t = self.sbuf("x1b", [128, 2 * D])
            sqt = self.sbuf("sqF", [128, D])
            stat = self.sbuf("statF", [128, 4])
            dma_done = {}
            prev_tr = None
            for t in range(NTILE):
                par = t % 2
                a0 = HALO + t * 128
                if prev_tr is not None:
                    p.v.wait(prev_tr)
                tspl = self.split(
                    AP(self.stream, a0, [[8 * TBUF, 128], [TBUF, 4], [1, 128]]),
                    AP(yh, 0, [[D, 128], [128, 4], [1, 128]]),
                    AP(yl, 0, [[D, 128], [128, 4], [1, 128]]))
                tspl = self.split(
                    AP(self.stream, 4 * TBUF + a0,
                       [[8 * TBUF, 128], [TBUF, 4], [1, 128]]),
                    AP(yh, 512, [[D, 128], [128, 4], [1, 128]]),
                    AP(yl, 512, [[D, 128], [128, 4], [1, 128]]))
                p.t.wait(tspl)
                pd = dma_done.get(t - 2)
                if pd is not None:
                    p.y.wait(pd)
                tx1 = p.y.dma(x1t[:, rr(par, D)],
                              self.x1_scr_t.ap()[rr(t, 128), :],
                              ch="fx%d" % par)
                if pd is not None:
                    p.t.wait(pd)
                    p.v.wait(pd)
                for c in range(4):
                    ppt = pp[c % 2][:, 0:256]
                    mmt = self.tr_pair(yh[:, rr(2 * c, 128)],
                                       yl[:, rr(2 * c, 128)],
                                       ppt[:, 0:128], start=True, stop=False)
                    mmt = self.tr_pair(yh[:, rr(2 * c + 1, 128)],
                                       yl[:, rr(2 * c + 1, 128)],
                                       ppt[:, 128:256], start=False, stop=True)
                    p.v.wait(mmt)
                    p.v.tensor_copy(yt[:, par * D + 2 * c * 128:
                                       par * D + (2 * c + 2) * 128], ppt)
                    p.v.drain()
                    tev = p.v.nop(_inc=True)
                    p.t.wait(tev)
                prev_tr = ("tE", p.cnt["tE"])
                p.v.wait(tx1)
                p.v.drain()
                p.v.tensor_tensor(yt[:, rr(par, D)], yt[:, rr(par, D)],
                                  x1t[:, rr(par, D)], ALU.add)
                p.v.drain()
                p.v.tensor_reduce(stat[:, 0:1], yt[:, rr(par, D)], AX.X,
                                  ALU.add)
                p.v.drain()
                p.v.tensor_scalar(stat[:, 1:2], stat[:, 0:1], 1.0 / D, None,
                                  ALU.mult)
                p.v.drain()
                p.v.tensor_scalar(yt[:, rr(par, D)], yt[:, rr(par, D)],
                                  stat[:, 1:2], None, ALU.subtract)
                p.v.drain()
                p.v.drain()
                tsq = p.v.nop(_inc=True)
                p.s.wait(tsq)
                t2 = p.s.activation(sqt[:, :], yt[:, rr(par, D)], AF.Square,
                                    accum_out=stat[:, 2:3], _inc=True)
                p.v.wait(t2)
                p.v.drain()
                tvv = p.v.nop(_inc=True)
                p.s.wait(tvv)
                t3 = p.s.activation(stat[:, 2:3], stat[:, 2:3], AF.Sqrt,
                                    scale=1.0 / D, bias=self.eps[:, :], _inc=True)
                p.v.wait(t3)
                p.v.reciprocal(stat[:, 3:4], stat[:, 2:3])
                p.v.drain()
                ty = p.v.tensor_scalar(yt[:, rr(par, D)], yt[:, rr(par, D)],
                                       stat[:, 3:4], None, ALU.mult,
                                       _inc=True)
                p.g.wait(ty)
                dma_done[t] = p.g.dma(self.y_t.ap()[rr(t, 128), :],
                                      yt[:, rr(par, D)], ch="yw%d" % par)
            p.g.wait(dma_done[NTILE - 1])
            p.g.wait_all(list(dma_done.values()))
            self.ectx = sv


def build_nc(stage=STAGE, n_layers=NL):
    return Kern(stage, n_layers).build()



def kernel(L0_attn_x, L0_attn_W_qkv, L0_attn_b_qkv, L0_ffn_Wqkv,
           L0_ffn_q_scale, L0_ffn_k_scale, L0_ffn_Wo, L0_ffn_ln_g,
           L0_ffn_ln_b, L0_ffn_W1, L0_ffn_b1, L0_ffn_W2, L0_ffn_b2):
    x = np.ascontiguousarray(np.asarray(L0_attn_x, dtype=np.float32))
    wqkv_g = np.asarray(L0_attn_W_qkv, np.float32)
    bqkv_g = np.asarray(L0_attn_b_qkv, np.float32)
    nc = build_nc()
    f32c = lambda a: np.ascontiguousarray(np.asarray(a, np.float32))
    wqkv_i = f32c(L0_ffn_Wqkv).reshape(NL * D, 3 * D)
    wo_i = f32c(L0_ffn_Wo).reshape(NL * D, D)
    w1_i = f32c(L0_ffn_W1).reshape(NL * D, 2 * DI)
    w2_i = f32c(L0_ffn_W2).reshape(NL * DI, D)
    w2_i = np.concatenate([w2_i, np.zeros((128, D), np.float32)], axis=0)
    b1T = f32c(L0_ffn_b1).reshape(NL * 2 * DI, 1)
    b2T = f32c(L0_ffn_b2).reshape(NL * D, 1)
    lng = f32c(L0_ffn_ln_g)
    lnb = f32c(L0_ffn_ln_b)
    in_maps = []
    for c in range(NCORE):
        b, half = c // 2, c % 2
        hc = host_constants(c)
        h0 = half * NHG
        cols = []
        for lh in range(NHG):
            h = h0 + lh
            cols += list(range(h * 64, h * 64 + 64))
            cols += list(range(D + h * 64, D + h * 64 + 64))
            cols += list(range(2 * D + h * 64, 2 * D + h * 64 + 64))
        cols = np.asarray(cols)
        m = {
            "x": x[b],
            "x_own": np.ascontiguousarray(x[b, half * NT:(half + 1) * NT]),
            "wqkv_g": np.ascontiguousarray(wqkv_g[:, cols]),
            "bqkv_gT": np.ascontiguousarray(bqkv_g[cols][:, None]),
            "ident": hc["ident"], "ones_row": hc["ones_row"],
            "ones_col": hc["ones_col"], "ones_sq": hc["ones_sq"],
            "masks": hc["masks"], "halo_gate": hc["halo_gate"],
            "rotab": host_rotab(c, L0_ffn_q_scale, L0_ffn_k_scale),
            "wqkv_i": wqkv_i, "wo_i": wo_i, "w1_i": w1_i, "w2_i": w2_i,
            "b1T": b1T, "b2T": b2T, "lng": lng, "lnb": lnb,
        }
        in_maps.append(m)
    res = run_bass_kernel_spmd(nc, in_maps, core_ids=list(range(NCORE)))
    out = np.empty((BATCH, SEQ, D), np.float32)
    for c in range(NCORE):
        b, half = c // 2, c % 2
        out[b, half * NT:(half + 1) * NT] = res.results[c]["y"]
    return out


# version 04 (final)
# speedup vs baseline: 1.0426x; 1.0315x over previous
import contextlib

import numpy as np
import concourse.bass as bass
import concourse.mybir as mybir
from concourse.bass_utils import run_bass_kernel_spmd

F32 = mybir.dt.float32
BF16 = mybir.dt.bfloat16
AF = mybir.ActivationFunctionType
ALU = mybir.AluOpType
AX = mybir.AxisListType
AP = bass.AP

D = 1024
SEQ = 4096
BATCH = 4
NCORE = 8
NT = 2048
HALO = 260
TBUF = HALO + NT
NHG = 8
DH = 64
WIN = 256
QKS = 8.0
DI = 2730
NL = 6
GB_MAR2 = 1.0201
GB_SHIFT = 75.0
NTILE = NT // 128
HTILE = 2
ATILE = NTILE + HTILE
NEG = -1.0e9
GBLK = 2
PAIRS = [[0, 1], [2, 3], [4, 5], [6, 7]]
STAGE = "full"
ENGINES = ("sync", "tensor", "vector", "scalar", "gpsimd")

ICHUNKS = [(i * 128, 128) for i in range(21)] + [(21 * 128, 42)]
IBLOCKS = []
_i = 0
while _i < len(ICHUNKS):
    IBLOCKS.append(ICHUNKS[_i:_i + GBLK])
    _i += GBLK


def host_constants(core):
    import ml_dtypes
    half = core % 2
    out = {}
    out["ident"] = np.eye(128, dtype=ml_dtypes.bfloat16)
    out["ones_row"] = np.ones((1, 128), dtype=ml_dtypes.bfloat16)
    out["ones_col"] = np.ones((128, 1), dtype=ml_dtypes.bfloat16)
    out["ones_sq"] = np.ones((128, 128), dtype=ml_dtypes.bfloat16)
    r = np.arange(128)[:, None]
    c = np.arange(384)[None, :]
    tri = np.zeros((128, 384), np.float32)
    tri[:, :128] = np.where(c[:, :128] >= r, 0.0, NEG)
    tri[:, 256:384] = np.where((c[:, 256:384] - 256) <= r, 0.0, NEG)
    m0 = tri.copy()
    m1 = tri.copy()
    if half == 0:
        m0[:, :256] = NEG
        m1[:, :128] = NEG
    out["masks"] = np.ascontiguousarray(np.concatenate([m0, m1, tri], axis=1))
    out["halo_gate"] = np.full((128, 1), float(half), np.float32)
    return out


def host_rotab(core, q_scale, k_scale):
    half = core % 2
    s = half * NT
    t = np.arange(s - HALO, s + NT, dtype=np.float64)
    d_half = np.arange(0, DH, 2, dtype=np.float64)
    inv_freq = 1.0 / 10000.0 ** (d_half / DH)
    freqs = t[:, None] * inv_freq[None, :]
    freqs = np.concatenate([freqs, freqs], -1)
    base_scale = (d_half + 0.4 * DH) / (1.4 * DH)
    power = (t - SEQ // 2) / (WIN // 2)
    scale = base_scale[None, :] ** power[:, None]
    scale = np.concatenate([scale, scale], -1)
    cos = np.cos(freqs)
    sin = np.sin(freqs)
    inv = 1.0 / scale
    qs = np.asarray(q_scale, np.float64)
    ks = np.asarray(k_scale, np.float64)
    tabs = np.empty((NL, TBUF, 4 * DH), np.float32)
    for l in range(NL):
        cq = cos * scale * qs[l][None, :]
        ck = cos * inv * ks[l][None, :]
        sq = sin * scale
        sk = sin * inv
        saq = np.empty_like(sq)
        saq[:, :32] = -sq[:, :32] * qs[l][None, 32:64]
        saq[:, 32:] = sq[:, 32:] * qs[l][None, 0:32]
        sak = np.empty_like(sk)
        sak[:, :32] = -sk[:, :32] * ks[l][None, 32:64]
        sak[:, 32:] = sk[:, 32:] * ks[l][None, 0:32]
        tabs[l] = np.concatenate([cq, saq, ck, sak], axis=1)
    return np.ascontiguousarray(tabs.reshape(NL * TBUF, 4 * DH))



class Em:
    def __init__(self, p, eng):
        self._p = p
        self._eng = eng

    def __getattr__(self, meth):
        p, eng = self._p, self._eng

        def call(*args, _inc=False, **kw):
            if _inc:
                sname = p.eng_sem_of[eng]
                sem = p.sem[sname]
                tok = p.tok(sname, 1)
                p.q[eng].append(
                    lambda e, a=args, k=kw, m=meth, s=sem:
                    getattr(e, m)(*a, **k).then_inc(s, 1))
                return tok
            p.q[eng].append(
                lambda e, a=args, k=kw, m=meth: getattr(e, m)(*a, **k))
            return None
        return call

    def dma(self, out, in_, ch=None, **kw):
        p, eng = self._p, self._eng
        sname = ch if ch is not None else p.dma_sem_of[eng]
        sem = p.get_dma_sem(sname)
        tok = p.tok(sname, 16)
        p.q[eng].append(
            lambda e, o=out, i=in_, s=sem, k=kw:
            e.dma_start(out=o, in_=i, **k).then_inc(s, 16))
        return tok

    def wait(self, tok):
        if tok is None:
            return
        p, eng = self._p, self._eng
        sname, v = tok
        sem = p.sem[sname]
        p.q[eng].append(lambda e, s=sem, vv=v: e.wait_ge(s, vv))

    def wait_all(self, toks):
        best = {}
        for t in toks:
            if t is None:
                continue
            sname, v = t
            best[sname] = max(best.get(sname, 0), v)
        for sname, v in best.items():
            self.wait((sname, v))


class P:
    def __init__(self, nc, ctx):
        self.nc = nc
        self.ctx = ctx
        self.q = {e: [] for e in ENGINES}
        self.sem = {}
        self.cnt = {}
        for name in ("dS", "dG", "tE", "vE", "sE", "gE", "yE", "cc", "bar"):
            self.sem[name] = ctx.enter_context(nc.semaphore(name))
            self.cnt[name] = 0
        self.n_barriers = 0
        self.dma_chs = set(["dS", "dG"])
        self.dma_sem_of = {"sync": "dS", "gpsimd": "dG"}
        self.eng_sem_of = {"tensor": "tE", "vector": "vE", "scalar": "sE",
                           "gpsimd": "gE", "sync": "yE"}
        self.y = Em(self, "sync")
        self.t = Em(self, "tensor")
        self.v = Em(self, "vector")
        self.s = Em(self, "scalar")
        self.g = Em(self, "gpsimd")
        self.all = (self.y, self.t, self.v, self.s, self.g)

    def tok(self, sname, n):
        self.cnt[sname] += n
        return (sname, self.cnt[sname])

    def get_dma_sem(self, sname):
        if sname not in self.sem:
            self.sem[sname] = self.ctx.enter_context(
                self.nc.semaphore(sname))
            self.cnt[sname] = 0
            self.dma_chs.add(sname)
        return self.sem[sname]

    def emit_raw(self, eng, fn):
        self.q[eng].append(fn)

    def collective(self, in_ap, out_ap):
        sem = self.sem["cc"]
        tok = self.tok("cc", 1)
        self.q["gpsimd"].append(
            lambda e, i=in_ap, o=out_ap, s=sem:
            e.collective_compute("AllGather", ALU.bypass,
                                 replica_groups=PAIRS, ins=[i],
                                 outs=[o]).then_inc(s, 1))
        self.q["gpsimd"].append(
            lambda e, s=sem, v=tok[1]: e.wait_ge(s, v))
        return tok

    def barrier(self):
        self.n_barriers += 1
        bar = self.sem["bar"]
        for em in self.all:
            eng = em._eng
            if eng != "sync":
                self.q[eng].append(lambda e: e.drain())
            self.q[eng].append(lambda e, b=bar: e.nop().then_inc(b, 1))
        toks = [("bar", 5 * self.n_barriers)]
        for ch in self.dma_chs:
            if self.cnt[ch]:
                toks.append((ch, self.cnt[ch]))
        for em in self.all:
            em.wait_all(toks)

    def finalize(self, block):
        for eng in ENGINES:
            dec = getattr(block, eng)
            lst = self.q[eng]

            def runner(e, lst=lst):
                for fn in lst:
                    fn(e)
            dec(runner)


def rr(i, n):
    return slice(i * n, (i + 1) * n)



class Kern:
    def __init__(self, stage=STAGE, n_layers=NL):
        self.stage = stage
        self.n_layers = n_layers
        nc = bass.Bass("TRN2", target_bir_lowering=False)
        self.nc = nc

        def dri(n, sh, dt=F32):
            h = nc.dram_tensor(n, sh, dt, kind="ExternalInput")
            return h
        self.x_own_t = dri("x_own", [NT, D])
        self.wqkv_g_t = dri("wqkv_g", [D, NHG * 192])
        self.bqkv_gT_t = dri("bqkv_gT", [NHG * 192, 1])
        self.ident_t = dri("ident", [128, 128], BF16)
        self.ones_row_t = dri("ones_row", [1, 128], BF16)
        self.ones_col_t = dri("ones_col", [128, 1], BF16)
        self.ones_sq_t = dri("ones_sq", [128, 128], BF16)
        self.masks_t = dri("masks", [128, 3 * 384])
        self.gate_t = dri("halo_gate", [128, 1])
        self.rotab_t = dri("rotab", [NL * TBUF, 4 * DH])
        self.x_t = dri("x", [SEQ, D])
        self.wqkv_i_t = dri("wqkv_i", [NL * D, 3 * D])
        self.wo_i_t = dri("wo_i", [NL * D, D])
        self.w1_i_t = dri("w1_i", [NL * D, 2 * DI])
        self.w2_i_t = dri("w2_i", [NL * DI + 128, D])
        self.b1T_t = dri("b1T", [NL * 2 * DI, 1])
        self.b2T_t = dri("b2T", [NL * D, 1])
        self.lng_t = dri("lng", [NL, D])
        self.lnb_t = dri("lnb", [NL, D])
        self.y_t = nc.dram_tensor("y", [NT, D], F32, kind="ExternalOutput")
        self.o_scr_t = nc.dram_tensor("o_scr", [NHG * 256, 16 * DH], F32)
        self.x1_scr_t = nc.dram_tensor("x1_scr", [NT, D], F32)
        self.oi_scr_t = nc.dram_tensor("oi_scr", [2 * D, NT], BF16)
        self.cb_in_t = nc.dram_tensor("cb_in", [128 * 8 * HALO], F32)
        self.cb_out_t = nc.dram_tensor("cb_out", [2 * 128 * 8 * HALO], F32)

    def sbuf(self, name, shape, dt=F32):
        self._uniq = getattr(self, "_uniq", 0) + 1
        return self.ectx.enter_context(
            self.nc.sbuf_tensor("%s_%d" % (name, self._uniq), shape, dt))

    def st(self, c, a, b):
        return self.stream[:, c * TBUF + a: c * TBUF + b]

    def build(self):
        nc = self.nc
        with contextlib.ExitStack() as ectx:
            self.ectx = ectx
            p = P(nc, ectx)
            self.p = p
            self.pp = [ectx.enter_context(nc.psum_tensor("pool%d" % i,
                                                         [128, 512], F32))
                       for i in range(8)]
            self.ident = self.sbuf("ident_sb", [128, 128], BF16)
            self.ones_row = self.sbuf("ones_row_sb", [1, 128], BF16)
            self.ones_col = self.sbuf("ones_col_sb", [128, 1], BF16)
            self.ones_sq = self.sbuf("ones_sq_sb", [128, 128], BF16)
            self.gate = self.sbuf("gate_sb", [128, 1])
            self.eps = self.sbuf("eps_sb", [128, 1])
            self.stream = None
            self.masks = None

            p.g.memset(self.eps[:, :], 1e-5)
            t0 = [p.y.dma(self.ident[:], self.ident_t.ap(), ch="init"),
                  p.y.dma(self.ones_row[:], self.ones_row_t.ap(), ch="init"),
                  p.y.dma(self.ones_col[:], self.ones_col_t.ap(), ch="init"),
                  p.y.dma(self.ones_sq[:], self.ones_sq_t.ap(), ch="init"),
                  p.y.dma(self.gate[:], self.gate_t.ap(), ch="init")]
            for em in p.all:
                em.wait_all(t0)
            p.barrier()

            self.phase_A()
            if self.stage != "a":
                self.exchange_halo(HALO)
                for layer in range(self.n_layers):
                    if layer > 0:
                        self.exchange_halo(HALO)
                    self.phase_attn(layer)
                    self.exchange_halo(1)
                    self.phase_ffn(layer)
                self.phase_final()

            with nc.Block() as block:
                p.finalize(block)
        return nc

    def split(self, src, hi, lo, em=None):
        p = self.p
        em = em or p.v
        em.tensor_copy(hi, src)
        em.drain()
        return em.tensor_tensor(lo, src, hi, ALU.subtract, _inc=True)

    def split_bias(self, src, bias, hi, lo):
        p = self.p
        th = p.s.activation(hi, src, AF.Identity, bias=bias, scale=1.0, _inc=True)
        p.v.wait(th)
        return p.v.scalar_tensor_tensor(lo, src, bias, hi, ALU.add,
                                        ALU.subtract, _inc=True)

    def tr_pair(self, hi, lo, out_psum, kdim=128, start=True, stop=True):
        p = self.p
        i_ = self.ident[0:kdim, 0:kdim]
        p.t.matmul(out_psum, hi, i_, start=start, stop=False)
        return p.t.matmul(out_psum, lo, i_, start=False, stop=stop, _inc=True)

    def mm3(self, out_psum, lh_h, lh_l, rh_h, rh_l, start, stop, inc=True):
        p = self.p
        p.t.matmul(out_psum, lh_h, rh_h, start=start, stop=False)
        p.t.matmul(out_psum, lh_h, rh_l, start=False, stop=False)
        return p.t.matmul(out_psum, lh_l, rh_h, start=False, stop=stop,
                          _inc=inc)

    def load_split_w(self, src_t, row0, col_sl, n_kc, ncols, st_buf, hi, lo,
                     dst_col0=0, dst_stride=None, ch="w"):
        p = self.p
        if dst_stride is None:
            dst_stride = ncols
        src = src_t.ap()
        if not hasattr(self, "_stage_guard"):
            self._stage_guard = {}
        if not hasattr(self, "_stage_geom"):
            self._stage_geom = {}
        if self._stage_geom.get(id(st_buf)) != ncols:
            for pq in range(4):
                tk = self._stage_guard.pop((id(st_buf), pq), None)
                if tk is not None:
                    p.y.wait(tk)
            self._stage_geom[id(st_buf)] = ncols
        toks = []
        for kc in range(n_kc):
            par = kc % 4
            key = (id(st_buf), par)
            prev = self._stage_guard.get(key)
            if prev is not None:
                p.y.wait(prev)
            tld = p.y.dma(st_buf[:, rr(par, ncols)],
                          src[row0 + kc * 128: row0 + (kc + 1) * 128, col_sl],
                          ch=ch + str(par))
            p.v.wait(tld)
            c0 = dst_col0 + kc * dst_stride
            tk = self.split(st_buf[:, rr(par, ncols)],
                            hi[:, c0:c0 + ncols], lo[:, c0:c0 + ncols])
            self._stage_guard[key] = tk
            toks.append(tk)
        return toks

    def exchange_halo(self, n):
        p = self.p
        p.barrier()
        nel = 128 * 8 * n
        src = AP(self.stream, TBUF - n, [[8 * TBUF, 128], [TBUF, 8], [1, n]])
        t1 = p.g.dma(AP(self.cb_in_t, 0, [[8 * n, 128], [n, 8], [1, n]]), src,
                     ch="cb", allow_slow_non_contiguous=True)
        p.g.wait(t1)
        p.collective(self.cb_in_t.ap()[0:nel], self.cb_out_t.ap()[0:2 * nel])
        dst = AP(self.stream, HALO - n, [[8 * TBUF, 128], [TBUF, 8], [1, n]])
        t2 = p.g.dma(dst, AP(self.cb_out_t, 0, [[8 * n, 128], [n, 8], [1, n]]),
                     ch="cb", allow_slow_non_contiguous=True)
        p.g.drain()
        tg = p.g.nop(_inc=True)
        p.v.wait(t2)
        p.v.wait(tg)
        p.v.tensor_scalar(dst, dst, self.gate[:, :], None, ALU.mult)
        p.barrier()

    def phase_A(self):
        p = self.p
        sv = self.ectx
        with contextlib.ExitStack() as actx:
            self.ectx = actx
            xfh = self.sbuf("xfh", [128, 8 * SEQ], BF16)
            xfl = self.sbuf("xfl", [128, 8 * SEQ], BF16)
            self.A0_loadx(xfh, xfl)
            p.barrier()
            self.A1_heads(xfh, xfl)
            p.barrier()
            self.ectx = sv
        self.stream = self.sbuf("stream", [128, 8 * TBUF])
        self.masks = self.sbuf("masks_sb", [128, 3 * 384])
        tmk = p.y.dma(self.masks[:], self.masks_t.ap(), ch="init")
        p.g.memset(AP(self.stream, 0,
                      [[8 * TBUF, 128], [TBUF, 8], [1, HALO]]), 0.0)
        p.g.drain()
        tms = p.g.nop(_inc=True)
        for em in p.all:
            em.wait(tms)
            em.wait(tmk)
        self.A2_x1()
        p.barrier()

    def A0_loadx(self, xfh, xfl):
        p = self.p
        sv = self.ectx
        a0ctx = contextlib.ExitStack()
        self.ectx = a0ctx
        xst = self.sbuf("xst", [128, 2 * D])
        xsh = self.sbuf("xsh", [128, 2 * D], BF16)
        xsl = self.sbuf("xsl", [128, 2 * D], BF16)
        x_ap = self.x_t.ap()
        sp_prev = [None, None]
        pe_prev = [None, None]
        ev_prev = {}
        for t in range(SEQ // 128):
            par = t % 2
            if sp_prev[par] is not None:
                p.y.wait(sp_prev[par])
            tok_ld = p.y.dma(xst[:, rr(par, D)], x_ap[rr(t, 128), :], ch="xA%d" % par)
            p.v.wait(tok_ld)
            if pe_prev[par] is not None:
                p.v.wait(pe_prev[par])
            sp = self.split(xst[:, rr(par, D)], xsh[:, rr(par, D)],
                            xsl[:, rr(par, D)])
            sp_prev[par] = sp
            p.t.wait(sp)
            for c in range(8):
                pp = self.pp[c % 2][:, 0:128]
                ev = ev_prev.get(c % 2)
                if ev is not None:
                    p.t.wait(ev)
                mm = self.tr_pair(
                    xsh[:, par * D + c * 128: par * D + (c + 1) * 128],
                    xsl[:, par * D + c * 128: par * D + (c + 1) * 128], pp)
                if c == 7:
                    pe_prev[par] = mm
                p.v.wait(mm)
                dst_h = xfh[:, c * SEQ + t * 128: c * SEQ + (t + 1) * 128]
                dst_l = xfl[:, c * SEQ + t * 128: c * SEQ + (t + 1) * 128]
                p.v.tensor_copy(dst_h, pp)
                p.v.drain()
                ev_prev[c % 2] = p.v.tensor_tensor(dst_l, pp, dst_h,
                                                   ALU.subtract, _inc=True)
        a0ctx.close()
        self.ectx = sv

    def A1_heads(self, xfh, xfl):
        p = self.p
        wqh = self.sbuf("wg_qh", [128, 8 * 64], BF16)
        wql = self.sbuf("wg_ql", [128, 8 * 64], BF16)
        wkh = self.sbuf("wg_kh", [128, 8 * 64], BF16)
        wkl = self.sbuf("wg_kl", [128, 8 * 64], BF16)
        wvh = self.sbuf("wg_vh", [128, 8 * 64], BF16)
        wvl = self.sbuf("wg_vl", [128, 8 * 64], BF16)
        wst = self.sbuf("wgst", [128, 4 * 64])
        bq = self.sbuf("bq", [64, 3])
        qh_s = self.sbuf("qh_s", [64, SEQ], BF16)
        ql_s = self.sbuf("ql_s", [64, SEQ], BF16)
        kh_s = self.sbuf("kh_s", [64, SEQ], BF16)
        kl_s = self.sbuf("kl_s", [64, SEQ], BF16)
        vtm = self.sbuf("vtm", [128, 32 * 128], BF16)
        vf = self.sbuf("vf", [64, 2 * 512])
        vfh = self.sbuf("vfh", [64, 2 * 512], BF16)
        vfl = self.sbuf("vfl", [64, 2 * 512], BF16)
        sq2 = self.sbuf("sq2", [64, 2 * 512], BF16)
        mxg = self.sbuf("mxg", [128, 8])
        mx = self.sbuf("mx", [128, 4])
        pf = self.sbuf("pf", [128, 512])
        ph = self.sbuf("ph", [128, 512], BF16)
        pl = self.sbuf("pl", [128, 512], BF16)
        rrow = self.sbuf("rrow", [1, 512])
        rhi = self.sbuf("rhi", [1, 512], BF16)
        rlo = self.sbuf("rlo", [1, 512], BF16)
        ot1 = vf[:, 0:512]
        ot2 = vf[:, 512:1024]
        oth = vfh[:, 0:512]
        otl = vfl[:, 0:512]
        otr = self.sbuf("otr", [128, 2 * 64])
        B = dict(wqh=wqh, wql=wql, wkh=wkh, wkl=wkl, wvh=wvh, wvl=wvl,
                 wst=wst, bq=bq, qh_s=qh_s, ql_s=ql_s, kh_s=kh_s, kl_s=kl_s,
                 vtm=vtm, vf=vf, vfh=vfh, vfl=vfl, sq2=sq2, mxg=mxg, mx=mx,
                 pf=pf, ph=ph, pl=pl, rrow=rrow, rhi=rhi, rlo=rlo, ot1=ot1,
                 ot2=ot2, oth=oth, otl=otl, otr=otr, xfh=xfh, xfl=xfl)
        for lh in range(NHG):
            self.A1_one_head(lh, B)

    def A1_one_head(self, lh, B):
        p = self.p
        pp = self.pp
        xfh, xfl = B["xfh"], B["xfl"]
        wtoks = []
        for i, nm in enumerate(("q", "k", "v")):
            wtoks += self.load_split_w(
                self.wqkv_g_t, 0,
                slice(lh * 192 + i * 64, lh * 192 + (i + 1) * 64),
                8, 64, B["wst"], B["w" + nm + "h"], B["w" + nm + "l"],
                ch="wg")
        btoks = [p.y.dma(B["bq"][:, i:i + 1],
                         self.bqkv_gT_t.ap()[lh * 192 + i * 64:
                                             lh * 192 + (i + 1) * 64, :],
                         ch="bias")
                 for i in range(3)]
        p.t.wait_all(wtoks)
        p.s.wait_all(btoks + wtoks)
        p.v.wait_all(btoks)

        ev_tok = {}
        vt_tok = {}
        for g in range(8):
            for i, nm in enumerate(("q", "k", "v")):
                ps = pp[i][0:64, :]
                prev = ev_tok.get((g - 1, i))
                if prev is not None:
                    p.t.wait(prev)
                for kc in range(8):
                    mm = self.mm3(
                        ps,
                        B["w" + nm + "h"][:, rr(kc, 64)],
                        B["w" + nm + "l"][:, rr(kc, 64)],
                        xfh[:, kc * SEQ + g * 512: kc * SEQ + (g + 1) * 512],
                        xfl[:, kc * SEQ + g * 512: kc * SEQ + (g + 1) * 512],
                        start=(kc == 0), stop=(kc == 7),
                        inc=(kc == 7))
                p.v.wait(mm)
                p.s.wait(mm)
                bias = B["bq"][:, i:i + 1]
                if i < 2:
                    hi = (B["qh_s"] if i == 0 else B["kh_s"])[:, rr(g, 512)]
                    lo = (B["ql_s"] if i == 0 else B["kl_s"])[:, rr(g, 512)]
                    ev_tok[(g, i)] = self.split_bias(ps, bias, hi, lo)
                else:
                    par = g % 2
                    tv = p.s.activation(B["vf"][:, rr(par, 512)], ps, AF.Identity,
                                        bias=bias, scale=1.0, _inc=True)
                    p.v.wait(tv)
                    spv = self.split(B["vf"][:, rr(par, 512)],
                                     B["vfh"][:, rr(par, 512)],
                                     B["vfl"][:, rr(par, 512)])
                    ev_tok[(g, i)] = spv
                    p.t.wait(spv)
                    for tc in range(4):
                        tg = g * 4 + tc
                        ptp = pp[3 + tc % 2][:, 0:128]
                        pv = vt_tok.get(tg - 2)
                        if pv is not None:
                            p.t.wait(pv)
                        p.t.matmul(ptp[:, 0:64],
                                   B["vfh"][:, par * 512 + tc * 128:
                                            par * 512 + (tc + 1) * 128],
                                   self.ident[0:64, 0:64],
                                   start=True, stop=False)
                        mmv = p.t.matmul(
                            ptp[:, 64:128],
                            B["vfl"][:, par * 512 + tc * 128:
                                     par * 512 + (tc + 1) * 128],
                            self.ident[0:64, 0:64],
                            start=False, stop=True, _inc=True)
                        p.v.wait(mmv)
                        vt_tok[tg] = p.v.tensor_copy(
                            B["vtm"][:, rr(tg, 128)], ptp, _inc=True)
        p.barrier()

        mx, mxg, sq2 = B["mx"], B["mxg"], B["sq2"]
        for i, src in enumerate((B["qh_s"], B["kh_s"])):
            prev_red = None
            for g in range(8):
                par = g % 2
                tsq = p.s.activation(sq2[:, rr(par, 512)], src[:, rr(g, 512)],
                                     AF.Square, _inc=True)
                p.t.wait(tsq)
                if prev_red is not None:
                    p.t.wait(prev_red)
                mm = p.t.matmul(pp[5][:, :], self.ones_sq[0:64, :],
                                sq2[:, rr(par, 512)], start=True, stop=True,
                                _inc=True)
                p.v.wait(mm)
                prev_red = p.v.tensor_reduce(mxg[:, g:g + 1], pp[5][:, :],
                                             AX.X, ALU.max, _inc=True)
                p.s.wait(prev_red)
            p.v.drain()
            p.v.tensor_reduce(mx[:, i:i + 1], mxg[:, 0:8], AX.X, ALU.max)
            p.v.drain()
        p.v.tensor_tensor(mx[:, 2:3], mx[:, 0:1], mx[:, 1:2], ALU.mult)
        p.v.drain()
        tb = p.v.nop(_inc=True)
        p.s.wait(tb)
        tb2 = p.s.activation(mx[:, 2:3], mx[:, 2:3], AF.Sqrt, scale=GB_MAR2,
                             _inc=True)
        p.v.wait(tb2)
        tb3 = p.v.tensor_scalar(mx[:, 3:4], mx[:, 2:3], -1.0, GB_SHIFT,
                                ALU.mult, ALU.add, _inc=True)
        p.s.wait(tb3)

        pf, ph, pl = B["pf"], B["ph"], B["pl"]
        qh_s, ql_s, kh_s, kl_s = B["qh_s"], B["ql_s"], B["kh_s"], B["kl_s"]
        p.t.wait(tb3)
        for g in range(8):
            exp_tok = {}
            split_tok = {}
            pv_tok = {}
            for jc in range(32):
                par = jc % 2
                ps = pp[5 + par][:, :]
                prev = exp_tok.get(jc - 2)
                if prev is not None:
                    p.t.wait(prev)
                p.t.matmul(ps, kh_s[:, rr(jc, 128)], qh_s[:, rr(g, 512)],
                           start=True, stop=False)
                p.t.matmul(ps, kh_s[:, rr(jc, 128)], ql_s[:, rr(g, 512)],
                           start=False, stop=False)
                mm = p.t.matmul(ps, kl_s[:, rr(jc, 128)], qh_s[:, rr(g, 512)],
                                start=False, stop=True, _inc=True)
                p.s.wait(mm)
                prev_sp = split_tok.get(jc - 1)
                if prev_sp is not None:
                    p.s.wait(prev_sp)
                te = p.s.activation(pf[:, :], ps, AF.Exp,
                                    bias=mx[:, 3:4], scale=1.0, _inc=True)
                exp_tok[jc] = te
                p.v.wait(te)
                prev_pv = pv_tok.get(jc - 1)
                if prev_pv is not None:
                    p.v.wait(prev_pv)
                p.v.tensor_copy(ph[:, :], pf[:, :])
                p.v.drain()
                tsp = p.v.tensor_tensor(pl[:, :], pf[:, :], ph[:, :],
                                        ALU.subtract, _inc=True)
                split_tok[jc] = tsp
                p.t.wait(tsp)
                vt = B["vtm"][:, rr(jc, 128)]
                p.t.matmul(pp[7][:, :], vt, ph[:, :],
                           start=(jc == 0), stop=False)
                p.t.matmul(pp[7][:, :], vt, pl[:, :],
                           start=False, stop=(jc == 31))
                p.t.matmul(pp[3][0:1, :], self.ones_col[:, :], ph[:, :],
                           start=(jc == 0), stop=False)
                pv_tok[jc] = p.t.matmul(
                    pp[3][0:1, :], self.ones_col[:, :], pl[:, :],
                    start=False, stop=(jc == 31), _inc=True)
            p.v.wait(pv_tok[31])
            p.v.tensor_scalar(B["rrow"][:, :], pp[3][0:1, :], 1e-35, None,
                              ALU.max)
            p.v.drain()
            p.v.reciprocal(B["rrow"][:, :], B["rrow"][:, :])
            p.v.drain()
            trr = self.split(B["rrow"][:, :], B["rhi"][:, :], B["rlo"][:, :])
            p.t.wait(trr)
            p.t.matmul(pp[4][0:64, :], self.ones_row[0:1, 0:64],
                       B["rhi"][:, :], start=True, stop=False)
            tR = p.t.matmul(pp[4][0:64, :], self.ones_row[0:1, 0:64],
                            B["rlo"][:, :], start=False, stop=True, _inc=True)
            p.v.wait(tR)
            ot1, ot2, oth, otl, otr = (B["ot1"], B["ot2"], B["oth"],
                                       B["otl"], B["otr"])
            p.v.tensor_copy(ot1[:, :], pp[7][64:128, :])
            p.v.drain()
            p.v.tensor_tensor(ot2[:, :], ot1[:, :], pp[7][0:64, :], ALU.add)
            p.v.drain()
            p.v.tensor_tensor(ot1[:, :], ot2[:, :], pp[4][0:64, :], ALU.mult)
            p.v.drain()
            tsp2 = self.split(ot1[:, :], oth[:, :], otl[:, :])
            p.t.wait(tsp2)
            ev_otr = {}
            for tc in range(4):
                par2 = tc % 2
                ptp = pp[par2][:, 0:64]
                pv = ev_otr.get(tc - 2)
                if pv is not None:
                    p.t.wait(pv)
                mmt = self.tr_pair(oth[:, rr(tc, 128)], otl[:, rr(tc, 128)],
                                   ptp, kdim=64)
                p.v.wait(mmt)
                if pv is not None:
                    p.v.wait(pv)
                tev = p.v.tensor_copy(otr[:, rr(par2, 64)], ptp, _inc=True)
                p.g.wait(tev)
                tg2 = g * 4 + tc
                dst = AP(self.o_scr_t, (lh * 256 + tg2 * 8) * (16 * DH),
                         [[16 * DH, 8], [DH, 16], [1, DH]])
                ev_otr[tc] = p.g.dma(dst, AP(otr, par2 * 64,
                                             [[2 * 64, 128], [1, 64]]),
                                     ch="oscr%d" % par2)
            p.barrier()

    def A2_x1(self):
        p = self.p
        sv = self.ectx
        with contextlib.ExitStack() as actx:
            self.ectx = actx
            at = self.sbuf("a_t", [128, 2 * D])
            xt = self.sbuf("x_t2", [128, 2 * D])
            x1t = self.sbuf("x1_t", [128, 2 * D])
            x1h = self.sbuf("x1_h", [128, D], BF16)
            x1l = self.sbuf("x1_l", [128, D], BF16)
            sqt = self.sbuf("sq_t", [128, D])
            stat = self.sbuf("statA", [128, 4])
            osc = self.o_scr_t.ap()
            xo = self.x_own_t.ap()
            done_tok = {}
            for t in range(NTILE):
                par = t % 2
                pd = done_tok.get(t - 2, [])
                for tk in pd:
                    p.y.wait(tk)
                ta = p.y.dma(at[:, rr(par, D)], osc[rr(t, 128), :], ch="a2a%d" % par)
                tx = p.y.dma(xt[:, rr(par, D)], xo[rr(t, 128), :], ch="a2x%d" % par)
                p.v.wait(ta)
                p.v.wait(tx)
                for tk in pd:
                    p.v.wait(tk)
                p.v.tensor_tensor(x1t[:, rr(par, D)], at[:, rr(par, D)],
                                  xt[:, rr(par, D)], ALU.add)
                p.v.drain()
                p.v.tensor_reduce(stat[:, 0:1], x1t[:, rr(par, D)], AX.X,
                                  ALU.add)
                p.v.drain()
                p.v.tensor_scalar(stat[:, 1:2], stat[:, 0:1], 1.0 / D, None,
                                  ALU.mult)
                p.v.drain()
                p.v.tensor_scalar(x1t[:, rr(par, D)], x1t[:, rr(par, D)],
                                  stat[:, 1:2], None, ALU.subtract)
                p.v.drain()
                p.v.drain()
                tsq = p.v.nop(_inc=True)
                p.s.wait(tsq)
                t2 = p.s.activation(sqt[:, :], x1t[:, rr(par, D)], AF.Square,
                                    accum_out=stat[:, 2:3], _inc=True)
                p.v.wait(t2)
                p.v.drain()
                tvv = p.v.nop(_inc=True)
                p.s.wait(tvv)
                t3 = p.s.activation(stat[:, 2:3], stat[:, 2:3], AF.Sqrt,
                                    scale=1.0 / D, bias=self.eps[:, :], _inc=True)
                p.v.wait(t3)
                p.v.reciprocal(stat[:, 3:4], stat[:, 2:3])
                p.v.drain()
                tx1 = p.v.tensor_scalar(x1t[:, rr(par, D)], x1t[:, rr(par, D)],
                                        stat[:, 3:4], None, ALU.mult,
                                        _inc=True)
                p.g.wait(tx1)
                tw1 = p.g.dma(self.x1_scr_t.ap()[rr(t, 128), :],
                              x1t[:, rr(par, D)], ch="x1w%d" % par)
                if self.stage == "a":
                    tw2 = p.g.dma(self.y_t.ap()[rr(t, 128), :],
                                  x1t[:, rr(par, D)], ch="yw%d" % par)
                    done_tok[t] = [tw1, tw2]
                    continue
                p.v.drain()
                tspl = self.split(x1t[:, rr(par, D)], x1h[:, :], x1l[:, :])
                p.t.wait(tspl)
                last_ev = None
                for c in range(8):
                    ppt = self.pp[c % 2][:, 0:128]
                    mmt = self.tr_pair(x1h[:, rr(c, 128)], x1l[:, rr(c, 128)],
                                       ppt)
                    p.v.wait(mmt)
                    last_ev = p.v.tensor_copy(
                        self.st(c, HALO + t * 128, HALO + (t + 1) * 128),
                        ppt, _inc=True)
                    p.t.wait(last_ev)
                done_tok[t] = [tw1, last_ev]
        self.ectx = sv

    def phase_attn(self, layer):
        p = self.p
        pp = self.pp
        sv = self.ectx
        with contextlib.ExitStack() as actx:
            self.ectx = actx
            wh = self.sbuf("wA_h", [128, 8 * 1536], BF16)
            wl = self.sbuf("wA_l", [128, 8 * 1536], BF16)
            wst = self.sbuf("wA_st", [128, 4 * 512])
            rt = self.sbuf("rtab", [128, 2 * 256])
            uh = self.sbuf("u_h", [128, 2 * D], BF16)
            ul = self.sbuf("u_l", [128, 2 * D], BF16)
            tq1 = self.sbuf("tq1", [128, 512])
            tq2 = self.sbuf("tq2", [128, 512])
            s2 = self.sbuf("s2", [128, 8])
            rsq = self.sbuf("rsq", [128, 8])
            sqf = self.sbuf("sqf", [128, 512])
            qsph = self.sbuf("qsph", [128, 512], BF16)
            qspl = self.sbuf("qspl", [128, 512], BF16)
            q_sh = self.sbuf("q_sh", [64, 8 * 128], BF16)
            q_sl = self.sbuf("q_sl", [64, 8 * 128], BF16)
            krh = self.sbuf("krh", [64, 8 * 4 * 128], BF16)
            krl = self.sbuf("krl", [64, 8 * 4 * 128], BF16)
            v_ring = self.sbuf("v_ring", [128, 8 * 4 * 128], BF16)
            sm = self.sbuf("sm", [128, 384])
            astat = self.sbuf("astat", [128, 4])
            pfi = self.sbuf("pfi", [128, 384])
            phb = self.sbuf("phb", [128, 384], BF16)
            plb = self.sbuf("plb", [128, 384], BF16)
            pth = self.sbuf("pth", [128, 384], BF16)
            ptl = self.sbuf("ptl", [128, 384], BF16)
            ogh = self.sbuf("ogh", [64, 8 * 512], BF16)
            ogl = self.sbuf("ogl", [64, 8 * 512], BF16)

            for hh in range(2):
                wtoks = []
                for i in range(3):
                    wtoks += self.load_split_w(
                        self.wqkv_i_t, layer * D,
                        slice(i * D + hh * 512, i * D + hh * 512 + 512),
                        8, 512, wst, wh, wl, dst_col0=i * 512,
                        dst_stride=1536, ch="wqi")
                p.t.wait_all(wtoks)

                qkv_cons = {}
                ring_cons = {}
                og_cons = None
                u_cons = [None, None]
                rt_cons = [None, None]
                for t in range(ATILE):
                    own = t >= HTILE
                    c0 = HALO - 256 + t * 128
                    par = t % 2
                    if u_cons[par] is not None:
                        p.v.wait(u_cons[par])
                    for lo_c, sh in ((0, 0), (4, -1)):
                        si = AP(self.stream, lo_c * TBUF + c0 + sh,
                                [[8 * TBUF, 128], [TBUF, 4], [1, 128]])
                        ho = AP(uh, par * D + lo_c * 128,
                                [[2 * D, 128], [128, 4], [1, 128]])
                        lo = AP(ul, par * D + lo_c * 128,
                                [[2 * D, 128], [128, 4], [1, 128]])
                        tu = self.split(si, ho, lo)
                    p.t.wait(tu)
                    if rt_cons[par] is not None:
                        p.y.wait(rt_cons[par])
                    trt = p.y.dma(rt[:, rr(par, 256)],
                                  self.rotab_t.ap()[layer * TBUF + c0:
                                                    layer * TBUF + c0 + 128, :],
                                  ch="rt%d" % par)
                    L = dict(rt=rt, s2=s2, rsq=rsq, sqf=sqf, tq1=tq1, tq2=tq2,
                             qsph=qsph, qspl=qspl, q_sh=q_sh, q_sl=q_sl,
                             krh=krh, krl=krl,
                             v_ring=v_ring, sm=sm, astat=astat, pfi=pfi,
                             phb=phb, plb=plb, pth=pth, ptl=ptl, ogh=ogh,
                             ogl=ogl, ring_cons=ring_cons, trt=trt)
                    for i, base in enumerate((0, 512, 1024)):
                        idx = 0 if i != 1 else 1
                        ps = pp[idx][:, :]
                        prev = qkv_cons.get(idx)
                        if prev is not None:
                            p.t.wait(prev)
                        for kc in range(8):
                            mm = self.mm3(
                                ps,
                                uh[:, par * D + kc * 128:
                                   par * D + (kc + 1) * 128],
                                ul[:, par * D + kc * 128:
                                   par * D + (kc + 1) * 128],
                                wh[:, kc * 1536 + base:
                                   kc * 1536 + base + 512],
                                wl[:, kc * 1536 + base:
                                   kc * 1536 + base + 512],
                                start=(kc == 0), stop=(kc == 7),
                                inc=(kc == 7))
                        if i == 0 and own:
                            qkv_cons[idx] = self.proc_q(t, par, mm, L)
                        elif i == 0:
                            qkv_cons[idx] = mm
                        elif i == 1:
                            qkv_cons[idx] = self.proc_k(t, par, mm, L)
                        else:
                            qkv_cons[idx] = self.proc_v(t, par, mm, L)
                    u_cons[par] = ("tE", p.cnt["tE"])
                    rt_cons[par] = qkv_cons[1]
                    if own:
                        to = t - HTILE
                        cls = 0 if to == 0 else (1 if to == 1 else 2)
                        p.t.wait(qkv_cons[0])
                        p.t.wait(qkv_cons[1])
                        atok = self.attn_tile(t, to, cls, hh, L)
                        for s in (t - 2, t - 1, t):
                            ring_cons[s % 4] = atok
                        if to % 4 == 3:
                            og_cons = self.flush_og(to // 4, hh, L, atok)
                            L["og_cons"] = og_cons
                        self._og_cons = og_cons
                p.barrier()
            self.ectx = sv
        self.phase_wo(layer)
        p.barrier()

    def proc_q(self, t, par, mm_tok, L):
        p = self.p
        pp = self.pp
        tok = self.qk_common(pp[0], mm_tok, L, par, 0)
        p.t.wait(tok)
        qsph, qspl = L["qsph"], L["qspl"]
        q_sh, q_sl = L["q_sh"], L["q_sl"]
        last = None
        prev_ev = [None, None]
        for h in range(8):
            hp = h % 2
            ptp = pp[2 + hp]
            if prev_ev[hp] is not None:
                p.t.wait(prev_ev[hp])
            p.t.matmul(ptp[0:64, 0:128], qsph[:, rr(h, 64)],
                       self.ident[:, :], start=True, stop=False)
            mmq = p.t.matmul(ptp[0:64, 128:256], qspl[:, rr(h, 64)],
                             self.ident[:, :], start=False, stop=True,
                             _inc=True)
            p.v.wait(mmq)
            p.v.tensor_copy(q_sh[:, rr(h, 128)], ptp[0:64, 0:128])
            last = p.v.tensor_copy(q_sl[:, rr(h, 128)], ptp[0:64, 128:256],
                                   _inc=True)
            prev_ev[hp] = last
        return last

    def proc_k(self, t, par, mm_tok, L):
        p = self.p
        pp = self.pp
        tok = self.qk_common(pp[1], mm_tok, L, par, 128)
        p.t.wait(tok)
        qsph, qspl = L["qsph"], L["qspl"]
        krh, krl = L["krh"], L["krl"]
        slot = t % 4
        rc = L["ring_cons"].get(slot)
        last = None
        prev_ev = [None, None]
        for h in range(8):
            hp = h % 2
            ptp = pp[2 + hp]
            if prev_ev[hp] is not None:
                p.t.wait(prev_ev[hp])
            if rc is not None and h == 0:
                p.v.wait(rc)
            p.t.matmul(ptp[0:64, 0:128], qsph[:, rr(h, 64)],
                       self.ident[:, :], start=True, stop=False)
            mmk = p.t.matmul(ptp[0:64, 128:256], qspl[:, rr(h, 64)],
                             self.ident[:, :], start=False, stop=True,
                             _inc=True)
            p.v.wait(mmk)
            p.v.tensor_copy(krh[:, rr(h * 4 + slot, 128)], ptp[0:64, 0:128])
            last = p.v.tensor_copy(krl[:, rr(h * 4 + slot, 128)],
                                   ptp[0:64, 128:256], _inc=True)
            prev_ev[hp] = last
        return last

    def qk_common(self, ps_t, mm_tok, L, par, rtbase):
        p = self.p
        rt, s2, rsq, sqf = L["rt"], L["s2"], L["rsq"], L["sqf"]
        tq1, tq2 = L["tq1"], L["tq2"]
        qsph, qspl = L["qsph"], L["qspl"]
        p.s.wait(mm_tok)
        p.s.wait(L["trt"])
        tsq = p.s.activation(sqf[:, :], ps_t[:, :], AF.Square, _inc=True)
        p.v.wait(tsq)
        p.v.tensor_reduce(s2[:, :],
                          AP(sqf, 0, [[512, 128], [64, 8], [1, 64]]),
                          AX.X, ALU.add)
        p.v.drain()
        p.v.tensor_scalar(s2[:, :], s2[:, :], 1e-24, None, ALU.max)
        p.v.drain()
        p.v.reciprocal(rsq[:, :], s2[:, :])
        p.v.drain()
        t_r = p.v.nop(_inc=True)
        p.s.wait(t_r)
        tsr = p.s.activation(rsq[:, :], rsq[:, :], AF.Sqrt, _inc=True)
        p.v.wait(tsr)
        p.v.tensor_tensor(
            AP(tq1, 0, [[512, 128], [64, 8], [1, 64]]),
            AP(ps_t, 0, [[512, 128], [64, 8], [1, 64]]),
            AP(rt, par * 256 + rtbase, [[512, 128], [0, 8], [1, 64]]),
            ALU.mult)
        p.v.tensor_tensor(
            AP(tq2, 0, [[512, 128], [64, 8], [1, 32]]),
            AP(ps_t, 32, [[512, 128], [64, 8], [1, 32]]),
            AP(rt, par * 256 + rtbase + 64, [[512, 128], [0, 8], [1, 32]]),
            ALU.mult)
        p.v.tensor_tensor(
            AP(tq2, 32, [[512, 128], [64, 8], [1, 32]]),
            AP(ps_t, 0, [[512, 128], [64, 8], [1, 32]]),
            AP(rt, par * 256 + rtbase + 96, [[512, 128], [0, 8], [1, 32]]),
            ALU.mult)
        p.v.drain()
        p.v.tensor_tensor(tq1[:, :], tq1[:, :], tq2[:, :], ALU.add)
        p.v.drain()
        p.v.tensor_tensor(
            AP(tq1, 0, [[512, 128], [64, 8], [1, 64]]),
            AP(tq1, 0, [[512, 128], [64, 8], [1, 64]]),
            AP(rsq, 0, [[8, 128], [1, 8], [0, 64]]), ALU.mult)
        p.v.drain()
        return self.split(tq1[:, :], qsph[:, :], qspl[:, :])

    def proc_v(self, t, par, mm_tok, L):
        p = self.p
        ps_t = self.pp[0]
        v_ring = L["v_ring"]
        slot = t % 4
        rc = L["ring_cons"].get(slot)
        p.v.wait(mm_tok)
        if rc is not None:
            p.v.wait(rc)
        p.v.tensor_copy(
            AP(v_ring, slot * 128, [[8 * 4 * 128, 128], [512, 8], [1, 64]]),
            AP(ps_t, 0, [[512, 128], [64, 8], [1, 64]]))
        p.v.drain()
        return p.v.tensor_tensor(
            AP(v_ring, slot * 128 + 64,
               [[8 * 4 * 128, 128], [512, 8], [1, 64]]),
            AP(ps_t, 0, [[512, 128], [64, 8], [1, 64]]),
            AP(v_ring, slot * 128, [[8 * 4 * 128, 128], [512, 8], [1, 64]]),
            ALU.subtract, _inc=True)

    def attn_tile(self, t, to, cls, hh, L):
        p = self.p
        pp = self.pp
        q_sh, q_sl, v_ring = L["q_sh"], L["q_sl"], L["v_ring"]
        krh, krl = L["krh"], L["krl"]
        sm, astat, pfi = L["sm"], L["astat"], L["pfi"]
        phb, plb, pth, ptl = L["phb"], L["plb"], L["pth"], L["ptl"]
        ogh, ogl = L["ogh"], L["ogl"]
        last_tok = None
        og_guard = getattr(self, "_og_cons", None) if to % 4 == 0 else None
        for h in range(8):
            psS = pp[4]
            for jc in range(3):
                slot = (t - 2 + jc) % 4
                p.t.matmul(psS[:, rr(jc, 128)], q_sh[:, rr(h, 128)],
                           krh[:, rr(h * 4 + slot, 128)],
                           start=(jc == 0), stop=False)
                p.t.matmul(psS[:, rr(jc, 128)], q_sh[:, rr(h, 128)],
                           krl[:, rr(h * 4 + slot, 128)],
                           start=False, stop=False)
                mmS = p.t.matmul(psS[:, rr(jc, 128)], q_sl[:, rr(h, 128)],
                                 krh[:, rr(h * 4 + slot, 128)],
                                 start=False, stop=(jc == 2),
                                 _inc=(jc == 2))
            p.v.wait(mmS)
            p.v.scalar_tensor_tensor(sm[:, :], psS[:, 0:384], QKS,
                                     self.masks[:, rr(cls, 384)], ALU.mult,
                                     ALU.add)
            p.v.drain()
            p.v.tensor_reduce(astat[:, 0:1], sm[:, :], AX.X, ALU.max)
            p.v.drain()
            tneg = p.v.tensor_scalar(astat[:, 1:2], astat[:, 0:1], -1.0, None,
                                     ALU.mult, _inc=True)
            p.s.wait(tneg)
            texp = p.s.activation(pfi[:, :], sm[:, :], AF.Exp,
                                  bias=astat[:, 1:2],
                                  accum_out=astat[:, 2:3], _inc=True)
            p.v.wait(texp)
            p.v.reciprocal(astat[:, 3:4], astat[:, 2:3])
            p.v.drain()
            p.v.tensor_scalar(phb[:, :], pfi[:, :], astat[:, 3:4], None,
                              ALU.mult)
            p.v.drain()
            tspl = p.v.scalar_tensor_tensor(plb[:, :], pfi[:, :],
                                            astat[:, 3:4], phb[:, :],
                                            ALU.mult, ALU.subtract, _inc=True)
            p.t.wait(tspl)
            for jc in range(3):
                p.t.matmul(pp[5][:, rr(jc, 128)], phb[:, rr(jc, 128)],
                           self.ident[:, :], start=(jc == 0), stop=(jc == 2))
            for jc in range(3):
                mmT = p.t.matmul(pp[6][:, rr(jc, 128)], plb[:, rr(jc, 128)],
                                 self.ident[:, :], start=(jc == 0),
                                 stop=(jc == 2), _inc=(jc == 2))
            p.v.wait(mmT)
            p.v.tensor_copy(pth[:, :], pp[5][:, 0:384])
            tptl = p.v.tensor_copy(ptl[:, :], pp[6][:, 0:384], _inc=True)
            p.t.wait(tptl)
            if last_tok is not None:
                p.t.wait(last_tok)
            psO = pp[7][0:64, 0:128]
            for jc in range(3):
                slot = (t - 2 + jc) % 4
                vh = v_ring[:, (h * 4 + slot) * 128:
                            (h * 4 + slot) * 128 + 64]
                vl = v_ring[:, (h * 4 + slot) * 128 + 64:
                            (h * 4 + slot) * 128 + 128]
                p.t.matmul(psO, vh, pth[:, rr(jc, 128)],
                           start=(jc == 0), stop=False)
                p.t.matmul(psO, vh, ptl[:, rr(jc, 128)], start=False,
                           stop=False)
                mmO = p.t.matmul(psO, vl, pth[:, rr(jc, 128)], start=False,
                                 stop=(jc == 2), _inc=(jc == 2))
            p.v.wait(mmO)
            if og_guard is not None and h == 0:
                p.v.wait(og_guard)
            col = h * 512 + (to % 4) * 128
            p.v.tensor_copy(ogh[:, col:col + 128], psO)
            p.v.drain()
            last_tok = p.v.tensor_tensor(ogl[:, col:col + 128], psO,
                                         ogh[:, col:col + 128], ALU.subtract,
                                         _inc=True)
        return last_tok

    def flush_og(self, grp, hh, L, atok):
        p = self.p
        ogh, ogl = L["ogh"], L["ogl"]
        oi = self.oi_scr_t.ap()
        p.g.wait(atok)
        toks = []
        for h in range(8):
            row = hh * 512 + h * 64
            toks.append(p.g.dma(oi[row:row + 64, rr(grp, 512)],
                                ogh[:, rr(h, 512)], ch="og"))
            toks.append(p.g.dma(oi[D + row:D + row + 64, rr(grp, 512)],
                                ogl[:, rr(h, 512)], ch="og"))
        return toks[-1]

    def phase_wo(self, layer):
        p = self.p
        pp = self.pp
        sv = self.ectx
        with contextlib.ExitStack() as actx:
            self.ectx = actx
            woh = self.sbuf("wo_h", [128, 8 * 1024], BF16)
            wol = self.sbuf("wo_l", [128, 8 * 1024], BF16)
            wst = self.sbuf("wo_st", [128, 4 * 1024])
            obh = self.sbuf("ob_h", [128, 8 * 512], BF16)
            obl = self.sbuf("ob_l", [128, 8 * 512], BF16)
            wtoks = self.load_split_w(self.wo_i_t, layer * D, slice(0, D),
                                      8, 1024, wst, woh, wol, ch="wwo")
            p.t.wait_all(wtoks)
            oi = self.oi_scr_t.ap()
            ev_par = [None, None]
            for g in range(4):
                ltoks = []
                for c in range(8):
                    ltoks.append(p.y.dma(obh[:, rr(c, 512)],
                                         oi[rr(c, 128), rr(g, 512)],
                                         ch="ob"))
                    ltoks.append(p.y.dma(obl[:, rr(c, 512)],
                                         oi[D + c * 128: D + (c + 1) * 128,
                                            rr(g, 512)], ch="ob"))
                p.t.wait_all(ltoks)
                last_ev = None
                for of in range(8):
                    ps = pp[of % 2][:, :]
                    if ev_par[of % 2] is not None:
                        p.t.wait(ev_par[of % 2])
                    for kc in range(8):
                        mm = self.mm3(ps,
                                      woh[:, kc * 1024 + of * 128:
                                          kc * 1024 + (of + 1) * 128],
                                      wol[:, kc * 1024 + of * 128:
                                          kc * 1024 + (of + 1) * 128],
                                      obh[:, rr(kc, 512)],
                                      obl[:, rr(kc, 512)],
                                      start=(kc == 0), stop=(kc == 7),
                                      inc=(kc == 7))
                    p.v.wait(mm)
                    dst = self.st(of, HALO + g * 512, HALO + (g + 1) * 512)
                    ev = p.v.tensor_tensor(dst, dst, ps, ALU.add, _inc=True)
                    ev_par[of % 2] = ev
                    last_ev = ev
                p.y.wait(("tE", p.cnt["tE"]))
            self.ectx = sv

    def phase_ffn(self, layer):
        p = self.p
        pp = self.pp
        sv = self.ectx
        with contextlib.ExitStack() as actx:
            self.ectx = actx
            xnh = self.sbuf("xn_h", [128, 8 * NT], BF16)
            xnl = self.sbuf("xn_l", [128, 8 * NT], BF16)
            uh = self.sbuf("uf_h", [128, 512], BF16)
            ul = self.sbuf("uf_l", [128, 512], BF16)
            u2f = self.sbuf("u2f", [128, 512])
            u2h = self.sbuf("u2h", [128, 512], BF16)
            u2l = self.sbuf("u2l", [128, 512], BF16)
            rowp = self.sbuf("rowp", [1, 5 * 512])
            rwh = self.sbuf("rwh", [1, 4 * 512], BF16)
            gbh = self.sbuf("gbh", [1, 4 * D], BF16)
            ones5 = self.sbuf("ones512", [1, 512], BF16)
            xnf = self.sbuf("xnf", [128, 512])
            hgf = self.sbuf("hgf", [128, 256])
            hfh = self.sbuf("hf_h", [128, 256], BF16)
            hfl = self.sbuf("hf_l", [128, 256], BF16)
            w1h = self.sbuf("w1_h", [128, GBLK * 8 * 256], BF16)
            w1l = self.sbuf("w1_l", [128, GBLK * 8 * 256], BF16)
            w2h = self.sbuf("w2_h", [128, GBLK * 1024], BF16)
            w2l = self.sbuf("w2_l", [128, GBLK * 1024], BF16)
            wst = self.sbuf("wf_st", [128, 4 * 512])
            b2c = self.sbuf("b2c", [128, 8])
            b1c = self.sbuf("b1c", [128, 2 * len(ICHUNKS)])

            p.g.memset(ones5[:, :], 1.0)
            p.g.drain()
            tg1 = p.g.nop(_inc=True)
            tgb = p.y.dma(rowp[0:1, 0:D], self.lng_t.ap()[layer:layer + 1, :], ch="lnb")
            p.v.wait(tgb)
            tg_s = self.split(rowp[0:1, 0:D], gbh[:, 0:D], gbh[:, D:2 * D])
            p.y.wait(tg_s)
            tgb = p.y.dma(rowp[0:1, 0:D], self.lnb_t.ap()[layer:layer + 1, :], ch="lnb")
            p.v.wait(tgb)
            tspl = self.split(rowp[0:1, 0:D], gbh[:, 2 * D:3 * D],
                              gbh[:, 3 * D:4 * D])
            tb2 = p.y.dma(b2c[:, :], AP(self.b2T_t, layer * D,
                                        [[1, 128], [128, 8]]),
                          ch="bias2", allow_slow_non_contiguous=True)
            b1toks = []
            for ci, (c0, w) in enumerate(ICHUNKS):
                b1toks.append(p.y.dma(
                    b1c[0:w, 2 * ci:2 * ci + 1],
                    self.b1T_t.ap()[layer * 2 * DI + c0:
                                    layer * 2 * DI + c0 + w, :], ch="bias2"))
                b1toks.append(p.y.dma(
                    b1c[0:w, 2 * ci + 1:2 * ci + 2],
                    self.b1T_t.ap()[layer * 2 * DI + DI + c0:
                                    layer * 2 * DI + DI + c0 + w, :],
                    ch="bias2"))

            stats_cons = None
            prev_spu2 = None
            for g in range(4):
                a0 = HALO + g * 512
                for c in range(8):
                    sh = -1 if c >= 4 else 0
                    u_ap = self.st(c, a0 + sh, a0 + 512 + sh)
                    tspu = self.split(u_ap, uh[:, :], ul[:, :])
                    if prev_spu2 is not None:
                        p.s.wait(prev_spu2)
                    tq = p.s.activation(u2f[:, :], u_ap, AF.Square, _inc=True)
                    p.v.wait(tq)
                    tspu2 = self.split(u2f[:, :], u2h[:, :], u2l[:, :])
                    prev_spu2 = tspu2
                    p.t.wait(tspu)
                    p.t.wait(tspu2)
                    if c == 0 and stats_cons is not None:
                        p.t.wait(stats_cons)
                    p.t.matmul(pp[2][0:1, :], self.ones_col[:, :], uh[:, :],
                               start=(c == 0), stop=False)
                    p.t.matmul(pp[2][0:1, :], self.ones_col[:, :], ul[:, :],
                               start=False, stop=(c == 7))
                    p.t.matmul(pp[3][0:1, :], self.ones_col[:, :], u2h[:, :],
                               start=(c == 0), stop=False)
                    tmm = p.t.matmul(pp[3][0:1, :], self.ones_col[:, :],
                                     u2l[:, :], start=False, stop=(c == 7),
                                     _inc=True)
                    p.v.wait(tmm)
                p.v.tensor_scalar(rowp[0:1, 0:512], pp[2][0:1, :], 1.0 / D,
                                  None, ALU.mult)
                p.v.drain()
                p.v.tensor_tensor(rowp[0:1, 512:1024], rowp[0:1, 0:512],
                                  rowp[0:1, 0:512], ALU.mult)
                p.v.drain()
                p.v.scalar_tensor_tensor(rowp[0:1, 1024:1536], pp[3][0:1, :],
                                         1.0 / D, rowp[0:1, 512:1024],
                                         ALU.mult, ALU.subtract)
                p.v.drain()
                stats_cons = tvar = p.v.nop(_inc=True)
                p.s.wait(tvar)
                tsr = p.s.activation(rowp[0:1, 1024:1536], rowp[0:1, 1024:1536],
                                     AF.Sqrt, bias=self.eps[0:1, :], _inc=True)
                p.v.wait(tsr)
                p.v.reciprocal(rowp[0:1, 1536:2048], rowp[0:1, 1024:1536])
                p.v.drain()
                p.v.tensor_tensor(rowp[0:1, 2048:2560], rowp[0:1, 0:512],
                                  rowp[0:1, 1536:2048], ALU.mult)
                p.v.drain()
                p.v.tensor_scalar(rowp[0:1, 2048:2560], rowp[0:1, 2048:2560],
                                  -1.0, None, ALU.mult)
                p.v.drain()
                self.split(rowp[0:1, 1536:2048], rwh[:, 0:512],
                           rwh[:, 512:1024])
                trw = self.split(rowp[0:1, 2048:2560], rwh[:, 1024:1536],
                                 rwh[:, 1536:2048])
                p.t.wait(trw)
                if g == 0:
                    p.t.wait(tspl)
                for c in range(8):
                    if g == 0 and c == 0:
                        p.t.wait(tg1)
                    gh_c = gbh[0:1, rr(c, 128)]
                    gl_c = gbh[0:1, D + c * 128: D + (c + 1) * 128]
                    bh_c = gbh[0:1, 2 * D + c * 128: 2 * D + (c + 1) * 128]
                    bl_c = gbh[0:1, 3 * D + c * 128: 3 * D + (c + 1) * 128]
                    p.t.matmul(pp[4][:, :], gh_c, rwh[:, 0:512],
                               start=True, stop=False)
                    p.t.matmul(pp[4][:, :], gh_c, rwh[:, 512:1024],
                               start=False, stop=False)
                    p.t.matmul(pp[4][:, :], gl_c, rwh[:, 0:512],
                               start=False, stop=True)
                    p.t.matmul(pp[5][:, :], gh_c, rwh[:, 1024:1536],
                               start=True, stop=False)
                    p.t.matmul(pp[5][:, :], gh_c, rwh[:, 1536:2048],
                               start=False, stop=False)
                    p.t.matmul(pp[5][:, :], gl_c, rwh[:, 1024:1536],
                               start=False, stop=False)
                    p.t.matmul(pp[5][:, :], bh_c, ones5[:, :],
                               start=False, stop=False)
                    tm2 = p.t.matmul(pp[5][:, :], bl_c, ones5[:, :],
                                     start=False, stop=True, _inc=True)
                    p.v.wait(tm2)
                    sh = -1 if c >= 4 else 0
                    u_ap = self.st(c, a0 + sh, a0 + 512 + sh)
                    p.v.tensor_tensor(xnf[:, :], u_ap, pp[4][:, :], ALU.mult)
                    p.v.drain()
                    p.v.tensor_tensor(xnf[:, :], xnf[:, :], pp[5][:, :],
                                      ALU.add)
                    p.v.drain()
                    txn = self.split(xnf[:, :],
                                     xnh[:, c * NT + g * 512:
                                         c * NT + (g + 1) * 512],
                                     xnl[:, c * NT + g * 512:
                                         c * NT + (g + 1) * 512])
                    p.t.wait(txn)
            p.barrier()

            psa_cons = {}
            psg_cons = {}
            for bi, blk in enumerate(IBLOCKS):
                p.y.wait(("tE", p.cnt["tE"]))
                w1toks = []
                w2toks = []
                for ci_in_b, (c0, w) in enumerate(blk):
                    w1toks += self.load_split_w(
                        self.w1_i_t, layer * D, slice(c0, c0 + w), 8, w,
                        wst, w1h, w1l, dst_col0=ci_in_b * 8 * 256,
                        dst_stride=256, ch="wf")
                    w1toks += self.load_split_w(
                        self.w1_i_t, layer * D, slice(DI + c0, DI + c0 + w),
                        8, w, wst, w1h, w1l,
                        dst_col0=ci_in_b * 8 * 256 + 128, dst_stride=256,
                        ch="wf")
                    w2toks += self.load_split_w(
                        self.w2_i_t, layer * DI + c0, slice(0, 512), 1,
                        512, wst, w2h, w2l, dst_col0=ci_in_b * 1024, ch="wf")
                    w2toks += self.load_split_w(
                        self.w2_i_t, layer * DI + c0, slice(512, 1024), 1,
                        512, wst, w2h, w2l, dst_col0=ci_in_b * 1024 + 512,
                        ch="wf")
                p.t.wait_all(w1toks + w2toks)
                for g in range(8):
                    a0 = HALO + g * 256
                    for ci_in_b, (c0, w) in enumerate(blk):
                        ci = bi * GBLK + ci_in_b
                        par = (g * len(blk) + ci_in_b) % 2
                        psa = pp[par][0:w, 0:256]
                        psg = pp[2 + par][0:w, 0:256]
                        if psa_cons.get(par) is not None:
                            p.t.wait(psa_cons[par])
                        if psg_cons.get(par) is not None:
                            p.t.wait(psg_cons[par])
                        for kc in range(8):
                            self.mm3(psa,
                                     w1h[:, ci_in_b * 8 * 256 + kc * 256:
                                         ci_in_b * 8 * 256 + kc * 256 + w],
                                     w1l[:, ci_in_b * 8 * 256 + kc * 256:
                                         ci_in_b * 8 * 256 + kc * 256 + w],
                                     xnh[:, kc * NT + g * 256:
                                         kc * NT + (g + 1) * 256],
                                     xnl[:, kc * NT + g * 256:
                                         kc * NT + (g + 1) * 256],
                                     start=(kc == 0), stop=(kc == 7),
                                     inc=False)
                        for kc in range(8):
                            self.mm3(psg,
                                     w1h[:, ci_in_b * 8 * 256 + kc * 256 + 128:
                                         ci_in_b * 8 * 256 + kc * 256 + 128 + w],
                                     w1l[:, ci_in_b * 8 * 256 + kc * 256 + 128:
                                         ci_in_b * 8 * 256 + kc * 256 + 128 + w],
                                     xnh[:, kc * NT + g * 256:
                                         kc * NT + (g + 1) * 256],
                                     xnl[:, kc * NT + g * 256:
                                         kc * NT + (g + 1) * 256],
                                     start=(kc == 0), stop=(kc == 7),
                                     inc=(kc == 7))
                        mmw1 = ("tE", p.cnt["tE"])
                        p.s.wait(mmw1)
                        tgl = p.s.activation(hgf[0:w, :], psg, AF.Gelu,
                                             bias=b1c[0:w, 2 * ci + 1:
                                                      2 * ci + 2],
                                             scale=1.0, _inc=True)
                        psg_cons[par] = tgl
                        p.v.wait(tgl)
                        tsa = p.v.scalar_tensor_tensor(
                            xnf[0:w, 0:256], psa,
                            b1c[0:w, 2 * ci: 2 * ci + 1],
                            hgf[0:w, :], ALU.add, ALU.mult, _inc=True)
                        psa_cons[par] = tsa
                        p.v.drain()
                        p.v.tensor_copy(hfh[0:w, :], xnf[0:w, 0:256])
                        p.v.drain()
                        thl = p.v.tensor_tensor(hfl[0:w, :], xnf[0:w, 0:256],
                                                hfh[0:w, :], ALU.subtract,
                                                _inc=True)
                        p.t.wait(thl)
                        first = ci_in_b == 0
                        last = ci_in_b == len(blk) - 1
                        for of in range(8):
                            pso = pp[4 + of // 2][:, (of % 2) * 256:
                                                  (of % 2) * 256 + 256]
                            self.mm3(pso,
                                     w2h[0:w, ci_in_b * 1024 + of * 128:
                                         ci_in_b * 1024 + (of + 1) * 128],
                                     w2l[0:w, ci_in_b * 1024 + of * 128:
                                         ci_in_b * 1024 + (of + 1) * 128],
                                     hfh[0:w, :], hfl[0:w, :],
                                     start=(first and (of % 2 == 0)),
                                     stop=(last and (of % 2 == 1)),
                                     inc=(of == 7))
                        mmw2 = ("tE", p.cnt["tE"])
                        p.v.wait(mmw2)
                    for of in range(8):
                        pso = pp[4 + of // 2][:, (of % 2) * 256:
                                              (of % 2) * 256 + 256]
                        dst = self.st(of, a0, a0 + 256)
                        if bi == len(IBLOCKS) - 1:
                            ev = p.v.scalar_tensor_tensor(
                                dst, pso, b2c[:, of:of + 1], dst, ALU.add,
                                ALU.add, _inc=(of == 7))
                        else:
                            ev = p.v.tensor_tensor(dst, dst, pso, ALU.add,
                                                   _inc=(of == 7))
                    p.t.wait(ev)
            self.ectx = sv
        p.barrier()

    def phase_final(self):
        p = self.p
        pp = self.pp
        sv = self.ectx
        with contextlib.ExitStack() as actx:
            self.ectx = actx
            yh = self.sbuf("y_h", [128, D], BF16)
            yl = self.sbuf("y_l", [128, D], BF16)
            yt = self.sbuf("y_t2", [128, 2 * D])
            x1t = self.sbuf("x1b", [128, 2 * D])
            sqt = self.sbuf("sqF", [128, D])
            stat = self.sbuf("statF", [128, 4])
            dma_done = {}
            prev_tr = None
            for t in range(NTILE):
                par = t % 2
                a0 = HALO + t * 128
                if prev_tr is not None:
                    p.v.wait(prev_tr)
                tspl = self.split(
                    AP(self.stream, a0, [[8 * TBUF, 128], [TBUF, 4], [1, 128]]),
                    AP(yh, 0, [[D, 128], [128, 4], [1, 128]]),
                    AP(yl, 0, [[D, 128], [128, 4], [1, 128]]))
                tspl = self.split(
                    AP(self.stream, 4 * TBUF + a0,
                       [[8 * TBUF, 128], [TBUF, 4], [1, 128]]),
                    AP(yh, 512, [[D, 128], [128, 4], [1, 128]]),
                    AP(yl, 512, [[D, 128], [128, 4], [1, 128]]))
                p.t.wait(tspl)
                pd = dma_done.get(t - 2)
                if pd is not None:
                    p.y.wait(pd)
                tx1 = p.y.dma(x1t[:, rr(par, D)],
                              self.x1_scr_t.ap()[rr(t, 128), :],
                              ch="fx%d" % par)
                if pd is not None:
                    p.t.wait(pd)
                    p.v.wait(pd)
                for c in range(4):
                    ppt = pp[c % 2][:, 0:256]
                    mmt = self.tr_pair(yh[:, rr(2 * c, 128)],
                                       yl[:, rr(2 * c, 128)],
                                       ppt[:, 0:128], start=True, stop=False)
                    mmt = self.tr_pair(yh[:, rr(2 * c + 1, 128)],
                                       yl[:, rr(2 * c + 1, 128)],
                                       ppt[:, 128:256], start=False, stop=True)
                    p.v.wait(mmt)
                    p.v.tensor_copy(yt[:, par * D + 2 * c * 128:
                                       par * D + (2 * c + 2) * 128], ppt)
                    p.v.drain()
                    tev = p.v.nop(_inc=True)
                    p.t.wait(tev)
                prev_tr = ("tE", p.cnt["tE"])
                p.v.wait(tx1)
                p.v.drain()
                p.v.tensor_tensor(yt[:, rr(par, D)], yt[:, rr(par, D)],
                                  x1t[:, rr(par, D)], ALU.add)
                p.v.drain()
                p.v.tensor_reduce(stat[:, 0:1], yt[:, rr(par, D)], AX.X,
                                  ALU.add)
                p.v.drain()
                p.v.tensor_scalar(stat[:, 1:2], stat[:, 0:1], 1.0 / D, None,
                                  ALU.mult)
                p.v.drain()
                p.v.tensor_scalar(yt[:, rr(par, D)], yt[:, rr(par, D)],
                                  stat[:, 1:2], None, ALU.subtract)
                p.v.drain()
                p.v.drain()
                tsq = p.v.nop(_inc=True)
                p.s.wait(tsq)
                t2 = p.s.activation(sqt[:, :], yt[:, rr(par, D)], AF.Square,
                                    accum_out=stat[:, 2:3], _inc=True)
                p.v.wait(t2)
                p.v.drain()
                tvv = p.v.nop(_inc=True)
                p.s.wait(tvv)
                t3 = p.s.activation(stat[:, 2:3], stat[:, 2:3], AF.Sqrt,
                                    scale=1.0 / D, bias=self.eps[:, :], _inc=True)
                p.v.wait(t3)
                p.v.reciprocal(stat[:, 3:4], stat[:, 2:3])
                p.v.drain()
                ty = p.v.tensor_scalar(yt[:, rr(par, D)], yt[:, rr(par, D)],
                                       stat[:, 3:4], None, ALU.mult,
                                       _inc=True)
                p.g.wait(ty)
                dma_done[t] = p.g.dma(self.y_t.ap()[rr(t, 128), :],
                                      yt[:, rr(par, D)], ch="yw%d" % par)
            p.g.wait(dma_done[NTILE - 1])
            p.g.wait_all(list(dma_done.values()))
            self.ectx = sv


def build_nc(stage=STAGE, n_layers=NL):
    return Kern(stage, n_layers).build()



def kernel(L0_attn_x, L0_attn_W_qkv, L0_attn_b_qkv, L0_ffn_Wqkv,
           L0_ffn_q_scale, L0_ffn_k_scale, L0_ffn_Wo, L0_ffn_ln_g,
           L0_ffn_ln_b, L0_ffn_W1, L0_ffn_b1, L0_ffn_W2, L0_ffn_b2):
    x = np.ascontiguousarray(np.asarray(L0_attn_x, dtype=np.float32))
    wqkv_g = np.asarray(L0_attn_W_qkv, np.float32)
    bqkv_g = np.asarray(L0_attn_b_qkv, np.float32)
    nc = build_nc()
    f32c = lambda a: np.ascontiguousarray(np.asarray(a, np.float32))
    wqkv_i = f32c(L0_ffn_Wqkv).reshape(NL * D, 3 * D)
    wo_i = f32c(L0_ffn_Wo).reshape(NL * D, D)
    w1_i = f32c(L0_ffn_W1).reshape(NL * D, 2 * DI)
    w2_i = f32c(L0_ffn_W2).reshape(NL * DI, D)
    w2_i = np.concatenate([w2_i, np.zeros((128, D), np.float32)], axis=0)
    b1T = f32c(L0_ffn_b1).reshape(NL * 2 * DI, 1)
    b2T = f32c(L0_ffn_b2).reshape(NL * D, 1)
    lng = f32c(L0_ffn_ln_g)
    lnb = f32c(L0_ffn_ln_b)
    in_maps = []
    for c in range(NCORE):
        b, half = c // 2, c % 2
        hc = host_constants(c)
        h0 = half * NHG
        cols = []
        for lh in range(NHG):
            h = h0 + lh
            cols += list(range(h * 64, h * 64 + 64))
            cols += list(range(D + h * 64, D + h * 64 + 64))
            cols += list(range(2 * D + h * 64, 2 * D + h * 64 + 64))
        cols = np.asarray(cols)
        m = {
            "x": x[b],
            "x_own": np.ascontiguousarray(x[b, half * NT:(half + 1) * NT]),
            "wqkv_g": np.ascontiguousarray(wqkv_g[:, cols]),
            "bqkv_gT": np.ascontiguousarray(bqkv_g[cols][:, None]),
            "ident": hc["ident"], "ones_row": hc["ones_row"],
            "ones_col": hc["ones_col"], "ones_sq": hc["ones_sq"],
            "masks": hc["masks"], "halo_gate": hc["halo_gate"],
            "rotab": host_rotab(c, L0_ffn_q_scale, L0_ffn_k_scale),
            "wqkv_i": wqkv_i, "wo_i": wo_i, "w1_i": w1_i, "w2_i": w2_i,
            "b1T": b1T, "b2T": b2T, "lng": lng, "lnb": lnb,
        }
        in_maps.append(m)
    res = run_bass_kernel_spmd(nc, in_maps, core_ids=list(range(NCORE)))
    out = np.empty((BATCH, SEQ, D), np.float32)
    for c in range(NCORE):
        b, half = c // 2, c % 2
        out[b, half * NT:(half + 1) * NT] = res.results[c]["y"]
    return out
